# version 13
# baseline (speedup 1.0000x reference)
import numpy as np
import ml_dtypes
import concourse.bass as bass
import concourse.mybir as mybir
from concourse.bass_utils import run_bass_kernel_spmd

F32 = mybir.dt.float32
BF16 = mybir.dt.bfloat16
AF = mybir.ActivationFunctionType
ALU = mybir.AluOpType
AX = mybir.AxisListType
BF = ml_dtypes.bfloat16

B, S, D, H = 4, 2048, 2048, 16
T = 1024
NCH = 16
PERM = [0, 3, 1, 2]
OBK = ([0, 3], [1, 2])
SCALE_A = float(1.0 / np.sqrt(128.0))
SCALE_B = float(1.0 / np.sqrt(192.0))
MSUB = 20.0
NEG = -30000.0
EPS = 1e-5
THETA = 10000.0
PI = float(np.pi)
ITEMS_QB = ([0, 1, 2, 3], [0, 1, 2, 3, 8, 9, 10, 11, 12, 13, 14, 15])
ENGINES = ("tensor", "scalar", "vector", "gpsimd", "sync")


STRICT_SAME_ENGINE = True

SLOT_GROUPS = ("wb0", "wb1", "kh0", "kh1", "vh0", "vh1", "ko0", "ko1",
               "vo0", "vo1", "frq", "ka0", "ka1", "va0", "va1",
               "kb0", "kb1", "vb0", "vb1")


class Sched:
    def __init__(self):
        self.steps = []
        self.last_w = {}
        self.readers = {}

    def add(self, engine, emit, reads=(), writes=(), n_dma=0, coll=False,
            tag="", dgroup=None):
        deps = set()
        for b in reads:
            w = self.last_w.get(b)
            if w is not None:
                deps.add(w)
        for b in writes:
            w = self.last_w.get(b)
            if w is not None:
                deps.add(w)
            for r in self.readers.get(b, ()):
                deps.add(r)
        i = len(self.steps)
        if n_dma:
            assert dgroup is not None, f"dma step {tag} needs dgroup"
        self.steps.append(dict(engine=engine, emit=emit, deps=deps,
                               n_dma=n_dma, coll=coll, idx=i, tag=tag,
                               dgroup=dgroup, pub=False, sem=None, cnt=None,
                               inc=0))
        for b in reads:
            self.readers.setdefault(b, []).append(i)
        for b in writes:
            self.last_w[b] = i
            self.readers[b] = []
        return i

    def finalize(self, nc, stack):
        self.semc = {en: stack.enter_context(nc.semaphore(f"sc_{en}"))
                     for en in ("tensor", "scalar", "vector", "gpsimd")}
        self.semcc = stack.enter_context(nc.semaphore("scc"))
        groups = []
        for s in self.steps:
            if s["n_dma"] and s["dgroup"] not in groups:
                groups.append(s["dgroup"])
        self.semd = {g: stack.enter_context(nc.semaphore(f"sg_{g}"))
                     for g in groups}
        for s in self.steps:
            for d in s["deps"]:
                p = self.steps[d]
                if (p["n_dma"] or p["coll"] or p["engine"] != s["engine"]
                        or STRICT_SAME_ENGINE):
                    p["pub"] = True
        ctr = {}
        self.gtotal = {}
        self.glast = {}
        for s in self.steps:
            en = s["engine"]
            if s["coll"]:
                ctr["cc"] = ctr.get("cc", 0) + 1
                s["sem"], s["cnt"], s["inc"] = self.semcc, ctr["cc"], 1
            elif s["n_dma"]:
                g = s["dgroup"]
                ctr[g] = ctr.get(g, 0) + 16 * s["n_dma"]
                s["sem"], s["cnt"], s["inc"] = self.semd[g], ctr[g], 16
                self.gtotal[g] = ctr[g]
                self.glast[g] = s["idx"]
            elif s["pub"]:
                k = ("c", en)
                ctr[k] = ctr.get(k, 0) + 1
                s["sem"], s["cnt"], s["inc"] = self.semc[en], ctr[k], 1
        for s in self.steps:
            for d in s["deps"]:
                p = self.steps[d]
                if p["n_dma"] and p["dgroup"] not in SLOT_GROUPS:
                    assert s["idx"] > self.glast[p["dgroup"]] or \
                        s["n_dma"] and s["dgroup"] == p["dgroup"], (
                        f"batch dgroup {p['dgroup']} consumed at step "
                        f"{s['idx']}({s['tag']}) before group closed at "
                        f"{self.glast[p['dgroup']]}")
        self._fd = {}
        for s in self.steps:
            if s["n_dma"]:
                self._fd.setdefault(s["engine"], {})[s["dgroup"]] = \
                    self.gtotal[s["dgroup"]]

    def emit_engine(self, eng, en):
        hw = {}
        for s in self.steps:
            if s["engine"] != en:
                continue
            need = {}
            for d in sorted(s["deps"]):
                p = self.steps[d]
                if p["sem"] is None:
                    continue
                if (not STRICT_SAME_ENGINE and not p["n_dma"]
                        and not p["coll"] and p["engine"] == en):
                    continue
                if (p["n_dma"] and s["n_dma"]
                        and s["dgroup"] == p["dgroup"]):
                    continue
                cnt = p["cnt"]
                if p["n_dma"] and p["dgroup"] not in SLOT_GROUPS:
                    cnt = self.gtotal[p["dgroup"]]
                key = id(p["sem"])
                if need.get(key, (None, -1))[1] < cnt:
                    need[key] = (p["sem"], cnt)
            for sem, cnt in need.values():
                if hw.get(id(sem), 0) < cnt:
                    eng.wait_ge(sem, cnt)
                    hw[id(sem)] = cnt
            out = s["emit"](eng, s["sem"])
            if s["sem"] is not None and not s["n_dma"] and not s["coll"]:
                last = out[-1] if isinstance(out, (list, tuple)) else out
                last.then_inc(s["sem"], s["inc"])
            elif s["coll"]:
                out.then_inc(s["sem"], 1)
        for g, tot in self._fd.get(en, {}).items():
            eng.wait_ge(self.semd[g], tot)


def tile_w(W, ob):
    Din, O = W.shape
    return np.ascontiguousarray(
        W.reshape(Din // 128, 128, O // ob, ob).transpose(2, 1, 0, 3))


def _positions(parity):
    b0, b1 = OBK[parity]
    return np.concatenate([512 * b0 + np.arange(512), 512 * b1 + np.arange(512)])


def _rope_tables_a(parity):
    pos = _positions(parity).astype(np.float64)
    idx = np.arange(0, 128, 2, dtype=np.float64)
    inv = THETA ** (-idx / 128.0)
    ang = inv[:, None] * pos[None, :]
    c = np.repeat(np.cos(ang), 2, axis=0)
    s = np.repeat(np.sin(ang), 2, axis=0)
    s[0::2] *= -1.0
    return c.astype(np.float32), s.astype(np.float32)


def _abias(parity):
    ab = np.zeros((128, 16), dtype=np.float32)
    col = 0
    for qb in range(2):
        gqb = OBK[parity][qb]
        for kt in ITEMS_QB[qb]:
            kb = PERM[kt // 4]
            ab[:, col] = -MSUB if kb < gqb else NEG
            col += 1
    return ab


def build_nc(stages=4, taps=()):
    from contextlib import ExitStack
    nc = bass.Bass("TRN2", target_bir_lowering=False)
    sc = Sched()
    stack = ExitStack()
    e = stack.enter_context

    def din(name, shape):
        return nc.dram_tensor(name, list(shape), F32, kind="ExternalInput").ap()

    x_in = din("x_fm", (D, T))
    y_out = nc.dram_tensor("y_fm", [D, T], F32, kind="ExternalOutput").ap()
    wqa = din("wqa", (16, 128, 16, 128))
    wka = din("wka", (16, 128, 16, 128))
    wva = din("wva", (8, 128, 16, 256))
    woa = din("woa", (4, 16, 128, 512))
    basew = din("basew", (16, 128, 16, 128))
    biasT_i = din("biasT", (128, 16))
    rb8_i = din("rb8", (128, 8))
    routr = din("routr", (128, 128))
    aall = din("aall", (128, 2048))
    ball = din("ball", (128, 2048))
    wqb = din("wqb", (24, 128, 16, 128))
    wkvac = din("wkvac", (4, 128, 16, 128))
    wkvar = din("wkvar", (1, 128, 16, 64))
    kvgb_i = din("kvgb", (128, 8))
    wkvbk = din("wkvbk", (16, 128, 4, 128))
    wkvbv = din("wkvbv", (8, 128, 4, 256))
    wob = din("wob", (4, 16, 128, 512))
    freqs = din("freqs", (T, 32))
    c2a_i = din("c2a", (128, T))
    s2a_i = din("s2a", (128, T))
    abias_i = din("abias", (128, 16))

    ka_in = nc.dram_tensor("ka_in", [D, T], BF16, kind="Internal")
    ka_out = nc.dram_tensor("ka_out", [2 * D, T], BF16, kind="Internal")
    va_in = nc.dram_tensor("va_in", [T, D], BF16, kind="Internal")
    va_out = nc.dram_tensor("va_out", [2 * T, D], BF16, kind="Internal")
    kb_in = nc.dram_tensor("kb_in", [2112, T], BF16, kind="Internal")
    kb_out = nc.dram_tensor("kb_out", [4224, T], BF16, kind="Internal")
    vb_in = nc.dram_tensor("vb_in", [T, D], BF16, kind="Internal")
    vb_out = nc.dram_tensor("vb_out", [2 * T, D], BF16, kind="Internal")
    CCG = [[0, 1], [2, 3], [4, 5], [6, 7]]

    ident_f = nc.inline_tensor(np.eye(128, dtype=np.float32), name="idf").ap()
    psw_np = np.zeros((128, 128), dtype=BF)
    for i in range(128):
        psw_np[i, i ^ 1] = 1
    pswap_c = nc.inline_tensor(psw_np, name="psw").ap()
    tri_np = np.where(np.triu(np.ones((128, 128))) > 0, 0.0, NEG).astype(BF)
    tri_c = nc.inline_tensor(tri_np, name="tri").ap()
    idb_c = nc.inline_tensor(np.eye(128, dtype=BF), name="idb").ap()
    onc_c = nc.inline_tensor(np.ones((128, 1), dtype=BF), name="onc").ap()
    onb_c = nc.inline_tensor(np.ones((128, 128), dtype=BF), name="onb").ap()
    onrf_c = nc.inline_tensor(np.ones((1, 128), dtype=np.float32),
                              name="onrf").ap()
    e2_np = np.zeros((8, 128), dtype=np.float32)
    for ex in range(8):
        e2_np[ex, 16 * ex:16 * ex + 16] = 2.0
    e2_c = nc.inline_tensor(e2_np, name="e2").ap()
    ec_np = np.zeros((32, 128), dtype=np.float32)
    es_np = np.zeros((32, 128), dtype=np.float32)
    for r in range(128):
        fi = (r % 64) // 2
        ec_np[fi, r] = 1.0
        es_np[fi, r] = -1.0 if r % 2 == 0 else 1.0
    ec_c = nc.inline_tensor(ec_np, name="ec").ap()
    mb20_c = nc.inline_tensor(np.full((128, 1), -MSUB, dtype=np.float32),
                              name="mb20").ap()
    es_c = nc.inline_tensor(es_np, name="es").ap()

    XR = e(nc.sbuf_tensor("XR", [128, NCH * T], F32))
    LNR = e(nc.sbuf_tensor("LNR", [128, NCH * T], BF16))
    QN = e(nc.sbuf_tensor("QN", [128, 8 * T], BF16))
    QR = e(nc.sbuf_tensor("QR", [128, 4 * T], BF16))
    KH = e(nc.sbuf_tensor("KH", [128, 2 * 2048], BF16))
    VH = e(nc.sbuf_tensor("VH", [128, 2 * 2048], BF16))
    KOWN = e(nc.sbuf_tensor("KOWN", [128, 2 * T], BF16))
    VOWN = e(nc.sbuf_tensor("VOWN", [128, 2 * T], BF16))
    KRB = KOWN
    WBUF = e(nc.sbuf_tensor("WBUF", [128, 2 * 4096], BF16))
    TBL = e(nc.sbuf_tensor("TBL", [128, 2 * T], F32))
    SCR = e(nc.sbuf_tensor("SCR", [128, 2 * T], F32))
    ROWB = e(nc.sbuf_tensor("ROWB", [1, 2 * T], F32))
    PTS = e(nc.sbuf_tensor("PTS", [128, 2 * 512], BF16))
    STG = e(nc.sbuf_tensor("STG", [128, 2 * T], BF16))
    RINV = e(nc.sbuf_tensor("RINV", [128, 512], F32))
    RAs = e(nc.sbuf_tensor("RAs", [128, 2048], BF16))
    RBs = e(nc.sbuf_tensor("RBs", [128, 2048], BF16))
    RRs = e(nc.sbuf_tensor("RRs", [128, 128], BF16))
    RB8 = e(nc.sbuf_tensor("RB8", [128, 8], F32))
    KVGB = e(nc.sbuf_tensor("KVGB", [128, 8], F32))
    ABIAS = e(nc.sbuf_tensor("ABIAS", [128, 16], F32))
    IDF = e(nc.sbuf_tensor("IDF", [128, 128], F32))
    IDB = e(nc.sbuf_tensor("IDB", [128, 128], BF16))
    PSW = e(nc.sbuf_tensor("PSW", [128, 128], BF16))
    TRIt = e(nc.sbuf_tensor("TRIt", [128, 128], BF16))
    ONC = e(nc.sbuf_tensor("ONC", [128, 1], BF16))
    ONB = e(nc.sbuf_tensor("ONB", [128, 128], BF16))
    ONRF = e(nc.sbuf_tensor("ONRF", [1, 128], F32))
    MB20 = e(nc.sbuf_tensor("MB20", [128, 1], F32))
    BIASC = e(nc.sbuf_tensor("BIASC", [128, 16], F32))
    E2 = e(nc.sbuf_tensor("E2", [8, 128], F32))
    EC = e(nc.sbuf_tensor("EC", [32, 128], F32))
    ES = e(nc.sbuf_tensor("ES", [32, 128], F32))
    PP = [e(nc.psum_tensor(f"PP{i}", [128, 1024], F32)) for i in range(4)]

    blk = e(nc.Block())
    add = sc.add

    def Xc(c):
        return XR[:, T * c:T * (c + 1)]

    def LN(c):
        return LNR[:, T * c:T * (c + 1)]

    def QNt(i):
        return QN[:, T * i:T * (i + 1)]

    def QRt(i):
        return QR[:, T * i:T * (i + 1)]

    def KVCB(i):
        return QNt(i)

    def KVCN(i):
        return QNt(4 + i)

    MEAN = ROWB[0:1, 0:T]
    AUX = ROWB[0:1, T:2 * T]
    DENR = ROWB[0:1, 0:512]
    FRQ = QN[:, 0:384].bitcast(F32)
    GT = SCR[0:8, 0:T]
    GDAT = QN[:, 384:512].bitcast(F32)
    GTMP = QN[:, 512:624].bitcast(F32)
    def SG(i, n=1):
        return SCR[:, 512 * i:512 * (i + n)]

    PKEY = [("P0A", "P0B"), ("P1A", "P1B"), ("P2A", "P2B"), ("P3A", "P3B")]

    def pk(i):
        return PKEY[i]

    def ld(en, dst, src, wbuf, reads=(), tag="ld", dgroup="init"):
        def f(g, sem, dst=dst, src=src):
            return g.dma_start(out=dst, in_=src).then_inc(sem, 16)
        return add(en, f, reads=reads, writes=(wbuf,) if isinstance(wbuf, str)
                   else tuple(wbuf), n_dma=1, tag=tag, dgroup=dgroup)

    for dst, src, b in ((IDF[:, :], ident_f, "IDF"), (IDB[:, :], idb_c, "IDB"),
                        (PSW[:, :], pswap_c, "PSW"), (TRIt[:, :], tri_c, "TRI"),
                        (ONC[:, :], onc_c, "ONC"), (ONRF[:, :], onrf_c, "ONRF"),
                        (ONB[:, :], onb_c, "ONB"),
                        (MB20[:, :], mb20_c, "MB20"),
                        (E2[:, :], e2_c, "E2"),
                        (EC[:, :], ec_c, "EC"), (ES[:, :], es_c, "ES"),
                        (ABIAS[:, :], abias_i, "ABIAS"),
                        (RB8[:, :], rb8_i, "RB8"), (KVGB[:, :], kvgb_i, "KVGB"),
                        (TBL[:, 0:T], c2a_i, "TBL"),
                        (TBL[:, T:2 * T], s2a_i, "TBLS")):
        ld("sync", dst, src, b)
    for c in range(NCH):
        ld("sync", Xc(c), x_in[128 * c:128 * (c + 1), :], f"X{c}", dgroup="x")
    ld("sync", BIASC[:, :], biasT_i, "BIASC", dgroup="init")
    ld("gpsimd", RAs[:, :], aall, "RA", dgroup="ginit")
    ld("gpsimd", RBs[:, :], ball, "RB_", dgroup="ginit")
    ld("gpsimd", RRs[:, :], routr, "RR", dgroup="ginit")

    tap_outs = {}

    def tap(name, ap, keys, cast=False):
        if name not in taps:
            return
        shp = list(ap.shape)
        o = nc.dram_tensor(f"tap_{name}", shp, F32, kind="ExternalOutput").ap()
        tap_outs[name] = True
        ld("gpsimd" if cast else "sync", o, ap, f"TAPO_{name}",
           reads=tuple(keys), tag=f"tap{name}", dgroup="tap")

    def load_w(wap, o, cols, slot, csel=None, tag="w"):
        if csel is None:
            src = wap[o].rearrange("p c j -> p (c j)")
        else:
            src = wap[o, :, csel[0]:csel[1], :].rearrange("p c j -> p (c j)")
        dst = WBUF[:, 4096 * slot:4096 * slot + cols]
        return ld("gpsimd", dst, src, f"WB{slot}", tag=tag,
                  dgroup=f"wb{slot}")

    def wt(slot, c, ob=128):
        return WBUF[:, 4096 * slot + ob * c:4096 * slot + ob * (c + 1)]

    def load_w_og(wap, g, o, slot, tag="w"):
        dst = WBUF[:, 4096 * slot:4096 * slot + 512]
        return ld("gpsimd", dst, wap[g, o], f"WB{slot}", tag=tag,
                  dgroup=f"wb{slot}")

    def emit_ln(src, skey, dst, dkey, nch, tag, gb=None):
        dsum = float(128 * nch)
        for c in range(nch):
            def f(s, sem, c=c):
                return s.activation(dst(c), src(c), AF.Copy)
            add("scalar", f, reads=(skey(c),), writes=(dkey(c),),
                tag=f"{tag}cast{c}")
        for c in range(nch):
            sqb = SG(c % 2, 1).bitcast(BF16)
            def fsq(s, sem, c=c, sqb=sqb):
                return s.activation(sqb, dst(c), AF.Square)
            add("scalar", fsq, reads=(dkey(c),),
                writes=(f"SCR_{'A' if c % 2 == 0 else 'B'}",),
                tag=f"{tag}sq{c}")
            for t in range(2):
                def fmu(te, sem, c=c, t=t):
                    return te.matmul(PP[2][0:1, 512 * t:512 * (t + 1)],
                                     ONC[:, :],
                                     dst(c)[:, 512 * t:512 * (t + 1)],
                                     start=(c == 0), stop=(c == nch - 1),
                                     skip_group_check=True)
                add("tensor", fmu, reads=(dkey(c), "ONC"),
                    writes=(pk(2)[t],), tag=f"{tag}mu{c}_{t}")
                def fs2(te, sem, c=c, t=t, sqb=sqb):
                    return te.matmul(PP[3][0:1, 512 * t:512 * (t + 1)],
                                     ONC[:, :], sqb[:, 512 * t:512 * (t + 1)],
                                     start=(c == 0), stop=(c == nch - 1),
                                     skip_group_check=True)
                add("tensor", fs2,
                    reads=(f"SCR_{'A' if c % 2 == 0 else 'B'}", "ONC"),
                    writes=(pk(3)[t],), tag=f"{tag}sm{c}_{t}")
        def f1(v, sem):
            return v.tensor_single_scalar(MEAN, PP[2][0:1, 0:T], 1.0 / dsum,
                                          ALU.mult)
        add("vector", f1, reads=("P2A", "P2B"), writes=("ROWA",))
        def f2(v, sem):
            return v.tensor_mul(AUX, MEAN, MEAN)
        add("vector", f2, reads=("ROWA",), writes=("ROWX",))
        def f3(v, sem):
            return v.scalar_tensor_tensor(AUX, PP[3][0:1, 0:T], 1.0 / dsum,
                                          AUX, ALU.mult, ALU.subtract)
        add("vector", f3, reads=("P3A", "P3B", "ROWX"), writes=("ROWX",))
        def f3b(v, sem):
            return v.tensor_single_scalar(AUX, AUX, EPS, ALU.add)
        add("vector", f3b, reads=("ROWX",), writes=("ROWX",))
        def f4(s, sem):
            return s.activation(AUX, AUX, AF.Ln)
        add("scalar", f4, reads=("ROWX",), writes=("ROWX",))
        def f5(s, sem):
            return s.activation(AUX, AUX, AF.Exp, scale=-0.5)
        add("scalar", f5, reads=("ROWX",), writes=("ROWX",))
        def f6(v, sem):
            return v.tensor_mul(MEAN, MEAN, AUX)
        add("vector", f6, reads=("ROWA", "ROWX"), writes=("ROWA",))
        for t in range(2):
            def fb(te, sem, t=t):
                return te.matmul(PP[2][:, 512 * t:512 * (t + 1)], ONRF[:, :],
                                 AUX[0:1, 512 * t:512 * (t + 1)],
                                 start=True, stop=True)
            add("tensor", fb, reads=("ROWX", "ONRF"), writes=(pk(2)[t],),
                tag=f"{tag}rb{t}")
            def fb2(te, sem, t=t):
                return te.matmul(PP[3][:, 512 * t:512 * (t + 1)], ONRF[:, :],
                                 MEAN[0:1, 512 * t:512 * (t + 1)],
                                 start=True, stop=True)
            add("tensor", fb2, reads=("ROWA", "ONRF"), writes=(pk(3)[t],),
                tag=f"{tag}mb{t}")
        MB = SG(0, 2)
        def fmb(v, sem):
            return v.tensor_copy(MB, PP[3][:, 0:T])
        add("vector", fmb, reads=("P3A", "P3B"), writes=("SCR_A", "SCR_B"),
            tag=f"{tag}mbev")
        for c in range(nch):
            pp = PP[c % 2]
            pkk = pk(c % 2)
            def fa(v, sem, c=c, pp=pp):
                return v.tensor_mul(pp[:, 0:T], src(c), PP[2][:, 0:T])
            add("vector", fa, reads=(skey(c), "P2A", "P2B"),
                writes=(pkk[0], pkk[1]), tag=f"{tag}a1_{c}")
            if gb is None:
                def fo(v, sem, c=c, pp=pp):
                    return v.tensor_sub(dst(c), pp[:, 0:T], MB)
                add("vector", fo, reads=(pkk[0], pkk[1], "SCR_A", "SCR_B"),
                    writes=(dkey(c),), tag=f"{tag}a2_{c}")
            else:
                def fo(v, sem, c=c, pp=pp):
                    return v.tensor_sub(PP[3][:, 0:T], pp[:, 0:T], MB)
                add("vector", fo,
                    reads=(pkk[0], pkk[1], "SCR_A", "SCR_B", "P3A", "P3B"),
                    writes=("P3A", "P3B"), tag=f"{tag}a2_{c}")
                def fg(v, sem, c=c):
                    return v.tensor_scalar(dst(c), PP[3][:, 0:T],
                                           KVGB[:, c:c + 1],
                                           KVGB[:, 4 + c:5 + c],
                                           ALU.mult, ALU.add)
                add("vector", fg, reads=("P3A", "P3B", "KVGB"),
                    writes=(dkey(c),), tag=f"{tag}a3_{c}")

    def rope_fm(pp, ppkeys, dst, dkey, rows, c2, s2, tag):
        qfv = [SG(2, 1).bitcast(BF16)[0:128, 512 * t:512 * (t + 1)]
               for t in range(2)]
        for t in range(2):
            def f1(v, sem, t=t):
                return v.tensor_copy(qfv[t][0:rows, :],
                                     pp[0:rows, 512 * t:512 * (t + 1)])
            add("vector", f1, reads=ppkeys, writes=(f"SCR_C",),
                tag=f"{tag}qf{t}")
            def f2(te, sem, t=t):
                return te.matmul(PP[2][0:rows, 512 * t:512 * (t + 1)],
                                 PSW[0:rows, 0:rows], qfv[t][0:rows, :],
                                 start=True, stop=True)
            add("tensor", f2, reads=("SCR_C", "PSW"), writes=(pk(2)[t],),
                tag=f"{tag}sw{t}")
        for t in range(2):
            t1 = SG(3, 1)
            def f3(v, sem, t=t, t1=t1):
                return v.tensor_mul(t1[0:rows, :],
                                    pp[0:rows, 512 * t:512 * (t + 1)],
                                    c2[0:rows, 512 * t:512 * (t + 1)])
            add("vector", f3, reads=ppkeys + ("TBL",), writes=("SCR_D",),
                tag=f"{tag}t1_{t}")
            t2 = SG(1, 1)
            def f4(v, sem, t=t, t2=t2):
                return v.tensor_mul(t2[0:rows, :],
                                    PP[2][0:rows, 512 * t:512 * (t + 1)],
                                    s2[0:rows, 512 * t:512 * (t + 1)])
            add("vector", f4, reads=(pk(2)[t], "TBL"), writes=("SCR_B",),
                tag=f"{tag}t2_{t}")
            def f5(v, sem, t=t, t1=t1, t2=t2):
                return v.tensor_add(dst[0:rows, 512 * t:512 * (t + 1)],
                                    t1[0:rows, :], t2[0:rows, :])
            add("vector", f5, reads=("SCR_D", "SCR_B"), writes=(dkey,),
                tag=f"{tag}o{t}")

    def fm_mms(wslot, nch_in, srcf, skeyf, ppi, tag, start_new=True):
        pp = PP[ppi]
        def fm(te, sem):
            r = None
            for c in range(nch_in):
                for t in range(2):
                    r = te.matmul(pp[:, 512 * t:512 * (t + 1)], wt(wslot, c),
                                  srcf(c)[:, 512 * t:512 * (t + 1)],
                                  start=(start_new and c == 0), stop=False,
                                  skip_group_check=True)
            return r
        rd = tuple([f"WB{wslot}"] + [skeyf(c) for c in range(nch_in)])
        add("tensor", fm, reads=rd, writes=pk(ppi), tag=tag)
        return pp

    emit_ln(Xc, lambda c: f"X{c}", LN, lambda c: f"LN{c}", NCH, "ln1")

    for c in range(NCH):
        tap(f"ln1_{c}", LN(c), (f"LN{c}",), cast=True)

    C2A = TBL[:, 0:T]
    S2A = TBL[:, T:2 * T]

    def attn_proj_kq(wap, n_o, dst_of, dkey_of, stage_dma, tag):
        for h in range(n_o):
            slot = h % 2
            load_w(wap, h, 2048, slot, tag=f"{tag}w{h}")
            pp = fm_mms(slot, NCH, LN, lambda c: f"LN{c}", h % 2,
                        f"{tag}mm{h}")
            rope_fm(pp, pk(h % 2), dst_of(h), dkey_of(h), 128, C2A, S2A,
                    f"{tag}ro{h}")
            if stage_dma is not None:
                stage_dma(h)

    import os as _os
    _chop0 = int(_os.environ.get("KCHOP", "0"))
    if stages >= 1 and _chop0 >= 3:
        C2A_ = TBL[:, 0:T]
        S2A_ = TBL[:, T:2 * T]
        for hh in range(4):
            h = hh
            slot = h % 2
            load_w(wqa, h, 2048, slot, tag=f"qaw{h}")
            pp = fm_mms(slot, NCH, LN, lambda c: f"LN{c}", h % 2, f"qamm{h}")
            rope_fm(pp, pk(h % 2), QNt(hh), f"Q{hh}", 128, C2A_, S2A_,
                    f"qaro{h}")
            tap(f"q0_{hh}", QNt(hh), (f"Q{hh}",), cast=True)
    if stages >= 1 and _chop0 < 3 and _chop0 != 7:
        def kst_dma(h):
            slot = h % 2
            ld("sync", ka_in.ap()[128 * h:128 * (h + 1), :],
               STG[:, T * slot:T * (slot + 1)], f"DRAM_ka{h}",
               reads=(f"STG{slot}",), tag=f"kadma{h}", dgroup=f"ka{h % 2}")
        attn_proj_kq(wka, H, lambda h: STG[:, T * (h % 2):T * (h % 2 + 1)],
                     lambda h: f"STG{h % 2}", kst_dma, "ka")

        pass
    if stages >= 1 and _chop0 < 3 and _chop0 != 6:
        def cc_ka(g, sem):
            return g.collective_compute(
                "AllGather", ALU.bypass, replica_groups=CCG,
                ins=[ka_in.ap().opt()], outs=[ka_out.ap().opt()])
        add("gpsimd", cc_ka,
            reads=tuple([f"DRAM_ka{h}" for h in range(H)]),
            writes=("DRAM_kaout",), coll=True, tag="cc_ka")
        for o in range(8):
            slot = o % 2
            load_w(wva, o, 4096, slot, tag=f"vaw{o}")
            for tt in range(8):
                ppi = tt % 2
                pp = PP[ppi]
                def fv(te, sem, slot=slot, tt=tt, pp=pp):
                    r = None
                    for c in range(NCH):
                        r = te.matmul(pp[:, 0:256],
                                      LN(c)[:, 128 * tt:128 * (tt + 1)],
                                      wt(slot, c, 256),
                                      start=(c == 0), stop=(c == NCH - 1),
                                      skip_group_check=True)
                    return r
                add("tensor", fv,
                    reads=tuple([f"WB{slot}"] + [f"LN{c}" for c in range(NCH)]),
                    writes=pk(ppi), tag=f"vamm{o}_{tt}")
                vs = STG[:, T * (tt % 2) + 256:T * (tt % 2) + 512]
                def fe(v, sem, pp=pp, vs=vs):
                    return v.tensor_copy(vs[:, 0:256], pp[:, 0:256])
                add("vector", fe, reads=pk(ppi), writes=(f"STG{tt % 2}",),
                    tag=f"vaev{o}_{tt}")
                ld("sync", va_in.ap()[128 * tt:128 * (tt + 1),
                                      256 * o:256 * (o + 1)],
                   vs[:, 0:256], f"DRAM_va{o}_{tt}", reads=(f"STG{tt % 2}",),
                   tag=f"vadma{o}_{tt}", dgroup=f"va{tt % 2}")

        pass
    if stages >= 1 and _chop0 < 3 and _chop0 not in (6, 7):
        def cc_va(g, sem):
            return g.collective_compute(
                "AllGather", ALU.bypass, replica_groups=CCG,
                ins=[va_in.ap().opt()], outs=[va_out.ap().opt()])
        add("gpsimd", cc_va,
            reads=tuple([f"DRAM_va{o}_{t}" for o in range(8)
                         for t in range(8)]),
            writes=("DRAM_vaout",), coll=True, tag="cc_va")

        pass
    if stages >= 1 and _chop0 < 3:
        tap("kain", ka_in.ap(),
            tuple(f"DRAM_ka{h}" for h in range(H)), cast=True)
        tap("kaout", ka_out.ap(), ("DRAM_kaout",), cast=True)
        tap("vain", va_in.ap(),
            tuple(f"DRAM_va{o}_{t}" for o in range(8) for t in range(8)),
            cast=True)

        def head_loads(h, layer):
            slot = h % 2
            kout = ka_out if layer == 0 else kb_out
            vout = va_out if layer == 0 else vb_out
            kok = "DRAM_kaout" if layer == 0 else "DRAM_kbout"
            vok = "DRAM_vaout" if layer == 0 else "DRAM_vbout"
            rowB = 2048 if layer == 0 else 2112
            ld("sync", KH[:, 2048 * slot:2048 * slot + 1024],
               kout.ap()[128 * h:128 * (h + 1), :], f"KH{slot}",
               reads=(kok,), tag=f"khA{h}", dgroup=f"kh{slot}")
            ld("sync", KH[:, 2048 * slot + 1024:2048 * slot + 2048],
               kout.ap()[rowB + 128 * h:rowB + 128 * (h + 1), :], f"KH{slot}",
               reads=(kok,), tag=f"khB{h}", dgroup=f"kh{slot}")
            for half in range(4):
                src = vout.ap()[:, 128 * h:128 * (h + 1)].rearrange(
                    "(k p) d -> p k d", p=128)[:, 4 * half:4 * (half + 1), :]
                dst = VH[:, 2048 * slot + 512 * half:
                         2048 * slot + 512 * (half + 1)].rearrange(
                    "p (k d) -> p k d", d=128)
                ld("sync", dst, src, f"VH{slot}", reads=(vok,),
                   tag=f"vh{h}_{half}", dgroup=f"vh{slot}")

        def head_loads_own(h):
            slot = h % 2
            ld("sync", KOWN[:, T * slot:T * (slot + 1)],
               ka_in.ap()[128 * h:128 * (h + 1), :], f"KOWN{slot}",
               reads=(f"DRAM_ka{h}",), tag=f"ko{h}", dgroup=f"ko{slot}")
            src = va_in.ap()[:, 128 * h:128 * (h + 1)].rearrange(
                "(k p) d -> p k d", p=128)
            ld("sync", VOWN[:, T * slot:T * (slot + 1)].rearrange(
                "p (k d) -> p k d", d=128), src, f"VOWN{slot}",
               reads=tuple([f"DRAM_va{h // 2}_{tt}" for tt in range(8)]),
               tag=f"vo{h}", dgroup=f"vo{slot}")

        def attn_head(h, hh, layer, qrope_src=None):
            slot = h % 2
            scale = SCALE_A if layer == 0 else SCALE_B
            qtile = QNt(hh)
            if layer == 1:
                emitters = []
                for it_i in range(16):
                    ps = it_i % 2
                    psl = PP[0] if ps == 0 else PP[3]
                    pskey = ("P0A", "P0B") if ps == 0 else ("P3A", "P3B")
                    ptv = STG[:, 1024 * ps:1024 * (ps + 1)]
                    ptkey = f"STG{ps}"
                    first = it_i == 0
                    def es(it_i=it_i, psl=psl, pskey=pskey):
                        def fs(te, sem, idx=it_i, psl=psl):
                            r = None
                            for qb in range(2):
                                te.matmul(
                                    psl[:, 512 * qb:512 * (qb + 1)],
                                    KH[:, 2048 * slot + 128 * idx:
                                       2048 * slot + 128 * (idx + 1)],
                                    qtile[:, 512 * qb:512 * (qb + 1)],
                                    start=True, stop=False,
                                    skip_group_check=True)
                            qrt, rbase = qrope_src
                            for qb in range(2):
                                r = te.matmul(
                                    psl[:, 512 * qb:512 * (qb + 1)],
                                    KRB[rbase:rbase + 64,
                                        128 * idx:128 * (idx + 1)],
                                    qrt[rbase:rbase + 64,
                                        512 * qb:512 * (qb + 1)],
                                    start=False, stop=True,
                                    skip_group_check=True)
                            return r
                        add("tensor", fs,
                            reads=(f"KH{slot}", f"Q{hh}", "KRB", "QROPE"),
                            writes=pskey, tag=f"S{h}_m{it_i}")
                    def ee(it_i=it_i, psl=psl, ptv=ptv, pskey=pskey,
                           ptkey=ptkey):
                        def fe_(s, sem, psl=psl, ptv=ptv):
                            return s.activation(ptv[:, :], psl[:, 0:1024],
                                                AF.Exp, bias=MB20[:, 0:1],
                                                scale=scale)
                        add("scalar", fe_, reads=pskey + ("MB20",),
                            writes=(ptkey,), tag=f"E{h}_m{it_i}")
                    def ep(it_i=it_i, ptv=ptv, ptkey=ptkey, first=first):
                        vsrc = VH[:, 2048 * slot + 128 * it_i:
                                  2048 * slot + 128 * (it_i + 1)]
                        def fpv(te, sem, vsrc=vsrc, ptv=ptv, first=first):
                            r = None
                            for qb in range(2):
                                te.matmul(
                                    PP[1][:, 512 * qb:512 * (qb + 1)],
                                    vsrc, ptv[:, 512 * qb:512 * (qb + 1)],
                                    start=first, stop=False,
                                    skip_group_check=True)
                                r = te.matmul(
                                    PP[2][:, 512 * qb:512 * (qb + 1)],
                                    ONB[:, :],
                                    ptv[:, 512 * qb:512 * (qb + 1)],
                                    start=first, stop=False,
                                    skip_group_check=True)
                            return r
                        add("tensor", fpv, reads=(ptkey, f"VH{slot}", "ONB"),
                            writes=(pk(1)[0], pk(1)[1], pk(2)[0], pk(2)[1]),
                            tag=f"PV{h}_m{it_i}")
                    emitters.append((es, ee, ep))
                emitters[0][0]()
                for i in range(1, 16):
                    emitters[i - 1][1]()
                    emitters[i][0]()
                    emitters[i - 1][2]()
                emitters[-1][1]()
                emitters[-1][2]()
                RINV2 = SG(0, 2)
                def frc(v, sem):
                    return v.reciprocal(RINV2, PP[2][:, 0:1024])
                add("vector", frc, reads=("P2A", "P2B"),
                    writes=("SCR_A", "SCR_B"), tag=f"rc{h}")
                att = QRt(hh)
                def fat(v, sem, att=att):
                    return v.tensor_mul(att[:, 0:1024], PP[1][:, 0:1024],
                                        RINV2)
                add("vector", fat, reads=("P1A", "P1B", "SCR_A", "SCR_B"),
                    writes=(f"QR{hh}",), tag=f"at{h}")
                return
            for qb in range(2):
                items = []
                if layer == 0:
                    for tw in range(4):
                        items.append(("d", tw, None))
                    for p_, kt in enumerate(ITEMS_QB[qb]):
                        items.append(("g", kt,
                                      (p_ if qb == 0 else 4 + p_)))
                else:
                    for kt in range(16):
                        items.append(("g", kt, None))
                emitters = []
                for it_i, (kind, idx, bcol) in enumerate(items):
                    ps_i = it_i % 2
                    pskey = pk(0)[ps_i]
                    psl = PP[0][:, 512 * ps_i:512 * (ps_i + 1)]
                    ptv = PTS[:, 512 * ps_i:512 * (ps_i + 1)]
                    ptkey = f"PT{ps_i}"
                    first = it_i == 0
                    if kind == "g":
                        qoff = 0
                        def es(it_i=it_i, idx=idx, psl=psl, pskey=pskey,
                               slot=slot, qb=qb, qtile=qtile):
                            def fs(te, sem, idx=idx, psl=psl, slot=slot,
                                   qb=qb, qtile=qtile):
                                r = te.matmul(
                                    psl[:, :],
                                    KH[:, 2048 * slot + 128 * idx:
                                       2048 * slot + 128 * (idx + 1)],
                                    qtile[:, 512 * qb:512 * (qb + 1)],
                                    start=True, stop=(layer == 0),
                                    skip_group_check=True)
                                if layer == 1:
                                    qrt, rbase = qrope_src
                                    r = te.matmul(
                                        psl[:, :],
                                        KRB[rbase:rbase + 64,
                                            128 * idx:128 * (idx + 1)],
                                        qrt[rbase:rbase + 64,
                                            512 * qb:512 * (qb + 1)],
                                        start=False, stop=True,
                                        skip_group_check=True)
                                return r
                            add("tensor", fs,
                                reads=(f"KH{slot}", f"Q{hh}") if layer == 0
                                else (f"KH{slot}", f"Q{hh}", "KRB", "QROPE"),
                                writes=(pskey,), tag=f"S{h}_{qb}_{it_i}")
                        if layer == 0:
                            def ee(it_i=it_i, psl=psl, ptv=ptv, bcol=bcol,
                                   pskey=pskey, ptkey=ptkey):
                                def fe_(s, sem, psl=psl, ptv=ptv, bcol=bcol):
                                    return s.activation(
                                        ptv[:, :], psl[:, :], AF.Exp,
                                        bias=ABIAS[:, bcol:bcol + 1],
                                        scale=scale)
                                add("scalar", fe_, reads=(pskey, "ABIAS"),
                                    writes=(ptkey,), tag=f"E{h}_{qb}_{it_i}")
                        else:
                            def ee(it_i=it_i, psl=psl, ptv=ptv,
                                   pskey=pskey, ptkey=ptkey):
                                def fe_(s, sem, psl=psl, ptv=ptv):
                                    return s.activation(ptv[:, :], psl[:, :],
                                                        AF.Exp,
                                                        bias=MB20[:, 0:1],
                                                        scale=scale)
                                add("scalar", fe_, reads=(pskey, "MB20"),
                                    writes=(ptkey,), tag=f"E{h}_{qb}_{it_i}")
                        vsrc = VH[:, 2048 * slot + 128 * idx:
                                  2048 * slot + 128 * (idx + 1)]
                        vkey = f"VH{slot}"
                    else:
                        tw = idx
                        qoff = 128 * tw
                        def es(it_i=it_i, tw=tw, psl=psl, pskey=pskey,
                               slot=slot, qb=qb, qtile=qtile, qoff=qoff):
                            def fs(te, sem, tw=tw, psl=psl, slot=slot, qb=qb,
                                   qtile=qtile, qoff=qoff):
                                te.matmul(
                                    psl[:, qoff:512],
                                    KOWN[:, T * slot + 512 * qb + qoff:
                                         T * slot + 512 * qb + qoff + 128],
                                    qtile[:, 512 * qb + qoff:512 * (qb + 1)],
                                    start=True, stop=False,
                                    skip_group_check=True)
                                return te.matmul(
                                    psl[:, qoff:qoff + 128], IDB[:, :],
                                    TRIt[:, :], start=False, stop=True,
                                    skip_group_check=True)
                            add("tensor", fs,
                                reads=(f"KOWN{slot}", f"Q{hh}", "IDB", "TRI"),
                                writes=(pskey,), tag=f"Sd{h}_{qb}_{tw}")
                        def ee(it_i=it_i, psl=psl, ptv=ptv, qoff=qoff,
                               pskey=pskey, ptkey=ptkey, tw=tw):
                            def fe_(s, sem, psl=psl, ptv=ptv, qoff=qoff):
                                return s.activation(ptv[:, qoff:512],
                                                    psl[:, qoff:512], AF.Exp,
                                                    bias=MB20[:, 0:1],
                                                    scale=scale)
                            add("scalar", fe_, reads=(pskey, "MB20"),
                                writes=(ptkey,), tag=f"Ed{h}_{qb}_{tw}")
                        vsrc = VOWN[:, T * slot + 512 * qb + qoff:
                                    T * slot + 512 * qb + qoff + 128]
                        vkey = f"VOWN{slot}"
                    pvkey = pk(1)[qb]
                    denkey = pk(2)[qb]
                    def ep(it_i=it_i, vsrc=vsrc, ptv=ptv, qoff=qoff, qb=qb,
                           first=first, ptkey=ptkey, vkey=vkey,
                           pvkey=pvkey, denkey=denkey):
                        def fpv(te, sem, vsrc=vsrc, ptv=ptv, qoff=qoff,
                                qb=qb, first=first):
                            te.matmul(
                                PP[1][:, 512 * qb + qoff:512 * (qb + 1)],
                                vsrc, ptv[:, qoff:512], start=first,
                                stop=False, skip_group_check=True)
                            return te.matmul(
                                PP[2][:, 512 * qb + qoff:512 * (qb + 1)],
                                ONB[:, :], ptv[:, qoff:512], start=first,
                                stop=False, skip_group_check=True)
                        add("tensor", fpv, reads=(ptkey, vkey, "ONB"),
                            writes=(pvkey, denkey), tag=f"PV{h}_{qb}_{it_i}")
                    emitters.append((es, ee, ep))
                emitters[0][0]()
                for i in range(1, len(emitters)):
                    emitters[i - 1][1]()
                    emitters[i][0]()
                    emitters[i - 1][2]()
                emitters[-1][1]()
                emitters[-1][2]()
                def frc(v, sem, qb=qb):
                    return v.reciprocal(RINV[:, :],
                                        PP[2][:, 512 * qb:512 * (qb + 1)])
                add("vector", frc, reads=(pk(2)[qb],), writes=("RINV",),
                    tag=f"rc{h}_{qb}")
                att = QRt(hh)
                def fat(v, sem, qb=qb, att=att):
                    return v.tensor_mul(att[:, 512 * qb:512 * (qb + 1)],
                                        PP[1][:, 512 * qb:512 * (qb + 1)],
                                        RINV[:, :])
                add("vector", fat, reads=(pk(1)[qb], "RINV"),
                    writes=(f"QR{hh}",), tag=f"at{h}_{qb}")

        def out_proj(wap, g, tag):
            for o in range(16):
                slot = o % 2
                load_w_og(wap, g, o, slot, tag=f"{tag}w{g}_{o}")
                ppi = o % 2
                pp = PP[ppi]
                def fm(te, sem, slot=slot, pp=pp):
                    r = None
                    for cc in range(4):
                        for t in range(2):
                            r = te.matmul(pp[:, 512 * t:512 * (t + 1)],
                                          wt(slot, cc),
                                          QRt(cc)[:, 512 * t:512 * (t + 1)],
                                          start=(cc == 0), stop=(cc == 3),
                                          skip_group_check=True)
                    return r
                add("tensor", fm,
                    reads=tuple([f"WB{slot}"] + [f"QR{cc}" for cc in range(4)]),
                    writes=pk(ppi), tag=f"{tag}mm{g}_{o}")
                def fr(v, sem, o=o, pp=pp):
                    return v.tensor_add(Xc(o), pp[:, 0:T], Xc(o))
                add("vector", fr, reads=pk(ppi) + (f"X{o}",),
                    writes=(f"X{o}",), tag=f"{tag}res{g}_{o}")

        import os as _os
        _chop = int(_os.environ.get("KCHOP", "0"))
        for g in range(4):
            if _chop >= 2 and g > 0:
                break
            for hh in range(4):
                h = 4 * g + hh
                slot = h % 2
                load_w(wqa, h, 2048, slot, tag=f"qaw{h}")
                pp = fm_mms(slot, NCH, LN, lambda c: f"LN{c}", h % 2,
                            f"qamm{h}")
                rope_fm(pp, pk(h % 2), QNt(hh), f"Q{hh}", 128, C2A, S2A,
                        f"qaro{h}")
            if g == 0:
                for hh in range(4):
                    tap(f"q0_{hh}", QNt(hh), (f"Q{hh}",), cast=True)
            if _chop >= 1:
                continue
            for hh in range(4):
                h = 4 * g + hh
                head_loads(h, 0)
                head_loads_own(h)
                attn_head(h, hh, 0)
            if g == 0:
                for hh in range(4):
                    tap(f"att0_{hh}", QRt(hh), (f"QR{hh}",), cast=True)
            out_proj(woa, g, "oa")

    def ffn(tag):
        for tt in range(8):
            ppi = tt % 2
            def fl(te, sem, tt=tt, ppi=ppi):
                r = None
                for c in range(NCH):
                    r = te.matmul(PP[ppi][:, 0:8],
                                  LN(c)[:, 128 * tt:128 * (tt + 1)],
                                  RRs[:, 8 * c:8 * (c + 1)],
                                  start=(c == 0), stop=(c == NCH - 1),
                                  skip_group_check=True)
                return r
            add("tensor", fl,
                reads=tuple([f"LN{c}" for c in range(NCH)] + ["RR"]),
                writes=pk(ppi), tag=f"{tag}lg{tt}")
            sl = 28 * (tt % 2)
            lg = GTMP[:, sl:sl + 8]
            ex = GTMP[:, sl + 8:sl + 16]
            msk = GTMP[:, sl + 16:sl + 24]
            sel = GTMP[:, sl + 16:sl + 24]
            m1 = GTMP[:, sl + 24:sl + 25]
            m2 = GTMP[:, sl + 25:sl + 26]
            nm1 = GTMP[:, sl + 26:sl + 27]
            s2c = GTMP[:, sl + 27:sl + 28]
            gsl = f"GT{tt % 2}"
            def g1(v, sem, tt=tt, ppi=ppi, lg=lg):
                return v.tensor_add(lg, PP[ppi][:, 0:8], RB8[:, :])
            add("vector", g1, reads=pk(ppi) + ("RB8",), writes=(gsl,))
            def g2(v, sem, lg=lg, m1=m1):
                return v.reduce_max(m1, lg, axis=AX.X)
            add("vector", g2, reads=(gsl,), writes=(gsl,))
            def g3(v, sem, m1=m1, nm1=nm1):
                return v.tensor_single_scalar(nm1, m1, -1.0, ALU.mult)
            add("vector", g3, reads=(gsl,), writes=(gsl,))
            def g4(v, sem, lg=lg, m1=m1, msk=msk):
                return v.tensor_single_scalar(msk, lg, m1[:, 0:1],
                                              ALU.is_equal)
            add("vector", g4, reads=(gsl,), writes=("GMSK",))
            def g5(v, sem, lg=lg, msk=msk):
                return v.scalar_tensor_tensor(msk, msk, NEG, lg, ALU.mult,
                                              ALU.add)
            add("vector", g5, reads=("GMSK", gsl), writes=("GMSK",))
            def g6(v, sem, msk=msk, m2=m2):
                return v.reduce_max(m2, msk, axis=AX.X)
            add("vector", g6, reads=("GMSK",), writes=(gsl,))
            def g7(s, sem, lg=lg, nm1=nm1, ex=ex):
                return s.activation(ex, lg, AF.Exp, bias=nm1[:, 0:1])
            add("scalar", g7, reads=(gsl,), writes=("GEX",))
            def g8(v, sem, lg=lg, m2=m2, sel=sel):
                return v.tensor_single_scalar(sel, lg, m2[:, 0:1], ALU.is_ge)
            add("vector", g8, reads=(gsl,), writes=("GSEL",))
            def g9(v, sem, ex=ex, sel=sel):
                return v.tensor_mul(ex, ex, sel)
            add("vector", g9, reads=("GEX", "GSEL"), writes=("GEX",))
            def g10(v, sem, ex=ex, s2c=s2c):
                return v.reduce_sum(s2c, ex, axis=AX.X)
            add("vector", g10, reads=("GEX",), writes=(gsl,))
            def g11(v, sem, s2c=s2c):
                return v.reciprocal(s2c, s2c)
            add("vector", g11, reads=(gsl,), writes=(gsl,))
            def g12(v, sem, ex=ex, s2c=s2c, tt=tt):
                return v.tensor_scalar_mul(GDAT[:, 8 * tt:8 * (tt + 1)], ex,
                                           s2c[:, 0:1])
            add("vector", g12, reads=("GEX", gsl), writes=(f"GD{tt}", "QN0"))
            def g13(te, sem, tt=tt):
                return te.transpose(PP[3][0:8, 0:128],
                                    GDAT[:, 8 * tt:8 * (tt + 1)], IDF[:, :])
            add("tensor", g13, reads=(f"GD{tt}", "QN0", "IDF"), writes=("P3A",),
                tag=f"{tag}gt{tt}")
            def g14(v, sem, tt=tt):
                return v.tensor_copy(GT[0:8, 128 * tt:128 * (tt + 1)],
                                     PP[3][0:8, 0:128])
            add("vector", g14, reads=("P3A",), writes=("SCR_A", "SCR_B"),
                tag=f"{tag}gte{tt}")
        def fmid(te, sem):
            r = None
            for c in range(NCH):
                for t in range(2):
                    r = te.matmul(PP[0][:, 512 * t:512 * (t + 1)],
                                  RAs[:, 128 * c:128 * (c + 1)],
                                  LN(c)[:, 512 * t:512 * (t + 1)],
                                  start=(c == 0), stop=(c == NCH - 1),
                                  skip_group_check=True)
            return r
        add("tensor", fmid,
            reads=tuple([f"LN{c}" for c in range(NCH)] + ["RA"]),
            writes=("P0A", "P0B"), tag=f"{tag}mid")
        for t in range(2):
            def fgm(te, sem, t=t):
                return te.matmul(PP[1][:, 512 * t:512 * (t + 1)], E2[:, :],
                                 GT[0:8, 512 * t:512 * (t + 1)], start=True,
                                 stop=True)
            add("tensor", fgm, reads=("SCR_A", "SCR_B", "E2"),
                writes=(pk(1)[t],), tag=f"{tag}gmm{t}")
        GS = SG(0, 2)
        def fgs(v, sem):
            return v.tensor_copy(GS, PP[1][:, 0:T])
        add("vector", fgs, reads=("P1A", "P1B"), writes=("SCR_A", "SCR_B"),
            tag=f"{tag}gs")
        MIDG = SG(3, 1).bitcast(BF16)
        def fmg(v, sem):
            return v.tensor_mul(MIDG, PP[0][:, 0:T], GS)
        add("vector", fmg, reads=("P0A", "P0B", "SCR_A", "SCR_B"),
            writes=("SCR_D",), tag=f"{tag}midg")
        for o in range(16):
            slot = o % 2
            load_w(basew, o, 2048, slot, tag=f"{tag}bw{o}")
            ppi = o % 2
            pp = PP[ppi]
            def fb(te, sem, slot=slot, pp=pp, o=o):
                for c in range(NCH):
                    for t in range(2):
                        te.matmul(pp[:, 512 * t:512 * (t + 1)], wt(slot, c),
                                  LN(c)[:, 512 * t:512 * (t + 1)],
                                  start=(c == 0), stop=False,
                                  skip_group_check=True)
                r = None
                for t in range(2):
                    r = te.matmul(pp[:, 512 * t:512 * (t + 1)],
                                  RBs[:, 128 * o:128 * (o + 1)],
                                  MIDG[:, 512 * t:512 * (t + 1)],
                                  start=False, stop=True,
                                  skip_group_check=True)
                return r
            add("tensor", fb,
                reads=tuple([f"WB{slot}", "RB_", "SCR_D"] +
                            [f"LN{c}" for c in range(NCH)]),
                writes=pk(ppi), tag=f"{tag}base{o}")
            def fr(v, sem, o=o, pp=pp):
                return v.scalar_tensor_tensor(Xc(o), pp[:, 0:T],
                                              BIASC[:, o:o + 1], Xc(o),
                                              ALU.add, ALU.add)
            add("vector", fr, reads=pk(ppi) + (f"X{o}", "BIASC"),
                writes=(f"X{o}",), tag=f"{tag}res{o}")

    if stages >= 2:
        emit_ln(Xc, lambda c: f"X{c}", LN, lambda c: f"LN{c}", NCH, "ln2")
        ffn("f1")

    if stages >= 2.2:
        CTT = SCR[0:32, 0:T]
        STT = SCR[0:32, T:2 * T]
        for tt in range(8):
            th = FRQ[:, 0:32]
            ld("sync", th, freqs[128 * tt:128 * (tt + 1), :], "QN0",
               tag=f"fq{tt}", dgroup="frq")
            a_ = FRQ[:, 32:64]
            b_ = FRQ[:, 64:96]
            m_ = FRQ[:, 96:128]
            ct = FRQ[:, 128:160]
            st = FRQ[:, 160:192]
            def w1(v, sem, th=th, m_=m_):
                return v.tensor_single_scalar(m_, th, PI, ALU.is_gt)
            add("vector", w1, reads=("QN0",), writes=("QN0",))
            def w2(v, sem, th=th, m_=m_, a_=a_):
                return v.scalar_tensor_tensor(a_, m_, -2 * PI, th, ALU.mult,
                                              ALU.add)
            add("vector", w2, reads=("QN0",), writes=("QN0",))
            def w3(s, sem, a_=a_, st=st):
                return s.activation(st, a_, AF.Sin)
            add("scalar", w3, reads=("QN0",), writes=("QN0",))
            def w4(v, sem, th=th, b_=b_):
                return v.tensor_scalar(b_, th, -1.0, PI / 2, ALU.mult, ALU.add)
            add("vector", w4, reads=("QN0",), writes=("QN0",))
            def w5(v, sem, b_=b_, m_=m_):
                return v.tensor_single_scalar(m_, b_, -PI, ALU.is_lt)
            add("vector", w5, reads=("QN0",), writes=("QN0",))
            def w6(v, sem, b_=b_, m_=m_):
                return v.scalar_tensor_tensor(b_, m_, 2 * PI, b_, ALU.mult,
                                              ALU.add)
            add("vector", w6, reads=("QN0",), writes=("QN0",))
            def w7(s, sem, b_=b_, ct=ct):
                return s.activation(ct, b_, AF.Sin)
            add("scalar", w7, reads=("QN0",), writes=("QN0",))
            def w8(te, sem, ct=ct):
                return te.transpose(PP[3][0:32, 0:128], ct, IDF[:, :])
            add("tensor", w8, reads=("QN0", "IDF"), writes=("P3A",))
            def w9(v, sem, tt=tt):
                return v.tensor_copy(CTT[:, 128 * tt:128 * (tt + 1)],
                                     PP[3][0:32, 0:128])
            add("vector", w9, reads=("P3A",), writes=("SCR_A", "SCR_B"),
                tag=f"ctt{tt}")
            def wA(te, sem, st=st):
                return te.transpose(PP[3][0:32, 0:128], st, IDF[:, :])
            add("tensor", wA, reads=("QN0", "IDF"), writes=("P3A",))
            def wB(v, sem, tt=tt):
                return v.tensor_copy(STT[:, 128 * tt:128 * (tt + 1)],
                                     PP[3][0:32, 0:128])
            add("vector", wB, reads=("P3A",), writes=("SCR_C", "SCR_D"),
                tag=f"stt{tt}")
        for t in range(2):
            def x1(te, sem, t=t):
                return te.matmul(PP[2][:, 512 * t:512 * (t + 1)], EC[:, :],
                                 CTT[:, 512 * t:512 * (t + 1)], start=True,
                                 stop=True)
            add("tensor", x1, reads=("SCR_A", "SCR_B", "EC"),
                writes=(pk(2)[t],))
            def x2(v, sem, t=t):
                return v.tensor_copy(TBL[:, 512 * t:512 * (t + 1)],
                                     PP[2][:, 512 * t:512 * (t + 1)])
            add("vector", x2, reads=(pk(2)[t],), writes=("TBL",),
                tag=f"c2b{t}")
            def x3(te, sem, t=t):
                return te.matmul(PP[3][:, 512 * t:512 * (t + 1)], ES[:, :],
                                 STT[:, 512 * t:512 * (t + 1)], start=True,
                                 stop=True)
            add("tensor", x3, reads=("SCR_C", "SCR_D", "ES"),
                writes=(pk(3)[t],))
            def x4(v, sem, t=t):
                return v.tensor_copy(TBL[:, T + 512 * t:T + 512 * (t + 1)],
                                     PP[3][:, 512 * t:512 * (t + 1)])
            add("vector", x4, reads=(pk(3)[t],), writes=("TBL",),
                tag=f"s2b{t}")

        emit_ln(Xc, lambda c: f"X{c}", LN, lambda c: f"LN{c}", NCH, "ln3")
        C2B = TBL[:, 0:T]
        S2B = TBL[:, T:2 * T]

    if stages >= 2.4:
        for o in range(4):
            slot = o % 2
            load_w(wkvac, o, 2048, slot, tag=f"kvaw{o}")
            pp = fm_mms(slot, NCH, LN, lambda c: f"LN{c}", o % 2, f"kvamm{o}")
            def fe_(s, sem, o=o, pp=pp):
                return s.activation(KVCB(o), pp[:, 0:T], AF.Copy)
            add("scalar", fe_, reads=pk(o % 2), writes=(f"QN{o}",),
                tag=f"kvaev{o}")
        load_w(wkvar, 0, 1024, 0, tag="kvrw")
        def fkr(te, sem):
            r = None
            for c in range(NCH):
                for t in range(2):
                    r = te.matmul(PP[0][0:64, 512 * t:512 * (t + 1)],
                                  wt(0, c, 64),
                                  LN(c)[:, 512 * t:512 * (t + 1)],
                                  start=(c == 0), stop=(c == NCH - 1),
                                  skip_group_check=True)
            return r
        add("tensor", fkr,
            reads=tuple(["WB0"] + [f"LN{c}" for c in range(NCH)]),
            writes=("P0A", "P0B"), tag="kvrmm")
        kst = STG[:, 0:T]
        rope_fm(PP[0], ("P0A", "P0B"), kst, "STG0", 64, C2B, S2B, "kvro")
        ld("sync", kb_in.ap()[2048:2112, :], kst[0:64, :], "DRAM_kbr",
           reads=("STG0",), tag="kbrdma", dgroup="kb0")
        emit_ln(lambda i: KVCB(i), lambda i: f"QN{i}",
                lambda i: KVCN(i), lambda i: f"QN{4 + i}", 4, "lnkv",
                gb=True)
        pass
    if stages >= 2.6:
        for o in range(16):
            slot = o % 2
            load_w(wkvbk, o, 512, slot, tag=f"kbkw{o}")
            pp = PP[o % 2]
            def fm(te, sem, slot=slot, pp=pp):
                r = None
                for c in range(4):
                    for t in range(2):
                        r = te.matmul(pp[:, 512 * t:512 * (t + 1)],
                                      wt(slot, c),
                                      KVCN(c)[:, 512 * t:512 * (t + 1)],
                                      start=(c == 0), stop=(c == 3),
                                      skip_group_check=True)
                return r
            add("tensor", fm,
                reads=tuple([f"WB{slot}"] + [f"QN{4 + c}" for c in range(4)]),
                writes=pk(o % 2), tag=f"kbkmm{o}")
            st_ = STG[:, T * (o % 2):T * (o % 2 + 1)]
            def fe_(s, sem, pp=pp, st_=st_):
                return s.activation(st_[:, :], pp[:, 0:T], AF.Copy)
            add("scalar", fe_, reads=pk(o % 2), writes=(f"STG{o % 2}",),
                tag=f"kbkev{o}")
            ld("sync", kb_in.ap()[128 * o:128 * (o + 1), :], st_[:, :],
               f"DRAM_kbk{o}", reads=(f"STG{o % 2}",), tag=f"kbkdma{o}",
               dgroup=f"kb{o % 2}")
        for o in range(8):
            slot = o % 2
            load_w(wkvbv, o, 1024, slot, tag=f"kbvw{o}")
            for tt in range(8):
                ppi = tt % 2
                pp = PP[ppi]
                def fv(te, sem, slot=slot, tt=tt, pp=pp):
                    r = None
                    for c in range(4):
                        r = te.matmul(pp[:, 0:256],
                                      KVCN(c)[:, 128 * tt:128 * (tt + 1)],
                                      wt(slot, c, 256), start=(c == 0),
                                      stop=(c == 3), skip_group_check=True)
                    return r
                add("tensor", fv,
                    reads=tuple([f"WB{slot}"] +
                                [f"QN{4 + c}" for c in range(4)]),
                    writes=pk(ppi), tag=f"kbvmm{o}_{tt}")
                vs = STG[:, T * (tt % 2) + 256:T * (tt % 2) + 512]
                def fe_(v, sem, pp=pp, vs=vs):
                    return v.tensor_copy(vs[:, 0:256], pp[:, 0:256])
                add("vector", fe_, reads=pk(ppi), writes=(f"STG{tt % 2}",),
                    tag=f"kbvev{o}_{tt}")
                ld("sync", vb_in.ap()[128 * tt:128 * (tt + 1),
                                      256 * o:256 * (o + 1)],
                   vs[:, 0:256], f"DRAM_vb{o}_{tt}", reads=(f"STG{tt % 2}",),
                   tag=f"kbvdma{o}_{tt}", dgroup=f"vb{tt % 2}")
        if stages >= 2.55:
            def cc_kb(g, sem):
                return g.collective_compute(
                    "AllGather", ALU.bypass, replica_groups=CCG,
                    ins=[kb_in.ap().opt()], outs=[kb_out.ap().opt()])
            add("gpsimd", cc_kb,
                reads=tuple([f"DRAM_kbk{o}" for o in range(16)] +
                            ["DRAM_kbr"]),
                writes=("DRAM_kbout",), coll=True, tag="cc_kb")
        if stages >= 2.56:
            def cc_vb(g, sem):
                return g.collective_compute(
                    "AllGather", ALU.bypass, replica_groups=CCG,
                    ins=[vb_in.ap().opt()], outs=[vb_out.ap().opt()])
            add("gpsimd", cc_vb,
                reads=tuple([f"DRAM_vb{o}_{t}" for o in range(8)
                             for t in range(8)]),
                writes=("DRAM_vbout",), coll=True, tag="cc_vb")
        for dst_base in ((0, 64) if stages >= 2.58 else ()):
            ld("sync", KRB[dst_base:dst_base + 64, 0:1024],
               kb_out.ap()[2048:2112, :], ("KRB", "KOWN0", "KOWN1"),
               reads=("DRAM_kbout",), tag=f"krb{dst_base}a", dgroup="krb")
            ld("sync", KRB[dst_base:dst_base + 64, 1024:2048],
               kb_out.ap()[4160:4224, :], ("KRB", "KOWN0", "KOWN1"),
               reads=("DRAM_kbout",), tag=f"krb{dst_base}b", dgroup="krb")

        pass
    if stages >= 2.8:
        for g in range(4):
            if stages < 3 and g > 0:
                break
            for hh in range(4):
                h = 4 * g + hh
                slot = h % 2
                load_w(wqb, h, 2048, slot, tag=f"qbnw{h}")
                pp = fm_mms(slot, NCH, LN, lambda c: f"LN{c}", h % 2,
                            f"qbnmm{h}")
                def fe_(s, sem, hh=hh, pp=pp):
                    return s.activation(QNt(hh), pp[:, 0:T], AF.Copy)
                add("scalar", fe_, reads=pk(h % 2), writes=(f"Q{hh}",),
                    tag=f"qbnev{h}")
            for jj in range(2):
                j = 2 * g + jj
                slot = j % 2
                load_w(wqb, 16 + j, 2048, slot, tag=f"qbrw{j}")
                pp = fm_mms(slot, NCH, LN, lambda c: f"LN{c}", j % 2,
                            f"qbrmm{j}")
                rope_fm(pp, pk(j % 2), QNt(4 + jj), "QROPE", 128, C2B, S2B,
                        f"qbro{j}")
            if stages < 2.9:
                continue
            for hh in range(4):
                h = 4 * g + hh
                head_loads(h, 1)
                attn_head(h, hh, 1,
                          qrope_src=(QNt(4 + hh // 2), 64 * (hh % 2)))
            if stages < 3:
                continue
            out_proj(wob, g, "ob")

    if stages >= 4:
        emit_ln(Xc, lambda c: f"X{c}", LN, lambda c: f"LN{c}", NCH, "ln4")
        ffn("f2")

    for c in range(NCH):
        ld("sync", y_out[128 * c:128 * (c + 1), :], Xc(c), f"Y{c}",
           reads=(f"X{c}",), tag=f"st{c}", dgroup="y")

    sc.finalize(nc, stack)
    for en in ENGINES:
        getattr(blk, en)(lambda eng, en=en: sc.emit_engine(eng, en))
    stack.close()
    return nc


def kernel(L0_attn_a_x, L0_attn_a_Wq, L0_attn_a_Wk, L0_attn_a_Wv, L0_attn_a_Wo,
           L0_attn_b_freqs_cis, L0_attn_b_Wq, L0_attn_b_Wkv_a,
           L0_attn_b_kv_gamma, L0_attn_b_kv_beta, L0_attn_b_Wkv_b,
           L0_attn_b_Wout, L0_ffn_base_w, L0_ffn_base_b, L0_ffn_router_w,
           L0_ffn_router_b, L0_ffn_lora_A, L0_ffn_lora_B, _stages=4,
           _taps=()):
    f32 = np.float32
    x = np.asarray(L0_attn_a_x, dtype=f32)

    def tile_w_og(W):
        A = W.reshape(16, 128, 16, 128)
        Bm = A.reshape(4, 4, 128, 16, 128).transpose(0, 3, 2, 1, 4)
        return np.ascontiguousarray(Bm.reshape(4, 16, 128, 512))

    wqa_h = tile_w(np.asarray(L0_attn_a_Wq, f32).T, 128)
    wka_h = tile_w(np.asarray(L0_attn_a_Wk, f32).T, 128)
    wva_h = tile_w(np.asarray(L0_attn_a_Wv, f32).T, 256)
    woa_h = tile_w_og(np.asarray(L0_attn_a_Wo, f32).T)
    basew_h = tile_w(np.asarray(L0_ffn_base_w, f32).T, 128)
    biasT_h = np.ascontiguousarray(np.asarray(L0_ffn_base_b, f32).reshape(16, 128).T)
    rb8_h = np.ascontiguousarray(
        np.broadcast_to(np.asarray(L0_ffn_router_b, f32), (128, 8)))
    routr_h = np.ascontiguousarray(
        np.asarray(L0_ffn_router_w, f32).T.reshape(16, 128, 8)
        .transpose(1, 0, 2).reshape(128, 128))
    aall_h = np.ascontiguousarray(
        np.asarray(L0_ffn_lora_A, f32).transpose(1, 0, 2).reshape(D, 128)
        .reshape(16, 128, 128).transpose(1, 0, 2).reshape(128, 2048))
    ball_h = np.ascontiguousarray(
        np.asarray(L0_ffn_lora_B, f32).reshape(128, D))
    wqb_r = np.asarray(L0_attn_b_Wq, f32)
    cols = []
    for h in range(16):
        cols.extend(range(192 * h, 192 * h + 128))
    for h in range(16):
        cols.extend(range(192 * h + 128, 192 * h + 192))
    wqb_h = tile_w(np.ascontiguousarray(wqb_r[:, cols]), 128)
    wkva_r = np.asarray(L0_attn_b_Wkv_a, f32)
    wkvac_h = tile_w(np.ascontiguousarray(wkva_r[:, 0:512]), 128)
    wkvar_h = tile_w(np.ascontiguousarray(wkva_r[:, 512:576]), 64)
    kvgb_h = np.zeros((128, 8), f32)
    kvgb_h[:, 0:4] = np.asarray(L0_attn_b_kv_gamma, f32).reshape(4, 128).T
    kvgb_h[:, 4:8] = np.asarray(L0_attn_b_kv_beta, f32).reshape(4, 128).T
    wkvb_r = np.asarray(L0_attn_b_Wkv_b, f32)
    kcols, vcols = [], []
    for h in range(16):
        kcols.extend(range(256 * h, 256 * h + 128))
        vcols.extend(range(256 * h + 128, 256 * h + 256))
    wkvbk_h = tile_w(np.ascontiguousarray(wkvb_r[:, kcols]), 128)
    wkvbv_h = tile_w(np.ascontiguousarray(wkvb_r[:, vcols]), 256)
    wob_h = tile_w_og(np.asarray(L0_attn_b_Wout, f32))
    fcis = np.asarray(L0_attn_b_freqs_cis, f32)

    nc = build_nc(_stages, taps=_taps)
    in_maps = []
    for core in range(8):
        b = core // 2
        par = core % 2
        pos = _positions(par)
        xc = np.ascontiguousarray(x[b][pos, :].T)
        c2a_h, s2a_h = _rope_tables_a(par)
        in_maps.append({
            "x_fm": xc, "wqa": wqa_h, "wka": wka_h, "wva": wva_h,
            "woa": woa_h, "basew": basew_h, "biasT": biasT_h, "rb8": rb8_h,
            "routr": routr_h, "aall": aall_h, "ball": ball_h, "wqb": wqb_h,
            "wkvac": wkvac_h, "wkvar": wkvar_h, "kvgb": kvgb_h,
            "wkvbk": wkvbk_h, "wkvbv": wkvbv_h, "wob": wob_h,
            "freqs": np.ascontiguousarray(fcis[pos, :]),
            "c2a": c2a_h, "s2a": s2a_h, "abias": _abias(par),
        })
    res = run_bass_kernel_spmd(nc, in_maps, core_ids=list(range(8)))
    y = np.empty((B, S, D), dtype=np.float32)
    for core in range(8):
        b = core // 2
        par = core % 2
        pos = _positions(par)
        y[b][pos, :] = res.results[core]["y_fm"].T
    if _taps:
        tapd = [{k[4:]: v for k, v in res.results[c].items()
                 if k.startswith("tap_")} for c in range(8)]
        return y, tapd
    return y


# version 14
# speedup vs baseline: 1.1307x; 1.1307x over previous
import numpy as np
import ml_dtypes
import concourse.bass as bass
import concourse.mybir as mybir
from concourse.bass_utils import run_bass_kernel_spmd

F32 = mybir.dt.float32
BF16 = mybir.dt.bfloat16
AF = mybir.ActivationFunctionType
ALU = mybir.AluOpType
AX = mybir.AxisListType
BF = ml_dtypes.bfloat16

B, S, D, H = 4, 2048, 2048, 16
T = 1024
NCH = 16
PERM = [0, 3, 1, 2]
OBK = ([0, 3], [1, 2])
SCALE_A = float(1.0 / np.sqrt(128.0))
SCALE_B = float(1.0 / np.sqrt(192.0))
MSUB = 20.0
NEG = -30000.0
EPS = 1e-5
THETA = 10000.0
PI = float(np.pi)
ITEMS_QB = ([0, 1, 2, 3], [0, 1, 2, 3, 8, 9, 10, 11, 12, 13, 14, 15])
ENGINES = ("tensor", "scalar", "vector", "gpsimd", "sync")


STRICT_SAME_ENGINE = True

SLOT_GROUPS = ("wb0", "wb1", "kh0", "kh1", "vh0", "vh1", "ko0", "ko1",
               "vo0", "vo1", "frq", "ka0", "ka1", "va0", "va1",
               "kb0", "kb1", "vb0", "vb1")


class Sched:
    def __init__(self):
        self.steps = []
        self.last_w = {}
        self.readers = {}

    def add(self, engine, emit, reads=(), writes=(), n_dma=0, coll=False,
            tag="", dgroup=None):
        deps = set()
        for b in reads:
            w = self.last_w.get(b)
            if w is not None:
                deps.add(w)
        for b in writes:
            w = self.last_w.get(b)
            if w is not None:
                deps.add(w)
            for r in self.readers.get(b, ()):
                deps.add(r)
        i = len(self.steps)
        if n_dma:
            assert dgroup is not None, f"dma step {tag} needs dgroup"
        self.steps.append(dict(engine=engine, emit=emit, deps=deps,
                               n_dma=n_dma, coll=coll, idx=i, tag=tag,
                               dgroup=dgroup, pub=False, sem=None, cnt=None,
                               inc=0))
        for b in reads:
            self.readers.setdefault(b, []).append(i)
        for b in writes:
            self.last_w[b] = i
            self.readers[b] = []
        return i

    def finalize(self, nc, stack):
        self.semc = {en: stack.enter_context(nc.semaphore(f"sc_{en}"))
                     for en in ("tensor", "scalar", "vector", "gpsimd")}
        self.semcc = stack.enter_context(nc.semaphore("scc"))
        groups = []
        for s in self.steps:
            if s["n_dma"] and s["dgroup"] not in groups:
                groups.append(s["dgroup"])
        self.semd = {g: stack.enter_context(nc.semaphore(f"sg_{g}"))
                     for g in groups}
        for s in self.steps:
            for d in s["deps"]:
                p = self.steps[d]
                if (p["n_dma"] or p["coll"] or p["engine"] != s["engine"]
                        or STRICT_SAME_ENGINE):
                    p["pub"] = True
        ctr = {}
        self.gtotal = {}
        self.glast = {}
        for s in self.steps:
            en = s["engine"]
            if s["coll"]:
                ctr["cc"] = ctr.get("cc", 0) + 1
                s["sem"], s["cnt"], s["inc"] = self.semcc, ctr["cc"], 1
            elif s["n_dma"]:
                g = s["dgroup"]
                ctr[g] = ctr.get(g, 0) + 16 * s["n_dma"]
                s["sem"], s["cnt"], s["inc"] = self.semd[g], ctr[g], 16
                self.gtotal[g] = ctr[g]
                self.glast[g] = s["idx"]
            elif s["pub"]:
                k = ("c", en)
                ctr[k] = ctr.get(k, 0) + 1
                s["sem"], s["cnt"], s["inc"] = self.semc[en], ctr[k], 1
        for s in self.steps:
            for d in s["deps"]:
                p = self.steps[d]
                if p["n_dma"] and p["dgroup"] not in SLOT_GROUPS:
                    assert s["idx"] > self.glast[p["dgroup"]] or \
                        s["n_dma"] and s["dgroup"] == p["dgroup"], (
                        f"batch dgroup {p['dgroup']} consumed at step "
                        f"{s['idx']}({s['tag']}) before group closed at "
                        f"{self.glast[p['dgroup']]}")
        self._fd = {}
        for s in self.steps:
            if s["n_dma"]:
                self._fd.setdefault(s["engine"], {})[s["dgroup"]] = \
                    self.gtotal[s["dgroup"]]

    def emit_engine(self, eng, en):
        hw = {}
        for s in self.steps:
            if s["engine"] != en:
                continue
            need = {}
            for d in sorted(s["deps"]):
                p = self.steps[d]
                if p["sem"] is None:
                    continue
                if (not STRICT_SAME_ENGINE and not p["n_dma"]
                        and not p["coll"] and p["engine"] == en):
                    continue
                if (p["n_dma"] and s["n_dma"]
                        and s["dgroup"] == p["dgroup"]):
                    continue
                cnt = p["cnt"]
                if p["n_dma"] and p["dgroup"] not in SLOT_GROUPS:
                    cnt = self.gtotal[p["dgroup"]]
                key = id(p["sem"])
                if need.get(key, (None, -1))[1] < cnt:
                    need[key] = (p["sem"], cnt)
            for sem, cnt in need.values():
                if hw.get(id(sem), 0) < cnt:
                    eng.wait_ge(sem, cnt)
                    hw[id(sem)] = cnt
            out = s["emit"](eng, s["sem"])
            if s["sem"] is not None and not s["n_dma"] and not s["coll"]:
                last = out[-1] if isinstance(out, (list, tuple)) else out
                last.then_inc(s["sem"], s["inc"])
            elif s["coll"]:
                out.then_inc(s["sem"], 1)
        for g, tot in self._fd.get(en, {}).items():
            eng.wait_ge(self.semd[g], tot)


def tile_w(W, ob):
    Din, O = W.shape
    return np.ascontiguousarray(
        W.reshape(Din // 128, 128, O // ob, ob).transpose(2, 1, 0, 3))


def _positions(parity):
    b0, b1 = OBK[parity]
    return np.concatenate([512 * b0 + np.arange(512), 512 * b1 + np.arange(512)])


def _rope_tables_a(parity):
    pos = _positions(parity).astype(np.float64)
    idx = np.arange(0, 128, 2, dtype=np.float64)
    inv = THETA ** (-idx / 128.0)
    ang = inv[:, None] * pos[None, :]
    c = np.repeat(np.cos(ang), 2, axis=0)
    s = np.repeat(np.sin(ang), 2, axis=0)
    s[0::2] *= -1.0
    return c.astype(np.float32), s.astype(np.float32)


def _abias(parity):
    ab = np.zeros((128, 16), dtype=np.float32)
    col = 0
    for qb in range(2):
        gqb = OBK[parity][qb]
        for kt in ITEMS_QB[qb]:
            kb = PERM[kt // 4]
            ab[:, col] = -MSUB if kb < gqb else NEG
            col += 1
    return ab


def build_nc(stages=4, taps=()):
    from contextlib import ExitStack
    nc = bass.Bass("TRN2", target_bir_lowering=False)
    sc = Sched()
    stack = ExitStack()
    e = stack.enter_context

    def din(name, shape):
        return nc.dram_tensor(name, list(shape), F32, kind="ExternalInput").ap()

    x_in = din("x_fm", (D, T))
    y_out = nc.dram_tensor("y_fm", [D, T], F32, kind="ExternalOutput").ap()
    wqa = din("wqa", (16, 128, 16, 128))
    wka = din("wka", (16, 128, 16, 128))
    wva = din("wva", (8, 128, 16, 256))
    woa = din("woa", (4, 16, 128, 512))
    basew = din("basew", (16, 128, 16, 128))
    biasT_i = din("biasT", (128, 16))
    rb8_i = din("rb8", (128, 8))
    routr = din("routr", (128, 128))
    aall = din("aall", (128, 2048))
    ball = din("ball", (128, 2048))
    wqb = din("wqb", (24, 128, 16, 128))
    wkvac = din("wkvac", (4, 128, 16, 128))
    wkvar = din("wkvar", (1, 128, 16, 64))
    kvgb_i = din("kvgb", (128, 8))
    wkvbk = din("wkvbk", (16, 128, 4, 128))
    wkvbv = din("wkvbv", (8, 128, 4, 256))
    wob = din("wob", (4, 16, 128, 512))
    freqs = din("freqs", (T, 32))
    c2a_i = din("c2a", (128, T))
    s2a_i = din("s2a", (128, T))
    abias_i = din("abias", (128, 16))

    ka_in = nc.dram_tensor("ka_in", [D, T], BF16, kind="Internal")
    ka_out = nc.dram_tensor("ka_out", [2 * D, T], BF16, kind="Internal")
    va_in = nc.dram_tensor("va_in", [T, D], BF16, kind="Internal")
    va_out = nc.dram_tensor("va_out", [2 * T, D], BF16, kind="Internal")
    kb_in = nc.dram_tensor("kb_in", [2112, T], BF16, kind="Internal")
    kb_out = nc.dram_tensor("kb_out", [4224, T], BF16, kind="Internal")
    vb_in = nc.dram_tensor("vb_in", [T, D], BF16, kind="Internal")
    vb_out = nc.dram_tensor("vb_out", [2 * T, D], BF16, kind="Internal")
    CCG = [[0, 1], [2, 3], [4, 5], [6, 7]]

    ident_f = nc.inline_tensor(np.eye(128, dtype=np.float32), name="idf").ap()
    psw_np = np.zeros((128, 128), dtype=BF)
    for i in range(128):
        psw_np[i, i ^ 1] = 1
    pswap_c = nc.inline_tensor(psw_np, name="psw").ap()
    tri_np = np.where(np.triu(np.ones((128, 128))) > 0, 0.0, NEG).astype(BF)
    tri_c = nc.inline_tensor(tri_np, name="tri").ap()
    idb_c = nc.inline_tensor(np.eye(128, dtype=BF), name="idb").ap()
    onc_c = nc.inline_tensor(np.ones((128, 1), dtype=BF), name="onc").ap()
    onb_c = nc.inline_tensor(np.ones((128, 128), dtype=BF), name="onb").ap()
    onrf_c = nc.inline_tensor(np.ones((1, 128), dtype=np.float32),
                              name="onrf").ap()
    e2_np = np.zeros((8, 128), dtype=np.float32)
    for ex in range(8):
        e2_np[ex, 16 * ex:16 * ex + 16] = 2.0
    e2_c = nc.inline_tensor(e2_np, name="e2").ap()
    ec_np = np.zeros((32, 128), dtype=np.float32)
    es_np = np.zeros((32, 128), dtype=np.float32)
    for r in range(128):
        fi = (r % 64) // 2
        ec_np[fi, r] = 1.0
        es_np[fi, r] = -1.0 if r % 2 == 0 else 1.0
    ec_c = nc.inline_tensor(ec_np, name="ec").ap()
    mb20_c = nc.inline_tensor(np.full((128, 1), -MSUB, dtype=np.float32),
                              name="mb20").ap()
    epsc_c = nc.inline_tensor(np.full((128, 1), EPS, dtype=np.float32),
                              name="epsc").ap()
    es_c = nc.inline_tensor(es_np, name="es").ap()

    XR = e(nc.sbuf_tensor("XR", [128, NCH * T], F32))
    LNR = e(nc.sbuf_tensor("LNR", [128, NCH * T], BF16))
    QN = e(nc.sbuf_tensor("QN", [128, 8 * T], BF16))
    QR = e(nc.sbuf_tensor("QR", [128, 4 * T], BF16))
    KH = e(nc.sbuf_tensor("KH", [128, 2 * 2048], BF16))
    VH = e(nc.sbuf_tensor("VH", [128, 2 * 2048], BF16))
    KOWN = e(nc.sbuf_tensor("KOWN", [128, 2 * T], BF16))
    VOWN = e(nc.sbuf_tensor("VOWN", [128, 2 * T], BF16))
    KRB = KOWN
    WBUF = e(nc.sbuf_tensor("WBUF", [128, 2 * 4096], BF16))
    TBL = e(nc.sbuf_tensor("TBL", [128, 2 * T], F32))
    SCR = e(nc.sbuf_tensor("SCR", [128, 2 * T], F32))
    ROWB = e(nc.sbuf_tensor("ROWB", [1, 2 * T], F32))
    PTS = e(nc.sbuf_tensor("PTS", [128, 2 * 512], BF16))
    STG = e(nc.sbuf_tensor("STG", [128, 2 * T], BF16))
    RINV = e(nc.sbuf_tensor("RINV", [128, 512], F32))
    RAs = e(nc.sbuf_tensor("RAs", [128, 2048], BF16))
    RBs = e(nc.sbuf_tensor("RBs", [128, 2048], BF16))
    RRs = e(nc.sbuf_tensor("RRs", [128, 128], BF16))
    RB8 = e(nc.sbuf_tensor("RB8", [128, 8], F32))
    KVGB = e(nc.sbuf_tensor("KVGB", [128, 8], F32))
    ABIAS = e(nc.sbuf_tensor("ABIAS", [128, 16], F32))
    IDF = e(nc.sbuf_tensor("IDF", [128, 128], F32))
    IDB = e(nc.sbuf_tensor("IDB", [128, 128], BF16))
    PSW = e(nc.sbuf_tensor("PSW", [128, 128], BF16))
    TRIt = e(nc.sbuf_tensor("TRIt", [128, 128], BF16))
    ONC = e(nc.sbuf_tensor("ONC", [128, 1], BF16))
    ONB = e(nc.sbuf_tensor("ONB", [128, 128], BF16))
    ONRF = e(nc.sbuf_tensor("ONRF", [1, 128], F32))
    MB20 = e(nc.sbuf_tensor("MB20", [128, 1], F32))
    EPSC = e(nc.sbuf_tensor("EPSC", [128, 1], F32))
    BIASC = e(nc.sbuf_tensor("BIASC", [128, 16], F32))
    E2 = e(nc.sbuf_tensor("E2", [8, 128], F32))
    EC = e(nc.sbuf_tensor("EC", [32, 128], F32))
    ES = e(nc.sbuf_tensor("ES", [32, 128], F32))
    PP = [e(nc.psum_tensor(f"PP{i}", [128, 1024], F32)) for i in range(4)]

    blk = e(nc.Block())
    add = sc.add

    def Xc(c):
        return XR[:, T * c:T * (c + 1)]

    def LN(c):
        return LNR[:, T * c:T * (c + 1)]

    def QNt(i):
        return QN[:, T * i:T * (i + 1)]

    def QRt(i):
        return QR[:, T * i:T * (i + 1)]

    def KVCB(i):
        return QNt(i)

    def KVCN(i):
        return QNt(4 + i)

    MEAN = ROWB[0:1, 0:T]
    AUX = ROWB[0:1, T:2 * T]
    DENR = ROWB[0:1, 0:512]
    FRQ = QN[:, 0:384].bitcast(F32)
    GT = SCR[0:8, 0:T]
    GDAT = QN[:, 384:512].bitcast(F32)
    GTMP = QN[:, 512:624].bitcast(F32)
    def SG(i, n=1):
        return SCR[:, 512 * i:512 * (i + n)]

    PKEY = [("P0A", "P0B"), ("P1A", "P1B"), ("P2A", "P2B"), ("P3A", "P3B")]

    def pk(i):
        return PKEY[i]

    def ld(en, dst, src, wbuf, reads=(), tag="ld", dgroup="init"):
        def f(g, sem, dst=dst, src=src):
            return g.dma_start(out=dst, in_=src).then_inc(sem, 16)
        return add(en, f, reads=reads, writes=(wbuf,) if isinstance(wbuf, str)
                   else tuple(wbuf), n_dma=1, tag=tag, dgroup=dgroup)

    for dst, src, b in ((IDF[:, :], ident_f, "IDF"), (IDB[:, :], idb_c, "IDB"),
                        (PSW[:, :], pswap_c, "PSW"), (TRIt[:, :], tri_c, "TRI"),
                        (ONC[:, :], onc_c, "ONC"), (ONRF[:, :], onrf_c, "ONRF"),
                        (ONB[:, :], onb_c, "ONB"),
                        (MB20[:, :], mb20_c, "MB20"),
                        (EPSC[:, :], epsc_c, "EPSC"),
                        (E2[:, :], e2_c, "E2"),
                        (EC[:, :], ec_c, "EC"), (ES[:, :], es_c, "ES"),
                        (ABIAS[:, :], abias_i, "ABIAS"),
                        (RB8[:, :], rb8_i, "RB8"), (KVGB[:, :], kvgb_i, "KVGB"),
                        (TBL[:, 0:T], c2a_i, "TBL"),
                        (TBL[:, T:2 * T], s2a_i, "TBLS")):
        ld("sync", dst, src, b)
    for c in range(NCH):
        ld("sync", Xc(c), x_in[128 * c:128 * (c + 1), :], f"X{c}", dgroup="x")
    ld("sync", BIASC[:, :], biasT_i, "BIASC", dgroup="init")
    ld("gpsimd", RAs[:, :], aall, "RA", dgroup="ginit")
    ld("gpsimd", RBs[:, :], ball, "RB_", dgroup="ginit")
    ld("gpsimd", RRs[:, :], routr, "RR", dgroup="ginit")

    tap_outs = {}

    def tap(name, ap, keys, cast=False):
        if name not in taps:
            return
        shp = list(ap.shape)
        o = nc.dram_tensor(f"tap_{name}", shp, F32, kind="ExternalOutput").ap()
        tap_outs[name] = True
        ld("gpsimd" if cast else "sync", o, ap, f"TAPO_{name}",
           reads=tuple(keys), tag=f"tap{name}", dgroup="tap")

    def load_w(wap, o, cols, slot, csel=None, tag="w"):
        if csel is None:
            src = wap[o].rearrange("p c j -> p (c j)")
        else:
            src = wap[o, :, csel[0]:csel[1], :].rearrange("p c j -> p (c j)")
        dst = WBUF[:, 4096 * slot:4096 * slot + cols]
        return ld("gpsimd", dst, src, f"WB{slot}", tag=tag,
                  dgroup=f"wb{slot}")

    def wt(slot, c, ob=128):
        return WBUF[:, 4096 * slot + ob * c:4096 * slot + ob * (c + 1)]

    def load_w_og(wap, g, o, slot, tag="w"):
        dst = WBUF[:, 4096 * slot:4096 * slot + 512]
        return ld("gpsimd", dst, wap[g, o], f"WB{slot}", tag=tag,
                  dgroup=f"wb{slot}")

    def emit_ln(src, skey, dst, dkey, nch, tag, gb=None):
        dsum = float(128 * nch)
        for c in range(nch):
            def f(s, sem, c=c):
                return s.activation(dst(c), src(c), AF.Copy)
            add("scalar", f, reads=(skey(c),), writes=(dkey(c),),
                tag=f"{tag}cast{c}")
        for c in range(nch):
            sqb = SG(c % 2, 1).bitcast(BF16)
            def fsq(s, sem, c=c, sqb=sqb):
                return s.activation(sqb, dst(c), AF.Square)
            add("scalar", fsq, reads=(dkey(c),),
                writes=(f"SCR_{'A' if c % 2 == 0 else 'B'}",),
                tag=f"{tag}sq{c}")
            for t in range(2):
                def fmu(te, sem, c=c, t=t):
                    return te.matmul(PP[2][:, 512 * t:512 * (t + 1)],
                                     ONB[:, :],
                                     dst(c)[:, 512 * t:512 * (t + 1)],
                                     start=(c == 0), stop=(c == nch - 1),
                                     skip_group_check=True)
                add("tensor", fmu, reads=(dkey(c), "ONB"),
                    writes=(pk(2)[t],), tag=f"{tag}mu{c}_{t}")
                def fs2(te, sem, c=c, t=t, sqb=sqb):
                    return te.matmul(PP[3][:, 512 * t:512 * (t + 1)],
                                     ONB[:, :], sqb[:, 512 * t:512 * (t + 1)],
                                     start=(c == 0), stop=(c == nch - 1),
                                     skip_group_check=True)
                add("tensor", fs2,
                    reads=(f"SCR_{'A' if c % 2 == 0 else 'B'}", "ONB"),
                    writes=(pk(3)[t],), tag=f"{tag}sm{c}_{t}")
        MB = SG(0, 2)
        TMPB = SG(2, 2)
        def v1(v, sem):
            return v.tensor_single_scalar(MB, PP[3][:, 0:T], 1.0 / dsum,
                                          ALU.mult)
        add("vector", v1, reads=("P3A", "P3B"), writes=("SCR_A", "SCR_B"))
        def v2(v, sem):
            return v.tensor_single_scalar(PP[3][:, 0:T], PP[2][:, 0:T],
                                          1.0 / dsum, ALU.mult)
        add("vector", v2, reads=("P2A", "P2B", "P3A", "P3B"),
            writes=("P3A", "P3B"))
        def s2(s, sem):
            return s.activation(TMPB, PP[3][:, 0:T], AF.Square)
        add("scalar", s2, reads=("P3A", "P3B"), writes=("SCR_C", "SCR_D"))
        def v3(v, sem):
            return v.tensor_sub(MB, MB, TMPB)
        add("vector", v3, reads=("SCR_A", "SCR_B", "SCR_C", "SCR_D"),
            writes=("SCR_A", "SCR_B"))
        def s3(s, sem):
            return s.activation(TMPB, MB, AF.Ln, bias=EPSC[:, 0:1])
        add("scalar", s3, reads=("SCR_A", "SCR_B", "EPSC"),
            writes=("SCR_C", "SCR_D"))
        def s4(s, sem):
            return s.activation(PP[2][:, 0:T], TMPB, AF.Exp, scale=-0.5)
        add("scalar", s4, reads=("SCR_C", "SCR_D", "P2A", "P2B"),
            writes=("P2A", "P2B"))
        for c in range(nch):
            def fa(v, sem, c=c):
                return v.tensor_sub(TMPB, src(c), PP[3][:, 0:T])
            add("vector", fa, reads=(skey(c), "P3A", "P3B"),
                writes=("SCR_C", "SCR_D"), tag=f"{tag}a1_{c}")
            if gb is None:
                def fo(v, sem, c=c):
                    return v.tensor_mul(dst(c), TMPB, PP[2][:, 0:T])
                add("vector", fo,
                    reads=("SCR_C", "SCR_D", "P2A", "P2B"),
                    writes=(dkey(c),), tag=f"{tag}a2_{c}")
            else:
                pp = PP[c % 2]
                pkk = pk(c % 2)
                def fo(v, sem, c=c, pp=pp):
                    return v.tensor_mul(pp[:, 0:T], TMPB, PP[2][:, 0:T])
                add("vector", fo,
                    reads=("SCR_C", "SCR_D", "P2A", "P2B"),
                    writes=(pkk[0], pkk[1]), tag=f"{tag}a2_{c}")
                def fg(v, sem, c=c, pp=pp):
                    return v.tensor_scalar(dst(c), pp[:, 0:T],
                                           KVGB[:, c:c + 1],
                                           KVGB[:, 4 + c:5 + c],
                                           ALU.mult, ALU.add)
                add("vector", fg, reads=(pkk[0], pkk[1], "KVGB"),
                    writes=(dkey(c),), tag=f"{tag}a3_{c}")

    def rope_fm(pp, ppkeys, dst, dkey, rows, c2, s2, tag):
        qfv = [SG(2, 1).bitcast(BF16)[0:128, 512 * t:512 * (t + 1)]
               for t in range(2)]
        for t in range(2):
            def f1(v, sem, t=t):
                return v.tensor_copy(qfv[t][0:rows, :],
                                     pp[0:rows, 512 * t:512 * (t + 1)])
            add("vector", f1, reads=ppkeys, writes=(f"SCR_C",),
                tag=f"{tag}qf{t}")
            def f2(te, sem, t=t):
                return te.matmul(PP[2][0:rows, 512 * t:512 * (t + 1)],
                                 PSW[0:rows, 0:rows], qfv[t][0:rows, :],
                                 start=True, stop=True)
            add("tensor", f2, reads=("SCR_C", "PSW"), writes=(pk(2)[t],),
                tag=f"{tag}sw{t}")
        for t in range(2):
            t1 = SG(3, 1)
            def f3(v, sem, t=t, t1=t1):
                return v.tensor_mul(t1[0:rows, :],
                                    pp[0:rows, 512 * t:512 * (t + 1)],
                                    c2[0:rows, 512 * t:512 * (t + 1)])
            add("vector", f3, reads=ppkeys + ("TBL",), writes=("SCR_D",),
                tag=f"{tag}t1_{t}")
            t2 = SG(1, 1)
            def f4(v, sem, t=t, t2=t2):
                return v.tensor_mul(t2[0:rows, :],
                                    PP[2][0:rows, 512 * t:512 * (t + 1)],
                                    s2[0:rows, 512 * t:512 * (t + 1)])
            add("vector", f4, reads=(pk(2)[t], "TBL"), writes=("SCR_B",),
                tag=f"{tag}t2_{t}")
            def f5(v, sem, t=t, t1=t1, t2=t2):
                return v.tensor_add(dst[0:rows, 512 * t:512 * (t + 1)],
                                    t1[0:rows, :], t2[0:rows, :])
            add("vector", f5, reads=("SCR_D", "SCR_B"), writes=(dkey,),
                tag=f"{tag}o{t}")

    def fm_mms(wslot, nch_in, srcf, skeyf, ppi, tag, start_new=True):
        pp = PP[ppi]
        def fm(te, sem):
            r = None
            for c in range(nch_in):
                for t in range(2):
                    r = te.matmul(pp[:, 512 * t:512 * (t + 1)], wt(wslot, c),
                                  srcf(c)[:, 512 * t:512 * (t + 1)],
                                  start=(start_new and c == 0), stop=False,
                                  skip_group_check=True)
            return r
        rd = tuple([f"WB{wslot}"] + [skeyf(c) for c in range(nch_in)])
        add("tensor", fm, reads=rd, writes=pk(ppi), tag=tag)
        return pp

    emit_ln(Xc, lambda c: f"X{c}", LN, lambda c: f"LN{c}", NCH, "ln1")

    for c in range(NCH):
        tap(f"ln1_{c}", LN(c), (f"LN{c}",), cast=True)

    C2A = TBL[:, 0:T]
    S2A = TBL[:, T:2 * T]

    def attn_proj_kq(wap, n_o, dst_of, dkey_of, stage_dma, tag):
        for h in range(n_o):
            slot = h % 2
            load_w(wap, h, 2048, slot, tag=f"{tag}w{h}")
            pp = fm_mms(slot, NCH, LN, lambda c: f"LN{c}", h % 2,
                        f"{tag}mm{h}")
            rope_fm(pp, pk(h % 2), dst_of(h), dkey_of(h), 128, C2A, S2A,
                    f"{tag}ro{h}")
            if stage_dma is not None:
                stage_dma(h)

    import os as _os
    _chop0 = int(_os.environ.get("KCHOP", "0"))
    if stages >= 1 and _chop0 >= 3:
        C2A_ = TBL[:, 0:T]
        S2A_ = TBL[:, T:2 * T]
        for hh in range(4):
            h = hh
            slot = h % 2
            load_w(wqa, h, 2048, slot, tag=f"qaw{h}")
            pp = fm_mms(slot, NCH, LN, lambda c: f"LN{c}", h % 2, f"qamm{h}")
            rope_fm(pp, pk(h % 2), QNt(hh), f"Q{hh}", 128, C2A_, S2A_,
                    f"qaro{h}")
            tap(f"q0_{hh}", QNt(hh), (f"Q{hh}",), cast=True)
    if stages >= 1 and _chop0 < 3 and _chop0 != 7:
        def kst_dma(h):
            slot = h % 2
            ld("sync", ka_in.ap()[128 * h:128 * (h + 1), :],
               STG[:, T * slot:T * (slot + 1)], f"DRAM_ka{h}",
               reads=(f"STG{slot}",), tag=f"kadma{h}", dgroup=f"ka{h % 2}")
        attn_proj_kq(wka, H, lambda h: STG[:, T * (h % 2):T * (h % 2 + 1)],
                     lambda h: f"STG{h % 2}", kst_dma, "ka")

        pass
    if stages >= 1 and _chop0 < 3 and _chop0 != 6:
        def cc_ka(g, sem):
            return g.collective_compute(
                "AllGather", ALU.bypass, replica_groups=CCG,
                ins=[ka_in.ap().opt()], outs=[ka_out.ap().opt()])
        add("gpsimd", cc_ka,
            reads=tuple([f"DRAM_ka{h}" for h in range(H)]),
            writes=("DRAM_kaout",), coll=True, tag="cc_ka")
        for o in range(8):
            slot = o % 2
            load_w(wva, o, 4096, slot, tag=f"vaw{o}")
            for tt in range(8):
                ppi = tt % 2
                pp = PP[ppi]
                def fv(te, sem, slot=slot, tt=tt, pp=pp):
                    r = None
                    for c in range(NCH):
                        r = te.matmul(pp[:, 0:256],
                                      LN(c)[:, 128 * tt:128 * (tt + 1)],
                                      wt(slot, c, 256),
                                      start=(c == 0), stop=(c == NCH - 1),
                                      skip_group_check=True)
                    return r
                add("tensor", fv,
                    reads=tuple([f"WB{slot}"] + [f"LN{c}" for c in range(NCH)]),
                    writes=pk(ppi), tag=f"vamm{o}_{tt}")
                vs = STG[:, T * (tt % 2) + 256:T * (tt % 2) + 512]
                def fe(v, sem, pp=pp, vs=vs):
                    return v.tensor_copy(vs[:, 0:256], pp[:, 0:256])
                add("vector", fe, reads=pk(ppi), writes=(f"STG{tt % 2}",),
                    tag=f"vaev{o}_{tt}")
                ld("sync", va_in.ap()[128 * tt:128 * (tt + 1),
                                      256 * o:256 * (o + 1)],
                   vs[:, 0:256], f"DRAM_va{o}_{tt}", reads=(f"STG{tt % 2}",),
                   tag=f"vadma{o}_{tt}", dgroup=f"va{tt % 2}")

        pass
    if stages >= 1 and _chop0 < 3 and _chop0 not in (6, 7):
        def cc_va(g, sem):
            return g.collective_compute(
                "AllGather", ALU.bypass, replica_groups=CCG,
                ins=[va_in.ap().opt()], outs=[va_out.ap().opt()])
        add("gpsimd", cc_va,
            reads=tuple([f"DRAM_va{o}_{t}" for o in range(8)
                         for t in range(8)]),
            writes=("DRAM_vaout",), coll=True, tag="cc_va")

        pass
    if stages >= 1 and _chop0 < 3:
        tap("kain", ka_in.ap(),
            tuple(f"DRAM_ka{h}" for h in range(H)), cast=True)
        tap("kaout", ka_out.ap(), ("DRAM_kaout",), cast=True)
        tap("vain", va_in.ap(),
            tuple(f"DRAM_va{o}_{t}" for o in range(8) for t in range(8)),
            cast=True)

        def head_loads(h, layer):
            slot = h % 2
            kout = ka_out if layer == 0 else kb_out
            vout = va_out if layer == 0 else vb_out
            kok = "DRAM_kaout" if layer == 0 else "DRAM_kbout"
            vok = "DRAM_vaout" if layer == 0 else "DRAM_vbout"
            rowB = 2048 if layer == 0 else 2112
            ld("sync", KH[:, 2048 * slot:2048 * slot + 1024],
               kout.ap()[128 * h:128 * (h + 1), :], f"KH{slot}",
               reads=(kok,), tag=f"khA{h}", dgroup=f"kh{slot}")
            ld("sync", KH[:, 2048 * slot + 1024:2048 * slot + 2048],
               kout.ap()[rowB + 128 * h:rowB + 128 * (h + 1), :], f"KH{slot}",
               reads=(kok,), tag=f"khB{h}", dgroup=f"kh{slot}")
            for half in range(4):
                src = vout.ap()[:, 128 * h:128 * (h + 1)].rearrange(
                    "(k p) d -> p k d", p=128)[:, 4 * half:4 * (half + 1), :]
                dst = VH[:, 2048 * slot + 512 * half:
                         2048 * slot + 512 * (half + 1)].rearrange(
                    "p (k d) -> p k d", d=128)
                ld("sync", dst, src, f"VH{slot}", reads=(vok,),
                   tag=f"vh{h}_{half}", dgroup=f"vh{slot}")

        def head_loads_own(h):
            slot = h % 2
            ld("sync", KOWN[:, T * slot:T * (slot + 1)],
               ka_in.ap()[128 * h:128 * (h + 1), :], f"KOWN{slot}",
               reads=(f"DRAM_ka{h}",), tag=f"ko{h}", dgroup=f"ko{slot}")
            src = va_in.ap()[:, 128 * h:128 * (h + 1)].rearrange(
                "(k p) d -> p k d", p=128)
            ld("sync", VOWN[:, T * slot:T * (slot + 1)].rearrange(
                "p (k d) -> p k d", d=128), src, f"VOWN{slot}",
               reads=tuple([f"DRAM_va{h // 2}_{tt}" for tt in range(8)]),
               tag=f"vo{h}", dgroup=f"vo{slot}")

        def attn_head(h, hh, layer, qrope_src=None):
            slot = h % 2
            scale = SCALE_A if layer == 0 else SCALE_B
            qtile = QNt(hh)
            if layer == 1:
                emitters = []
                for it_i in range(16):
                    ps = it_i % 2
                    psl = PP[0] if ps == 0 else PP[3]
                    pskey = ("P0A", "P0B") if ps == 0 else ("P3A", "P3B")
                    ptv = STG[:, 1024 * ps:1024 * (ps + 1)]
                    ptkey = f"STG{ps}"
                    first = it_i == 0
                    def es(it_i=it_i, psl=psl, pskey=pskey):
                        def fs(te, sem, idx=it_i, psl=psl):
                            r = None
                            for qb in range(2):
                                te.matmul(
                                    psl[:, 512 * qb:512 * (qb + 1)],
                                    KH[:, 2048 * slot + 128 * idx:
                                       2048 * slot + 128 * (idx + 1)],
                                    qtile[:, 512 * qb:512 * (qb + 1)],
                                    start=True, stop=False,
                                    skip_group_check=True)
                            qrt, rbase = qrope_src
                            for qb in range(2):
                                r = te.matmul(
                                    psl[:, 512 * qb:512 * (qb + 1)],
                                    KRB[rbase:rbase + 64,
                                        128 * idx:128 * (idx + 1)],
                                    qrt[rbase:rbase + 64,
                                        512 * qb:512 * (qb + 1)],
                                    start=False, stop=True,
                                    skip_group_check=True)
                            return r
                        add("tensor", fs,
                            reads=(f"KH{slot}", f"Q{hh}", "KRB", "QROPE"),
                            writes=pskey, tag=f"S{h}_m{it_i}")
                    def ee(it_i=it_i, psl=psl, ptv=ptv, pskey=pskey,
                           ptkey=ptkey):
                        def fe_(s, sem, psl=psl, ptv=ptv):
                            return s.activation(ptv[:, :], psl[:, 0:1024],
                                                AF.Exp, bias=MB20[:, 0:1],
                                                scale=scale)
                        add("scalar", fe_, reads=pskey + ("MB20",),
                            writes=(ptkey,), tag=f"E{h}_m{it_i}")
                    def ep(it_i=it_i, ptv=ptv, ptkey=ptkey, first=first):
                        vsrc = VH[:, 2048 * slot + 128 * it_i:
                                  2048 * slot + 128 * (it_i + 1)]
                        def fpv(te, sem, vsrc=vsrc, ptv=ptv, first=first):
                            r = None
                            for qb in range(2):
                                te.matmul(
                                    PP[1][:, 512 * qb:512 * (qb + 1)],
                                    vsrc, ptv[:, 512 * qb:512 * (qb + 1)],
                                    start=first, stop=False,
                                    skip_group_check=True)
                                r = te.matmul(
                                    PP[2][:, 512 * qb:512 * (qb + 1)],
                                    ONB[:, :],
                                    ptv[:, 512 * qb:512 * (qb + 1)],
                                    start=first, stop=False,
                                    skip_group_check=True)
                            return r
                        add("tensor", fpv, reads=(ptkey, f"VH{slot}", "ONB"),
                            writes=(pk(1)[0], pk(1)[1], pk(2)[0], pk(2)[1]),
                            tag=f"PV{h}_m{it_i}")
                    emitters.append((es, ee, ep))
                emitters[0][0]()
                for i in range(1, 16):
                    emitters[i - 1][1]()
                    emitters[i][0]()
                    emitters[i - 1][2]()
                emitters[-1][1]()
                emitters[-1][2]()
                RINV2 = SG(0, 2)
                def frc(v, sem):
                    return v.reciprocal(RINV2, PP[2][:, 0:1024])
                add("vector", frc, reads=("P2A", "P2B"),
                    writes=("SCR_A", "SCR_B"), tag=f"rc{h}")
                att = QRt(hh)
                def fat(v, sem, att=att):
                    return v.tensor_mul(att[:, 0:1024], PP[1][:, 0:1024],
                                        RINV2)
                add("vector", fat, reads=("P1A", "P1B", "SCR_A", "SCR_B"),
                    writes=(f"QR{hh}",), tag=f"at{h}")
                return
            for qb in range(2):
                items = []
                if layer == 0:
                    for tw in range(4):
                        items.append(("d", tw, None))
                    for p_, kt in enumerate(ITEMS_QB[qb]):
                        items.append(("g", kt,
                                      (p_ if qb == 0 else 4 + p_)))
                else:
                    for kt in range(16):
                        items.append(("g", kt, None))
                emitters = []
                for it_i, (kind, idx, bcol) in enumerate(items):
                    ps_i = it_i % 2
                    pskey = pk(0)[ps_i]
                    psl = PP[0][:, 512 * ps_i:512 * (ps_i + 1)]
                    ptv = PTS[:, 512 * ps_i:512 * (ps_i + 1)]
                    ptkey = f"PT{ps_i}"
                    first = it_i == 0
                    if kind == "g":
                        qoff = 0
                        def es(it_i=it_i, idx=idx, psl=psl, pskey=pskey,
                               slot=slot, qb=qb, qtile=qtile):
                            def fs(te, sem, idx=idx, psl=psl, slot=slot,
                                   qb=qb, qtile=qtile):
                                r = te.matmul(
                                    psl[:, :],
                                    KH[:, 2048 * slot + 128 * idx:
                                       2048 * slot + 128 * (idx + 1)],
                                    qtile[:, 512 * qb:512 * (qb + 1)],
                                    start=True, stop=(layer == 0),
                                    skip_group_check=True)
                                if layer == 1:
                                    qrt, rbase = qrope_src
                                    r = te.matmul(
                                        psl[:, :],
                                        KRB[rbase:rbase + 64,
                                            128 * idx:128 * (idx + 1)],
                                        qrt[rbase:rbase + 64,
                                            512 * qb:512 * (qb + 1)],
                                        start=False, stop=True,
                                        skip_group_check=True)
                                return r
                            add("tensor", fs,
                                reads=(f"KH{slot}", f"Q{hh}") if layer == 0
                                else (f"KH{slot}", f"Q{hh}", "KRB", "QROPE"),
                                writes=(pskey,), tag=f"S{h}_{qb}_{it_i}")
                        if layer == 0:
                            def ee(it_i=it_i, psl=psl, ptv=ptv, bcol=bcol,
                                   pskey=pskey, ptkey=ptkey):
                                def fe_(s, sem, psl=psl, ptv=ptv, bcol=bcol):
                                    return s.activation(
                                        ptv[:, :], psl[:, :], AF.Exp,
                                        bias=ABIAS[:, bcol:bcol + 1],
                                        scale=scale)
                                add("scalar", fe_, reads=(pskey, "ABIAS"),
                                    writes=(ptkey,), tag=f"E{h}_{qb}_{it_i}")
                        else:
                            def ee(it_i=it_i, psl=psl, ptv=ptv,
                                   pskey=pskey, ptkey=ptkey):
                                def fe_(s, sem, psl=psl, ptv=ptv):
                                    return s.activation(ptv[:, :], psl[:, :],
                                                        AF.Exp,
                                                        bias=MB20[:, 0:1],
                                                        scale=scale)
                                add("scalar", fe_, reads=(pskey, "MB20"),
                                    writes=(ptkey,), tag=f"E{h}_{qb}_{it_i}")
                        vsrc = VH[:, 2048 * slot + 128 * idx:
                                  2048 * slot + 128 * (idx + 1)]
                        vkey = f"VH{slot}"
                    else:
                        tw = idx
                        qoff = 128 * tw
                        def es(it_i=it_i, tw=tw, psl=psl, pskey=pskey,
                               slot=slot, qb=qb, qtile=qtile, qoff=qoff):
                            def fs(te, sem, tw=tw, psl=psl, slot=slot, qb=qb,
                                   qtile=qtile, qoff=qoff):
                                te.matmul(
                                    psl[:, qoff:512],
                                    KOWN[:, T * slot + 512 * qb + qoff:
                                         T * slot + 512 * qb + qoff + 128],
                                    qtile[:, 512 * qb + qoff:512 * (qb + 1)],
                                    start=True, stop=False,
                                    skip_group_check=True)
                                return te.matmul(
                                    psl[:, qoff:qoff + 128], IDB[:, :],
                                    TRIt[:, :], start=False, stop=True,
                                    skip_group_check=True)
                            add("tensor", fs,
                                reads=(f"KOWN{slot}", f"Q{hh}", "IDB", "TRI"),
                                writes=(pskey,), tag=f"Sd{h}_{qb}_{tw}")
                        def ee(it_i=it_i, psl=psl, ptv=ptv, qoff=qoff,
                               pskey=pskey, ptkey=ptkey, tw=tw):
                            def fe_(s, sem, psl=psl, ptv=ptv, qoff=qoff):
                                return s.activation(ptv[:, qoff:512],
                                                    psl[:, qoff:512], AF.Exp,
                                                    bias=MB20[:, 0:1],
                                                    scale=scale)
                            add("scalar", fe_, reads=(pskey, "MB20"),
                                writes=(ptkey,), tag=f"Ed{h}_{qb}_{tw}")
                        vsrc = VOWN[:, T * slot + 512 * qb + qoff:
                                    T * slot + 512 * qb + qoff + 128]
                        vkey = f"VOWN{slot}"
                    pvkey = pk(1)[qb]
                    denkey = pk(2)[qb]
                    def ep(it_i=it_i, vsrc=vsrc, ptv=ptv, qoff=qoff, qb=qb,
                           first=first, ptkey=ptkey, vkey=vkey,
                           pvkey=pvkey, denkey=denkey):
                        def fpv(te, sem, vsrc=vsrc, ptv=ptv, qoff=qoff,
                                qb=qb, first=first):
                            te.matmul(
                                PP[1][:, 512 * qb + qoff:512 * (qb + 1)],
                                vsrc, ptv[:, qoff:512], start=first,
                                stop=False, skip_group_check=True)
                            return te.matmul(
                                PP[2][:, 512 * qb + qoff:512 * (qb + 1)],
                                ONB[:, :], ptv[:, qoff:512], start=first,
                                stop=False, skip_group_check=True)
                        add("tensor", fpv, reads=(ptkey, vkey, "ONB"),
                            writes=(pvkey, denkey), tag=f"PV{h}_{qb}_{it_i}")
                    emitters.append((es, ee, ep))
                emitters[0][0]()
                for i in range(1, len(emitters)):
                    emitters[i - 1][1]()
                    emitters[i][0]()
                    emitters[i - 1][2]()
                emitters[-1][1]()
                emitters[-1][2]()
                def frc(v, sem, qb=qb):
                    return v.reciprocal(RINV[:, :],
                                        PP[2][:, 512 * qb:512 * (qb + 1)])
                add("vector", frc, reads=(pk(2)[qb],), writes=("RINV",),
                    tag=f"rc{h}_{qb}")
                att = QRt(hh)
                def fat(v, sem, qb=qb, att=att):
                    return v.tensor_mul(att[:, 512 * qb:512 * (qb + 1)],
                                        PP[1][:, 512 * qb:512 * (qb + 1)],
                                        RINV[:, :])
                add("vector", fat, reads=(pk(1)[qb], "RINV"),
                    writes=(f"QR{hh}",), tag=f"at{h}_{qb}")

        def out_proj(wap, g, tag):
            for o in range(16):
                slot = o % 2
                load_w_og(wap, g, o, slot, tag=f"{tag}w{g}_{o}")
                ppi = o % 2
                pp = PP[ppi]
                def fm(te, sem, slot=slot, pp=pp):
                    r = None
                    for cc in range(4):
                        for t in range(2):
                            r = te.matmul(pp[:, 512 * t:512 * (t + 1)],
                                          wt(slot, cc),
                                          QRt(cc)[:, 512 * t:512 * (t + 1)],
                                          start=(cc == 0), stop=(cc == 3),
                                          skip_group_check=True)
                    return r
                add("tensor", fm,
                    reads=tuple([f"WB{slot}"] + [f"QR{cc}" for cc in range(4)]),
                    writes=pk(ppi), tag=f"{tag}mm{g}_{o}")
                def fr(v, sem, o=o, pp=pp):
                    return v.tensor_add(Xc(o), pp[:, 0:T], Xc(o))
                add("vector", fr, reads=pk(ppi) + (f"X{o}",),
                    writes=(f"X{o}",), tag=f"{tag}res{g}_{o}")

        import os as _os
        _chop = int(_os.environ.get("KCHOP", "0"))
        for g in range(4):
            if _chop >= 2 and g > 0:
                break
            for hh in range(4):
                h = 4 * g + hh
                slot = h % 2
                load_w(wqa, h, 2048, slot, tag=f"qaw{h}")
                pp = fm_mms(slot, NCH, LN, lambda c: f"LN{c}", h % 2,
                            f"qamm{h}")
                rope_fm(pp, pk(h % 2), QNt(hh), f"Q{hh}", 128, C2A, S2A,
                        f"qaro{h}")
            if g == 0:
                for hh in range(4):
                    tap(f"q0_{hh}", QNt(hh), (f"Q{hh}",), cast=True)
            if _chop >= 1:
                continue
            for hh in range(4):
                h = 4 * g + hh
                head_loads(h, 0)
                head_loads_own(h)
                attn_head(h, hh, 0)
            if g == 0:
                for hh in range(4):
                    tap(f"att0_{hh}", QRt(hh), (f"QR{hh}",), cast=True)
            out_proj(woa, g, "oa")

    def ffn(tag):
        for tt in range(8):
            ppi = tt % 2
            def fl(te, sem, tt=tt, ppi=ppi):
                r = None
                for c in range(NCH):
                    r = te.matmul(PP[ppi][:, 0:8],
                                  LN(c)[:, 128 * tt:128 * (tt + 1)],
                                  RRs[:, 8 * c:8 * (c + 1)],
                                  start=(c == 0), stop=(c == NCH - 1),
                                  skip_group_check=True)
                return r
            add("tensor", fl,
                reads=tuple([f"LN{c}" for c in range(NCH)] + ["RR"]),
                writes=pk(ppi), tag=f"{tag}lg{tt}")
            sl = 28 * (tt % 2)
            lg = GTMP[:, sl:sl + 8]
            ex = GTMP[:, sl + 8:sl + 16]
            msk = GTMP[:, sl + 16:sl + 24]
            sel = GTMP[:, sl + 16:sl + 24]
            m1 = GTMP[:, sl + 24:sl + 25]
            m2 = GTMP[:, sl + 25:sl + 26]
            nm1 = GTMP[:, sl + 26:sl + 27]
            s2c = GTMP[:, sl + 27:sl + 28]
            gsl = f"GT{tt % 2}"
            def g1(v, sem, tt=tt, ppi=ppi, lg=lg):
                return v.tensor_add(lg, PP[ppi][:, 0:8], RB8[:, :])
            add("vector", g1, reads=pk(ppi) + ("RB8",), writes=(gsl,))
            def g2(v, sem, lg=lg, m1=m1):
                return v.reduce_max(m1, lg, axis=AX.X)
            add("vector", g2, reads=(gsl,), writes=(gsl,))
            def g3(v, sem, m1=m1, nm1=nm1):
                return v.tensor_single_scalar(nm1, m1, -1.0, ALU.mult)
            add("vector", g3, reads=(gsl,), writes=(gsl,))
            def g4(v, sem, lg=lg, m1=m1, msk=msk):
                return v.tensor_single_scalar(msk, lg, m1[:, 0:1],
                                              ALU.is_equal)
            add("vector", g4, reads=(gsl,), writes=("GMSK",))
            def g5(v, sem, lg=lg, msk=msk):
                return v.scalar_tensor_tensor(msk, msk, NEG, lg, ALU.mult,
                                              ALU.add)
            add("vector", g5, reads=("GMSK", gsl), writes=("GMSK",))
            def g6(v, sem, msk=msk, m2=m2):
                return v.reduce_max(m2, msk, axis=AX.X)
            add("vector", g6, reads=("GMSK",), writes=(gsl,))
            def g7(s, sem, lg=lg, nm1=nm1, ex=ex):
                return s.activation(ex, lg, AF.Exp, bias=nm1[:, 0:1])
            add("scalar", g7, reads=(gsl,), writes=("GEX",))
            def g8(v, sem, lg=lg, m2=m2, sel=sel):
                return v.tensor_single_scalar(sel, lg, m2[:, 0:1], ALU.is_ge)
            add("vector", g8, reads=(gsl,), writes=("GSEL",))
            def g9(v, sem, ex=ex, sel=sel):
                return v.tensor_mul(ex, ex, sel)
            add("vector", g9, reads=("GEX", "GSEL"), writes=("GEX",))
            def g10(v, sem, ex=ex, s2c=s2c):
                return v.reduce_sum(s2c, ex, axis=AX.X)
            add("vector", g10, reads=("GEX",), writes=(gsl,))
            def g11(v, sem, s2c=s2c):
                return v.reciprocal(s2c, s2c)
            add("vector", g11, reads=(gsl,), writes=(gsl,))
            def g12(v, sem, ex=ex, s2c=s2c, tt=tt):
                return v.tensor_scalar_mul(GDAT[:, 8 * tt:8 * (tt + 1)], ex,
                                           s2c[:, 0:1])
            add("vector", g12, reads=("GEX", gsl), writes=(f"GD{tt}", "QN0"))
            def g13(te, sem, tt=tt):
                return te.transpose(PP[3][0:8, 0:128],
                                    GDAT[:, 8 * tt:8 * (tt + 1)], IDF[:, :])
            add("tensor", g13, reads=(f"GD{tt}", "QN0", "IDF"), writes=("P3A",),
                tag=f"{tag}gt{tt}")
            def g14(v, sem, tt=tt):
                return v.tensor_copy(GT[0:8, 128 * tt:128 * (tt + 1)],
                                     PP[3][0:8, 0:128])
            add("vector", g14, reads=("P3A",), writes=("SCR_A", "SCR_B"),
                tag=f"{tag}gte{tt}")
        def fmid(te, sem):
            r = None
            for c in range(NCH):
                for t in range(2):
                    r = te.matmul(PP[0][:, 512 * t:512 * (t + 1)],
                                  RAs[:, 128 * c:128 * (c + 1)],
                                  LN(c)[:, 512 * t:512 * (t + 1)],
                                  start=(c == 0), stop=(c == NCH - 1),
                                  skip_group_check=True)
            return r
        add("tensor", fmid,
            reads=tuple([f"LN{c}" for c in range(NCH)] + ["RA"]),
            writes=("P0A", "P0B"), tag=f"{tag}mid")
        for t in range(2):
            def fgm(te, sem, t=t):
                return te.matmul(PP[1][:, 512 * t:512 * (t + 1)], E2[:, :],
                                 GT[0:8, 512 * t:512 * (t + 1)], start=True,
                                 stop=True)
            add("tensor", fgm, reads=("SCR_A", "SCR_B", "E2"),
                writes=(pk(1)[t],), tag=f"{tag}gmm{t}")
        GS = SG(0, 2)
        def fgs(v, sem):
            return v.tensor_copy(GS, PP[1][:, 0:T])
        add("vector", fgs, reads=("P1A", "P1B"), writes=("SCR_A", "SCR_B"),
            tag=f"{tag}gs")
        MIDG = SG(3, 1).bitcast(BF16)
        def fmg(v, sem):
            return v.tensor_mul(MIDG, PP[0][:, 0:T], GS)
        add("vector", fmg, reads=("P0A", "P0B", "SCR_A", "SCR_B"),
            writes=("SCR_D",), tag=f"{tag}midg")
        for o in range(16):
            slot = o % 2
            load_w(basew, o, 2048, slot, tag=f"{tag}bw{o}")
            ppi = o % 2
            pp = PP[ppi]
            def fb(te, sem, slot=slot, pp=pp, o=o):
                for c in range(NCH):
                    for t in range(2):
                        te.matmul(pp[:, 512 * t:512 * (t + 1)], wt(slot, c),
                                  LN(c)[:, 512 * t:512 * (t + 1)],
                                  start=(c == 0), stop=False,
                                  skip_group_check=True)
                r = None
                for t in range(2):
                    r = te.matmul(pp[:, 512 * t:512 * (t + 1)],
                                  RBs[:, 128 * o:128 * (o + 1)],
                                  MIDG[:, 512 * t:512 * (t + 1)],
                                  start=False, stop=True,
                                  skip_group_check=True)
                return r
            add("tensor", fb,
                reads=tuple([f"WB{slot}", "RB_", "SCR_D"] +
                            [f"LN{c}" for c in range(NCH)]),
                writes=pk(ppi), tag=f"{tag}base{o}")
            def fr(v, sem, o=o, pp=pp):
                return v.scalar_tensor_tensor(Xc(o), pp[:, 0:T],
                                              BIASC[:, o:o + 1], Xc(o),
                                              ALU.add, ALU.add)
            add("vector", fr, reads=pk(ppi) + (f"X{o}", "BIASC"),
                writes=(f"X{o}",), tag=f"{tag}res{o}")

    if stages >= 2:
        emit_ln(Xc, lambda c: f"X{c}", LN, lambda c: f"LN{c}", NCH, "ln2")
        ffn("f1")

    if stages >= 2.2:
        CTT = SCR[0:32, 0:T]
        STT = SCR[0:32, T:2 * T]
        for tt in range(8):
            th = FRQ[:, 0:32]
            ld("sync", th, freqs[128 * tt:128 * (tt + 1), :], "QN0",
               tag=f"fq{tt}", dgroup="frq")
            a_ = FRQ[:, 32:64]
            b_ = FRQ[:, 64:96]
            m_ = FRQ[:, 96:128]
            ct = FRQ[:, 128:160]
            st = FRQ[:, 160:192]
            def w1(v, sem, th=th, m_=m_):
                return v.tensor_single_scalar(m_, th, PI, ALU.is_gt)
            add("vector", w1, reads=("QN0",), writes=("QN0",))
            def w2(v, sem, th=th, m_=m_, a_=a_):
                return v.scalar_tensor_tensor(a_, m_, -2 * PI, th, ALU.mult,
                                              ALU.add)
            add("vector", w2, reads=("QN0",), writes=("QN0",))
            def w3(s, sem, a_=a_, st=st):
                return s.activation(st, a_, AF.Sin)
            add("scalar", w3, reads=("QN0",), writes=("QN0",))
            def w4(v, sem, th=th, b_=b_):
                return v.tensor_scalar(b_, th, -1.0, PI / 2, ALU.mult, ALU.add)
            add("vector", w4, reads=("QN0",), writes=("QN0",))
            def w5(v, sem, b_=b_, m_=m_):
                return v.tensor_single_scalar(m_, b_, -PI, ALU.is_lt)
            add("vector", w5, reads=("QN0",), writes=("QN0",))
            def w6(v, sem, b_=b_, m_=m_):
                return v.scalar_tensor_tensor(b_, m_, 2 * PI, b_, ALU.mult,
                                              ALU.add)
            add("vector", w6, reads=("QN0",), writes=("QN0",))
            def w7(s, sem, b_=b_, ct=ct):
                return s.activation(ct, b_, AF.Sin)
            add("scalar", w7, reads=("QN0",), writes=("QN0",))
            def w8(te, sem, ct=ct):
                return te.transpose(PP[3][0:32, 0:128], ct, IDF[:, :])
            add("tensor", w8, reads=("QN0", "IDF"), writes=("P3A",))
            def w9(v, sem, tt=tt):
                return v.tensor_copy(CTT[:, 128 * tt:128 * (tt + 1)],
                                     PP[3][0:32, 0:128])
            add("vector", w9, reads=("P3A",), writes=("SCR_A", "SCR_B"),
                tag=f"ctt{tt}")
            def wA(te, sem, st=st):
                return te.transpose(PP[3][0:32, 0:128], st, IDF[:, :])
            add("tensor", wA, reads=("QN0", "IDF"), writes=("P3A",))
            def wB(v, sem, tt=tt):
                return v.tensor_copy(STT[:, 128 * tt:128 * (tt + 1)],
                                     PP[3][0:32, 0:128])
            add("vector", wB, reads=("P3A",), writes=("SCR_C", "SCR_D"),
                tag=f"stt{tt}")
        for t in range(2):
            def x1(te, sem, t=t):
                return te.matmul(PP[2][:, 512 * t:512 * (t + 1)], EC[:, :],
                                 CTT[:, 512 * t:512 * (t + 1)], start=True,
                                 stop=True)
            add("tensor", x1, reads=("SCR_A", "SCR_B", "EC"),
                writes=(pk(2)[t],))
            def x2(v, sem, t=t):
                return v.tensor_copy(TBL[:, 512 * t:512 * (t + 1)],
                                     PP[2][:, 512 * t:512 * (t + 1)])
            add("vector", x2, reads=(pk(2)[t],), writes=("TBL",),
                tag=f"c2b{t}")
            def x3(te, sem, t=t):
                return te.matmul(PP[3][:, 512 * t:512 * (t + 1)], ES[:, :],
                                 STT[:, 512 * t:512 * (t + 1)], start=True,
                                 stop=True)
            add("tensor", x3, reads=("SCR_C", "SCR_D", "ES"),
                writes=(pk(3)[t],))
            def x4(v, sem, t=t):
                return v.tensor_copy(TBL[:, T + 512 * t:T + 512 * (t + 1)],
                                     PP[3][:, 512 * t:512 * (t + 1)])
            add("vector", x4, reads=(pk(3)[t],), writes=("TBL",),
                tag=f"s2b{t}")

        emit_ln(Xc, lambda c: f"X{c}", LN, lambda c: f"LN{c}", NCH, "ln3")
        C2B = TBL[:, 0:T]
        S2B = TBL[:, T:2 * T]

    if stages >= 2.4:
        for o in range(4):
            slot = o % 2
            load_w(wkvac, o, 2048, slot, tag=f"kvaw{o}")
            pp = fm_mms(slot, NCH, LN, lambda c: f"LN{c}", o % 2, f"kvamm{o}")
            def fe_(s, sem, o=o, pp=pp):
                return s.activation(KVCB(o), pp[:, 0:T], AF.Copy)
            add("scalar", fe_, reads=pk(o % 2), writes=(f"QN{o}",),
                tag=f"kvaev{o}")
        load_w(wkvar, 0, 1024, 0, tag="kvrw")
        def fkr(te, sem):
            r = None
            for c in range(NCH):
                for t in range(2):
                    r = te.matmul(PP[0][0:64, 512 * t:512 * (t + 1)],
                                  wt(0, c, 64),
                                  LN(c)[:, 512 * t:512 * (t + 1)],
                                  start=(c == 0), stop=(c == NCH - 1),
                                  skip_group_check=True)
            return r
        add("tensor", fkr,
            reads=tuple(["WB0"] + [f"LN{c}" for c in range(NCH)]),
            writes=("P0A", "P0B"), tag="kvrmm")
        kst = STG[:, 0:T]
        rope_fm(PP[0], ("P0A", "P0B"), kst, "STG0", 64, C2B, S2B, "kvro")
        ld("sync", kb_in.ap()[2048:2112, :], kst[0:64, :], "DRAM_kbr",
           reads=("STG0",), tag="kbrdma", dgroup="kb0")
        emit_ln(lambda i: KVCB(i), lambda i: f"QN{i}",
                lambda i: KVCN(i), lambda i: f"QN{4 + i}", 4, "lnkv",
                gb=True)
        pass
    if stages >= 2.6:
        for o in range(16):
            slot = o % 2
            load_w(wkvbk, o, 512, slot, tag=f"kbkw{o}")
            pp = PP[o % 2]
            def fm(te, sem, slot=slot, pp=pp):
                r = None
                for c in range(4):
                    for t in range(2):
                        r = te.matmul(pp[:, 512 * t:512 * (t + 1)],
                                      wt(slot, c),
                                      KVCN(c)[:, 512 * t:512 * (t + 1)],
                                      start=(c == 0), stop=(c == 3),
                                      skip_group_check=True)
                return r
            add("tensor", fm,
                reads=tuple([f"WB{slot}"] + [f"QN{4 + c}" for c in range(4)]),
                writes=pk(o % 2), tag=f"kbkmm{o}")
            st_ = STG[:, T * (o % 2):T * (o % 2 + 1)]
            def fe_(s, sem, pp=pp, st_=st_):
                return s.activation(st_[:, :], pp[:, 0:T], AF.Copy)
            add("scalar", fe_, reads=pk(o % 2), writes=(f"STG{o % 2}",),
                tag=f"kbkev{o}")
            ld("sync", kb_in.ap()[128 * o:128 * (o + 1), :], st_[:, :],
               f"DRAM_kbk{o}", reads=(f"STG{o % 2}",), tag=f"kbkdma{o}",
               dgroup=f"kb{o % 2}")
        for o in range(8):
            slot = o % 2
            load_w(wkvbv, o, 1024, slot, tag=f"kbvw{o}")
            for tt in range(8):
                ppi = tt % 2
                pp = PP[ppi]
                def fv(te, sem, slot=slot, tt=tt, pp=pp):
                    r = None
                    for c in range(4):
                        r = te.matmul(pp[:, 0:256],
                                      KVCN(c)[:, 128 * tt:128 * (tt + 1)],
                                      wt(slot, c, 256), start=(c == 0),
                                      stop=(c == 3), skip_group_check=True)
                    return r
                add("tensor", fv,
                    reads=tuple([f"WB{slot}"] +
                                [f"QN{4 + c}" for c in range(4)]),
                    writes=pk(ppi), tag=f"kbvmm{o}_{tt}")
                vs = STG[:, T * (tt % 2) + 256:T * (tt % 2) + 512]
                def fe_(v, sem, pp=pp, vs=vs):
                    return v.tensor_copy(vs[:, 0:256], pp[:, 0:256])
                add("vector", fe_, reads=pk(ppi), writes=(f"STG{tt % 2}",),
                    tag=f"kbvev{o}_{tt}")
                ld("sync", vb_in.ap()[128 * tt:128 * (tt + 1),
                                      256 * o:256 * (o + 1)],
                   vs[:, 0:256], f"DRAM_vb{o}_{tt}", reads=(f"STG{tt % 2}",),
                   tag=f"kbvdma{o}_{tt}", dgroup=f"vb{tt % 2}")
        if stages >= 2.55:
            def cc_kb(g, sem):
                return g.collective_compute(
                    "AllGather", ALU.bypass, replica_groups=CCG,
                    ins=[kb_in.ap().opt()], outs=[kb_out.ap().opt()])
            add("gpsimd", cc_kb,
                reads=tuple([f"DRAM_kbk{o}" for o in range(16)] +
                            ["DRAM_kbr"]),
                writes=("DRAM_kbout",), coll=True, tag="cc_kb")
        if stages >= 2.56:
            def cc_vb(g, sem):
                return g.collective_compute(
                    "AllGather", ALU.bypass, replica_groups=CCG,
                    ins=[vb_in.ap().opt()], outs=[vb_out.ap().opt()])
            add("gpsimd", cc_vb,
                reads=tuple([f"DRAM_vb{o}_{t}" for o in range(8)
                             for t in range(8)]),
                writes=("DRAM_vbout",), coll=True, tag="cc_vb")
        for dst_base in ((0, 64) if stages >= 2.58 else ()):
            ld("sync", KRB[dst_base:dst_base + 64, 0:1024],
               kb_out.ap()[2048:2112, :], ("KRB", "KOWN0", "KOWN1"),
               reads=("DRAM_kbout",), tag=f"krb{dst_base}a", dgroup="krb")
            ld("sync", KRB[dst_base:dst_base + 64, 1024:2048],
               kb_out.ap()[4160:4224, :], ("KRB", "KOWN0", "KOWN1"),
               reads=("DRAM_kbout",), tag=f"krb{dst_base}b", dgroup="krb")

        pass
    if stages >= 2.8:
        for g in range(4):
            if stages < 3 and g > 0:
                break
            for hh in range(4):
                h = 4 * g + hh
                slot = h % 2
                load_w(wqb, h, 2048, slot, tag=f"qbnw{h}")
                pp = fm_mms(slot, NCH, LN, lambda c: f"LN{c}", h % 2,
                            f"qbnmm{h}")
                def fe_(s, sem, hh=hh, pp=pp):
                    return s.activation(QNt(hh), pp[:, 0:T], AF.Copy)
                add("scalar", fe_, reads=pk(h % 2), writes=(f"Q{hh}",),
                    tag=f"qbnev{h}")
            for jj in range(2):
                j = 2 * g + jj
                slot = j % 2
                load_w(wqb, 16 + j, 2048, slot, tag=f"qbrw{j}")
                pp = fm_mms(slot, NCH, LN, lambda c: f"LN{c}", j % 2,
                            f"qbrmm{j}")
                rope_fm(pp, pk(j % 2), QNt(4 + jj), "QROPE", 128, C2B, S2B,
                        f"qbro{j}")
            if stages < 2.9:
                continue
            for hh in range(4):
                h = 4 * g + hh
                head_loads(h, 1)
                attn_head(h, hh, 1,
                          qrope_src=(QNt(4 + hh // 2), 64 * (hh % 2)))
            if stages < 3:
                continue
            out_proj(wob, g, "ob")

    if stages >= 4:
        emit_ln(Xc, lambda c: f"X{c}", LN, lambda c: f"LN{c}", NCH, "ln4")
        ffn("f2")

    for c in range(NCH):
        ld("sync", y_out[128 * c:128 * (c + 1), :], Xc(c), f"Y{c}",
           reads=(f"X{c}",), tag=f"st{c}", dgroup="y")

    sc.finalize(nc, stack)
    for en in ENGINES:
        getattr(blk, en)(lambda eng, en=en: sc.emit_engine(eng, en))
    stack.close()
    return nc


def kernel(L0_attn_a_x, L0_attn_a_Wq, L0_attn_a_Wk, L0_attn_a_Wv, L0_attn_a_Wo,
           L0_attn_b_freqs_cis, L0_attn_b_Wq, L0_attn_b_Wkv_a,
           L0_attn_b_kv_gamma, L0_attn_b_kv_beta, L0_attn_b_Wkv_b,
           L0_attn_b_Wout, L0_ffn_base_w, L0_ffn_base_b, L0_ffn_router_w,
           L0_ffn_router_b, L0_ffn_lora_A, L0_ffn_lora_B, _stages=4,
           _taps=()):
    f32 = np.float32
    x = np.asarray(L0_attn_a_x, dtype=f32)

    def tile_w_og(W):
        A = W.reshape(16, 128, 16, 128)
        Bm = A.reshape(4, 4, 128, 16, 128).transpose(0, 3, 2, 1, 4)
        return np.ascontiguousarray(Bm.reshape(4, 16, 128, 512))

    wqa_h = tile_w(np.asarray(L0_attn_a_Wq, f32).T, 128)
    wka_h = tile_w(np.asarray(L0_attn_a_Wk, f32).T, 128)
    wva_h = tile_w(np.asarray(L0_attn_a_Wv, f32).T, 256)
    woa_h = tile_w_og(np.asarray(L0_attn_a_Wo, f32).T)
    basew_h = tile_w(np.asarray(L0_ffn_base_w, f32).T, 128)
    biasT_h = np.ascontiguousarray(np.asarray(L0_ffn_base_b, f32).reshape(16, 128).T)
    rb8_h = np.ascontiguousarray(
        np.broadcast_to(np.asarray(L0_ffn_router_b, f32), (128, 8)))
    routr_h = np.ascontiguousarray(
        np.asarray(L0_ffn_router_w, f32).T.reshape(16, 128, 8)
        .transpose(1, 0, 2).reshape(128, 128))
    aall_h = np.ascontiguousarray(
        np.asarray(L0_ffn_lora_A, f32).transpose(1, 0, 2).reshape(D, 128)
        .reshape(16, 128, 128).transpose(1, 0, 2).reshape(128, 2048))
    ball_h = np.ascontiguousarray(
        np.asarray(L0_ffn_lora_B, f32).reshape(128, D))
    wqb_r = np.asarray(L0_attn_b_Wq, f32)
    cols = []
    for h in range(16):
        cols.extend(range(192 * h, 192 * h + 128))
    for h in range(16):
        cols.extend(range(192 * h + 128, 192 * h + 192))
    wqb_h = tile_w(np.ascontiguousarray(wqb_r[:, cols]), 128)
    wkva_r = np.asarray(L0_attn_b_Wkv_a, f32)
    wkvac_h = tile_w(np.ascontiguousarray(wkva_r[:, 0:512]), 128)
    wkvar_h = tile_w(np.ascontiguousarray(wkva_r[:, 512:576]), 64)
    kvgb_h = np.zeros((128, 8), f32)
    kvgb_h[:, 0:4] = np.asarray(L0_attn_b_kv_gamma, f32).reshape(4, 128).T
    kvgb_h[:, 4:8] = np.asarray(L0_attn_b_kv_beta, f32).reshape(4, 128).T
    wkvb_r = np.asarray(L0_attn_b_Wkv_b, f32)
    kcols, vcols = [], []
    for h in range(16):
        kcols.extend(range(256 * h, 256 * h + 128))
        vcols.extend(range(256 * h + 128, 256 * h + 256))
    wkvbk_h = tile_w(np.ascontiguousarray(wkvb_r[:, kcols]), 128)
    wkvbv_h = tile_w(np.ascontiguousarray(wkvb_r[:, vcols]), 256)
    wob_h = tile_w_og(np.asarray(L0_attn_b_Wout, f32))
    fcis = np.asarray(L0_attn_b_freqs_cis, f32)

    nc = build_nc(_stages, taps=_taps)
    in_maps = []
    for core in range(8):
        b = core // 2
        par = core % 2
        pos = _positions(par)
        xc = np.ascontiguousarray(x[b][pos, :].T)
        c2a_h, s2a_h = _rope_tables_a(par)
        in_maps.append({
            "x_fm": xc, "wqa": wqa_h, "wka": wka_h, "wva": wva_h,
            "woa": woa_h, "basew": basew_h, "biasT": biasT_h, "rb8": rb8_h,
            "routr": routr_h, "aall": aall_h, "ball": ball_h, "wqb": wqb_h,
            "wkvac": wkvac_h, "wkvar": wkvar_h, "kvgb": kvgb_h,
            "wkvbk": wkvbk_h, "wkvbv": wkvbv_h, "wob": wob_h,
            "freqs": np.ascontiguousarray(fcis[pos, :]),
            "c2a": c2a_h, "s2a": s2a_h, "abias": _abias(par),
        })
    res = run_bass_kernel_spmd(nc, in_maps, core_ids=list(range(8)))
    y = np.empty((B, S, D), dtype=np.float32)
    for core in range(8):
        b = core // 2
        par = core % 2
        pos = _positions(par)
        y[b][pos, :] = res.results[core]["y_fm"].T
    if _taps:
        tapd = [{k[4:]: v for k, v in res.results[c].items()
                 if k.startswith("tap_")} for c in range(8)]
        return y, tapd
    return y


# version 15
# speedup vs baseline: 1.1457x; 1.0133x over previous
import numpy as np
import ml_dtypes
import concourse.bass as bass
import concourse.mybir as mybir
from concourse.bass_utils import run_bass_kernel_spmd

F32 = mybir.dt.float32
BF16 = mybir.dt.bfloat16
AF = mybir.ActivationFunctionType
ALU = mybir.AluOpType
AX = mybir.AxisListType
BF = ml_dtypes.bfloat16

B, S, D, H = 4, 2048, 2048, 16
T = 1024
NCH = 16
PERM = [0, 3, 1, 2]
OBK = ([0, 3], [1, 2])
SCALE_A = float(1.0 / np.sqrt(128.0))
SCALE_B = float(1.0 / np.sqrt(192.0))
MSUB = 20.0
NEG = -30000.0
EPS = 1e-5
THETA = 10000.0
PI = float(np.pi)
ITEMS_QB = ([0, 1, 2, 3], [0, 1, 2, 3, 8, 9, 10, 11, 12, 13, 14, 15])
ENGINES = ("tensor", "scalar", "vector", "gpsimd", "sync")


STRICT_SAME_ENGINE = True

SLOT_GROUPS = ("wb0", "wb1", "wbo0", "wbo1", "wbo2", "wbo3",
               "kh0", "kh1", "vh0", "vh1", "ko0", "ko1",
               "vo0", "vo1", "frq", "ka0", "ka1", "va0", "va1",
               "kb0", "kb1", "vb0", "vb1")


class Sched:
    def __init__(self):
        self.steps = []
        self.last_w = {}
        self.readers = {}

    def add(self, engine, emit, reads=(), writes=(), n_dma=0, coll=False,
            tag="", dgroup=None):
        deps = set()
        for b in reads:
            w = self.last_w.get(b)
            if w is not None:
                deps.add(w)
        for b in writes:
            w = self.last_w.get(b)
            if w is not None:
                deps.add(w)
            for r in self.readers.get(b, ()):
                deps.add(r)
        i = len(self.steps)
        if n_dma:
            assert dgroup is not None, f"dma step {tag} needs dgroup"
        self.steps.append(dict(engine=engine, emit=emit, deps=deps,
                               n_dma=n_dma, coll=coll, idx=i, tag=tag,
                               dgroup=dgroup, pub=False, sem=None, cnt=None,
                               inc=0))
        for b in reads:
            self.readers.setdefault(b, []).append(i)
        for b in writes:
            self.last_w[b] = i
            self.readers[b] = []
        return i

    def finalize(self, nc, stack):
        self.semc = {en: stack.enter_context(nc.semaphore(f"sc_{en}"))
                     for en in ("tensor", "scalar", "vector", "gpsimd")}
        self.semcc = stack.enter_context(nc.semaphore("scc"))
        groups = []
        for s in self.steps:
            if s["n_dma"] and s["dgroup"] not in groups:
                groups.append(s["dgroup"])
        self.semd = {g: stack.enter_context(nc.semaphore(f"sg_{g}"))
                     for g in groups}
        for s in self.steps:
            for d in s["deps"]:
                p = self.steps[d]
                if (p["n_dma"] or p["coll"] or p["engine"] != s["engine"]
                        or STRICT_SAME_ENGINE):
                    p["pub"] = True
        ctr = {}
        self.gtotal = {}
        self.glast = {}
        for s in self.steps:
            en = s["engine"]
            if s["coll"]:
                ctr["cc"] = ctr.get("cc", 0) + 1
                s["sem"], s["cnt"], s["inc"] = self.semcc, ctr["cc"], 1
            elif s["n_dma"]:
                g = s["dgroup"]
                ctr[g] = ctr.get(g, 0) + 16 * s["n_dma"]
                s["sem"], s["cnt"], s["inc"] = self.semd[g], ctr[g], 16
                self.gtotal[g] = ctr[g]
                self.glast[g] = s["idx"]
            elif s["pub"]:
                k = ("c", en)
                ctr[k] = ctr.get(k, 0) + 1
                s["sem"], s["cnt"], s["inc"] = self.semc[en], ctr[k], 1
        for s in self.steps:
            for d in s["deps"]:
                p = self.steps[d]
                if p["n_dma"] and p["dgroup"] not in SLOT_GROUPS:
                    assert s["idx"] > self.glast[p["dgroup"]] or \
                        s["n_dma"] and s["dgroup"] == p["dgroup"], (
                        f"batch dgroup {p['dgroup']} consumed at step "
                        f"{s['idx']}({s['tag']}) before group closed at "
                        f"{self.glast[p['dgroup']]}")
        self._fd = {}
        for s in self.steps:
            if s["n_dma"]:
                self._fd.setdefault(s["engine"], {})[s["dgroup"]] = \
                    self.gtotal[s["dgroup"]]

    def emit_engine(self, eng, en):
        hw = {}
        for s in self.steps:
            if s["engine"] != en:
                continue
            need = {}
            for d in sorted(s["deps"]):
                p = self.steps[d]
                if p["sem"] is None:
                    continue
                if (not STRICT_SAME_ENGINE and not p["n_dma"]
                        and not p["coll"] and p["engine"] == en):
                    continue
                if (p["n_dma"] and s["n_dma"]
                        and s["dgroup"] == p["dgroup"]):
                    continue
                cnt = p["cnt"]
                if p["n_dma"] and p["dgroup"] not in SLOT_GROUPS:
                    cnt = self.gtotal[p["dgroup"]]
                key = id(p["sem"])
                if need.get(key, (None, -1))[1] < cnt:
                    need[key] = (p["sem"], cnt)
            for sem, cnt in need.values():
                if hw.get(id(sem), 0) < cnt:
                    eng.wait_ge(sem, cnt)
                    hw[id(sem)] = cnt
            out = s["emit"](eng, s["sem"])
            if s["sem"] is not None and not s["n_dma"] and not s["coll"]:
                last = out[-1] if isinstance(out, (list, tuple)) else out
                last.then_inc(s["sem"], s["inc"])
            elif s["coll"]:
                out.then_inc(s["sem"], 1)
        for g, tot in self._fd.get(en, {}).items():
            eng.wait_ge(self.semd[g], tot)


def tile_w(W, ob):
    Din, O = W.shape
    return np.ascontiguousarray(
        W.reshape(Din // 128, 128, O // ob, ob).transpose(2, 1, 0, 3))


def _positions(parity):
    b0, b1 = OBK[parity]
    return np.concatenate([512 * b0 + np.arange(512), 512 * b1 + np.arange(512)])


def _rope_tables_a(parity):
    pos = _positions(parity).astype(np.float64)
    idx = np.arange(0, 128, 2, dtype=np.float64)
    inv = THETA ** (-idx / 128.0)
    ang = inv[:, None] * pos[None, :]
    c = np.repeat(np.cos(ang), 2, axis=0)
    s = np.repeat(np.sin(ang), 2, axis=0)
    s[0::2] *= -1.0
    return c.astype(np.float32), s.astype(np.float32)


def _abias(parity):
    ab = np.zeros((128, 16), dtype=np.float32)
    col = 0
    for qb in range(2):
        gqb = OBK[parity][qb]
        for kt in ITEMS_QB[qb]:
            kb = PERM[kt // 4]
            ab[:, col] = -MSUB if kb < gqb else NEG
            col += 1
    return ab


def build_nc(stages=4, taps=()):
    from contextlib import ExitStack
    nc = bass.Bass("TRN2", target_bir_lowering=False)
    sc = Sched()
    stack = ExitStack()
    e = stack.enter_context

    def din(name, shape):
        return nc.dram_tensor(name, list(shape), F32, kind="ExternalInput").ap()

    x_in = din("x_fm", (D, T))
    y_out = nc.dram_tensor("y_fm", [D, T], F32, kind="ExternalOutput").ap()
    wqa = din("wqa", (16, 128, 16, 128))
    wka = din("wka", (16, 128, 16, 128))
    wva = din("wva", (8, 128, 16, 256))
    woa = din("woa", (4, 16, 128, 512))
    basew = din("basew", (16, 128, 16, 128))
    biasT_i = din("biasT", (128, 16))
    rb8_i = din("rb8", (128, 8))
    routr = din("routr", (128, 128))
    aall = din("aall", (128, 2048))
    ball = din("ball", (128, 2048))
    wqb = din("wqb", (24, 128, 16, 128))
    wkvac = din("wkvac", (4, 128, 16, 128))
    wkvar = din("wkvar", (1, 128, 16, 64))
    kvgb_i = din("kvgb", (128, 8))
    wkvbk = din("wkvbk", (16, 128, 4, 128))
    wkvbv = din("wkvbv", (8, 128, 4, 256))
    wob = din("wob", (4, 16, 128, 512))
    freqs = din("freqs", (T, 32))
    c2a_i = din("c2a", (128, T))
    s2a_i = din("s2a", (128, T))
    abias_i = din("abias", (128, 16))

    ka_in = nc.dram_tensor("ka_in", [D, T], BF16, kind="Internal")
    ka_out = nc.dram_tensor("ka_out", [2 * D, T], BF16, kind="Internal")
    va_in = nc.dram_tensor("va_in", [T, D], BF16, kind="Internal")
    va_out = nc.dram_tensor("va_out", [2 * T, D], BF16, kind="Internal")
    kb_in = nc.dram_tensor("kb_in", [2112, T], BF16, kind="Internal")
    kb_out = nc.dram_tensor("kb_out", [4224, T], BF16, kind="Internal")
    vb_in = nc.dram_tensor("vb_in", [T, D], BF16, kind="Internal")
    vb_out = nc.dram_tensor("vb_out", [2 * T, D], BF16, kind="Internal")
    CCG = [[0, 1], [2, 3], [4, 5], [6, 7]]

    ident_f = nc.inline_tensor(np.eye(128, dtype=np.float32), name="idf").ap()
    psw_np = np.zeros((128, 128), dtype=BF)
    for i in range(128):
        psw_np[i, i ^ 1] = 1
    pswap_c = nc.inline_tensor(psw_np, name="psw").ap()
    tri_np = np.where(np.triu(np.ones((128, 128))) > 0, 0.0, NEG).astype(BF)
    tri_c = nc.inline_tensor(tri_np, name="tri").ap()
    idb_c = nc.inline_tensor(np.eye(128, dtype=BF), name="idb").ap()
    onc_c = nc.inline_tensor(np.ones((128, 1), dtype=BF), name="onc").ap()
    onb_c = nc.inline_tensor(np.ones((128, 128), dtype=BF), name="onb").ap()
    onrf_c = nc.inline_tensor(np.ones((1, 128), dtype=np.float32),
                              name="onrf").ap()
    e2_np = np.zeros((8, 128), dtype=np.float32)
    for ex in range(8):
        e2_np[ex, 16 * ex:16 * ex + 16] = 2.0
    e2_c = nc.inline_tensor(e2_np, name="e2").ap()
    ec_np = np.zeros((32, 128), dtype=np.float32)
    es_np = np.zeros((32, 128), dtype=np.float32)
    for r in range(128):
        fi = (r % 64) // 2
        ec_np[fi, r] = 1.0
        es_np[fi, r] = -1.0 if r % 2 == 0 else 1.0
    ec_c = nc.inline_tensor(ec_np, name="ec").ap()
    mb20_c = nc.inline_tensor(np.full((128, 1), -MSUB, dtype=np.float32),
                              name="mb20").ap()
    epsc_c = nc.inline_tensor(np.full((128, 1), EPS, dtype=np.float32),
                              name="epsc").ap()
    es_c = nc.inline_tensor(es_np, name="es").ap()

    XR = e(nc.sbuf_tensor("XR", [128, NCH * T], F32))
    LNR = e(nc.sbuf_tensor("LNR", [128, NCH * T], BF16))
    QN = e(nc.sbuf_tensor("QN", [128, 8 * T], BF16))
    QR = e(nc.sbuf_tensor("QR", [128, 4 * T], BF16))
    KH = e(nc.sbuf_tensor("KH", [128, 2 * 2048], BF16))
    VH = e(nc.sbuf_tensor("VH", [128, 2 * 2048], BF16))
    KOWN = e(nc.sbuf_tensor("KOWN", [128, 2 * T], BF16))
    VOWN = e(nc.sbuf_tensor("VOWN", [128, 2 * T], BF16))
    KRB = KOWN
    WBUF = e(nc.sbuf_tensor("WBUF", [128, 2 * 4096], BF16))
    TBL = e(nc.sbuf_tensor("TBL", [128, 2 * T], F32))
    SCR = e(nc.sbuf_tensor("SCR", [128, 2 * T], F32))
    ROWB = e(nc.sbuf_tensor("ROWB", [1, 2 * T], F32))
    PTS = e(nc.sbuf_tensor("PTS", [128, 2 * 512], BF16))
    STG = e(nc.sbuf_tensor("STG", [128, 2 * T], BF16))
    RINV = e(nc.sbuf_tensor("RINV", [128, 512], F32))
    RAs = e(nc.sbuf_tensor("RAs", [128, 2048], BF16))
    RBs = e(nc.sbuf_tensor("RBs", [128, 2048], BF16))
    RRs = e(nc.sbuf_tensor("RRs", [128, 128], BF16))
    RB8 = e(nc.sbuf_tensor("RB8", [128, 8], F32))
    KVGB = e(nc.sbuf_tensor("KVGB", [128, 8], F32))
    ABIAS = e(nc.sbuf_tensor("ABIAS", [128, 16], F32))
    IDF = e(nc.sbuf_tensor("IDF", [128, 128], F32))
    IDB = e(nc.sbuf_tensor("IDB", [128, 128], BF16))
    PSW = e(nc.sbuf_tensor("PSW", [128, 128], BF16))
    TRIt = e(nc.sbuf_tensor("TRIt", [128, 128], BF16))
    ONC = e(nc.sbuf_tensor("ONC", [128, 1], BF16))
    ONB = e(nc.sbuf_tensor("ONB", [128, 128], BF16))
    ONRF = e(nc.sbuf_tensor("ONRF", [1, 128], F32))
    MB20 = e(nc.sbuf_tensor("MB20", [128, 1], F32))
    EPSC = e(nc.sbuf_tensor("EPSC", [128, 1], F32))
    BIASC = e(nc.sbuf_tensor("BIASC", [128, 16], F32))
    E2 = e(nc.sbuf_tensor("E2", [8, 128], F32))
    EC = e(nc.sbuf_tensor("EC", [32, 128], F32))
    ES = e(nc.sbuf_tensor("ES", [32, 128], F32))
    PP = [e(nc.psum_tensor(f"PP{i}", [128, 1024], F32)) for i in range(4)]

    blk = e(nc.Block())
    add = sc.add

    def Xc(c):
        return XR[:, T * c:T * (c + 1)]

    def LN(c):
        return LNR[:, T * c:T * (c + 1)]

    def QNt(i):
        return QN[:, T * i:T * (i + 1)]

    def QRt(i):
        return QR[:, T * i:T * (i + 1)]

    def KVCB(i):
        return QNt(i)

    def KVCN(i):
        return QNt(4 + i)

    MEAN = ROWB[0:1, 0:T]
    AUX = ROWB[0:1, T:2 * T]
    DENR = ROWB[0:1, 0:512]
    FRQ = QN[:, 0:384].bitcast(F32)
    GT = SCR[0:8, 0:T]
    GDAT = QN[:, 384:512].bitcast(F32)
    GTMP = QN[:, 512:624].bitcast(F32)
    def SG(i, n=1):
        return SCR[:, 512 * i:512 * (i + n)]

    PKEY = [("P0A", "P0B"), ("P1A", "P1B"), ("P2A", "P2B"), ("P3A", "P3B")]

    def pk(i):
        return PKEY[i]

    def ld(en, dst, src, wbuf, reads=(), tag="ld", dgroup="init"):
        def f(g, sem, dst=dst, src=src):
            return g.dma_start(out=dst, in_=src).then_inc(sem, 16)
        return add(en, f, reads=reads, writes=(wbuf,) if isinstance(wbuf, str)
                   else tuple(wbuf), n_dma=1, tag=tag, dgroup=dgroup)

    for dst, src, b in ((IDF[:, :], ident_f, "IDF"), (IDB[:, :], idb_c, "IDB"),
                        (PSW[:, :], pswap_c, "PSW"), (TRIt[:, :], tri_c, "TRI"),
                        (ONC[:, :], onc_c, "ONC"), (ONRF[:, :], onrf_c, "ONRF"),
                        (ONB[:, :], onb_c, "ONB"),
                        (MB20[:, :], mb20_c, "MB20"),
                        (EPSC[:, :], epsc_c, "EPSC"),
                        (E2[:, :], e2_c, "E2"),
                        (EC[:, :], ec_c, "EC"), (ES[:, :], es_c, "ES"),
                        (ABIAS[:, :], abias_i, "ABIAS"),
                        (RB8[:, :], rb8_i, "RB8"), (KVGB[:, :], kvgb_i, "KVGB"),
                        (TBL[:, 0:T], c2a_i, "TBL"),
                        (TBL[:, T:2 * T], s2a_i, "TBLS")):
        ld("sync", dst, src, b)
    for c in range(NCH):
        ld("sync", Xc(c), x_in[128 * c:128 * (c + 1), :], f"X{c}", dgroup="x")
    ld("sync", BIASC[:, :], biasT_i, "BIASC", dgroup="init")
    ld("gpsimd", RAs[:, :], aall, "RA", dgroup="ginit")
    ld("gpsimd", RBs[:, :], ball, "RB_", dgroup="ginit")
    ld("gpsimd", RRs[:, :], routr, "RR", dgroup="ginit")

    tap_outs = {}

    def tap(name, ap, keys, cast=False):
        if name not in taps:
            return
        shp = list(ap.shape)
        o = nc.dram_tensor(f"tap_{name}", shp, F32, kind="ExternalOutput").ap()
        tap_outs[name] = True
        ld("gpsimd" if cast else "sync", o, ap, f"TAPO_{name}",
           reads=tuple(keys), tag=f"tap{name}", dgroup="tap")

    def load_w(wap, o, cols, slot, csel=None, tag="w"):
        if csel is None:
            src = wap[o].rearrange("p c j -> p (c j)")
        else:
            src = wap[o, :, csel[0]:csel[1], :].rearrange("p c j -> p (c j)")
        dst = WBUF[:, 4096 * slot:4096 * slot + cols]
        return ld("gpsimd", dst, src,
                  (f"WB{slot}", f"WBo{2 * slot}", f"WBo{2 * slot + 1}"),
                  tag=tag, dgroup=f"wb{slot}")

    def wt(slot, c, ob=128):
        return WBUF[:, 4096 * slot + ob * c:4096 * slot + ob * (c + 1)]

    def load_w_og(wap, g, o, tag="w"):
        sub = o % 4
        base = 4096 * (sub // 2) + 512 * (sub % 2)
        dst = WBUF[:, base:base + 512]
        return ld("gpsimd", dst, wap[g, o],
                  (f"WBo{sub}", f"WB{sub // 2}"), tag=tag,
                  dgroup=f"wbo{sub}")

    def emit_ln(src, skey, dst, dkey, nch, tag, gb=None):
        dsum = float(128 * nch)
        for c in range(nch):
            def f(s, sem, c=c):
                return s.activation(dst(c), src(c), AF.Copy)
            add("scalar", f, reads=(skey(c),), writes=(dkey(c),),
                tag=f"{tag}cast{c}")
        for c in range(nch):
            sqb = SG(c % 2, 1).bitcast(BF16)
            def fsq(s, sem, c=c, sqb=sqb):
                return s.activation(sqb, dst(c), AF.Square)
            add("scalar", fsq, reads=(dkey(c),),
                writes=(f"SCR_{'A' if c % 2 == 0 else 'B'}",),
                tag=f"{tag}sq{c}")
            for t in range(2):
                def fmu(te, sem, c=c, t=t):
                    return te.matmul(PP[2][:, 512 * t:512 * (t + 1)],
                                     ONB[:, :],
                                     dst(c)[:, 512 * t:512 * (t + 1)],
                                     start=(c == 0), stop=(c == nch - 1),
                                     skip_group_check=True)
                add("tensor", fmu, reads=(dkey(c), "ONB"),
                    writes=(pk(2)[t],), tag=f"{tag}mu{c}_{t}")
                def fs2(te, sem, c=c, t=t, sqb=sqb):
                    return te.matmul(PP[3][:, 512 * t:512 * (t + 1)],
                                     ONB[:, :], sqb[:, 512 * t:512 * (t + 1)],
                                     start=(c == 0), stop=(c == nch - 1),
                                     skip_group_check=True)
                add("tensor", fs2,
                    reads=(f"SCR_{'A' if c % 2 == 0 else 'B'}", "ONB"),
                    writes=(pk(3)[t],), tag=f"{tag}sm{c}_{t}")
        MB = SG(0, 2)
        TMPB = SG(2, 2)
        def v1(v, sem):
            return v.tensor_single_scalar(MB, PP[3][:, 0:T], 1.0 / dsum,
                                          ALU.mult)
        add("vector", v1, reads=("P3A", "P3B"), writes=("SCR_A", "SCR_B"))
        def v2(v, sem):
            return v.tensor_single_scalar(PP[3][:, 0:T], PP[2][:, 0:T],
                                          1.0 / dsum, ALU.mult)
        add("vector", v2, reads=("P2A", "P2B", "P3A", "P3B"),
            writes=("P3A", "P3B"))
        def s2(s, sem):
            return s.activation(TMPB, PP[3][:, 0:T], AF.Square)
        add("scalar", s2, reads=("P3A", "P3B"), writes=("SCR_C", "SCR_D"))
        def v3(v, sem):
            return v.tensor_sub(MB, MB, TMPB)
        add("vector", v3, reads=("SCR_A", "SCR_B", "SCR_C", "SCR_D"),
            writes=("SCR_A", "SCR_B"))
        def s3(s, sem):
            return s.activation(TMPB, MB, AF.Ln, bias=EPSC[:, 0:1])
        add("scalar", s3, reads=("SCR_A", "SCR_B", "EPSC"),
            writes=("SCR_C", "SCR_D"))
        def s4(s, sem):
            return s.activation(PP[2][:, 0:T], TMPB, AF.Exp, scale=-0.5)
        add("scalar", s4, reads=("SCR_C", "SCR_D", "P2A", "P2B"),
            writes=("P2A", "P2B"))
        for c in range(nch):
            def fa(v, sem, c=c):
                return v.tensor_sub(TMPB, src(c), PP[3][:, 0:T])
            add("vector", fa, reads=(skey(c), "P3A", "P3B"),
                writes=("SCR_C", "SCR_D"), tag=f"{tag}a1_{c}")
            if gb is None:
                def fo(v, sem, c=c):
                    return v.tensor_mul(dst(c), TMPB, PP[2][:, 0:T])
                add("vector", fo,
                    reads=("SCR_C", "SCR_D", "P2A", "P2B"),
                    writes=(dkey(c),), tag=f"{tag}a2_{c}")
            else:
                pp = PP[c % 2]
                pkk = pk(c % 2)
                def fo(v, sem, c=c, pp=pp):
                    return v.tensor_mul(pp[:, 0:T], TMPB, PP[2][:, 0:T])
                add("vector", fo,
                    reads=("SCR_C", "SCR_D", "P2A", "P2B"),
                    writes=(pkk[0], pkk[1]), tag=f"{tag}a2_{c}")
                def fg(v, sem, c=c, pp=pp):
                    return v.tensor_scalar(dst(c), pp[:, 0:T],
                                           KVGB[:, c:c + 1],
                                           KVGB[:, 4 + c:5 + c],
                                           ALU.mult, ALU.add)
                add("vector", fg, reads=(pkk[0], pkk[1], "KVGB"),
                    writes=(dkey(c),), tag=f"{tag}a3_{c}")

    def rope_fm(pp, ppkeys, dst, dkey, rows, c2, s2, tag):
        qfv = [SG(2, 1).bitcast(BF16)[0:128, 512 * t:512 * (t + 1)]
               for t in range(2)]
        for t in range(2):
            def f1(v, sem, t=t):
                return v.tensor_copy(qfv[t][0:rows, :],
                                     pp[0:rows, 512 * t:512 * (t + 1)])
            add("vector", f1, reads=ppkeys, writes=(f"SCR_C",),
                tag=f"{tag}qf{t}")
            def f2(te, sem, t=t):
                return te.matmul(PP[2][0:rows, 512 * t:512 * (t + 1)],
                                 PSW[0:rows, 0:rows], qfv[t][0:rows, :],
                                 start=True, stop=True)
            add("tensor", f2, reads=("SCR_C", "PSW"), writes=(pk(2)[t],),
                tag=f"{tag}sw{t}")
        for t in range(2):
            t1 = SG(3, 1)
            def f3(v, sem, t=t, t1=t1):
                return v.tensor_mul(t1[0:rows, :],
                                    pp[0:rows, 512 * t:512 * (t + 1)],
                                    c2[0:rows, 512 * t:512 * (t + 1)])
            add("vector", f3, reads=ppkeys + ("TBL",), writes=("SCR_D",),
                tag=f"{tag}t1_{t}")
            t2 = SG(1, 1)
            def f4(v, sem, t=t, t2=t2):
                return v.tensor_mul(t2[0:rows, :],
                                    PP[2][0:rows, 512 * t:512 * (t + 1)],
                                    s2[0:rows, 512 * t:512 * (t + 1)])
            add("vector", f4, reads=(pk(2)[t], "TBL"), writes=("SCR_B",),
                tag=f"{tag}t2_{t}")
            def f5(v, sem, t=t, t1=t1, t2=t2):
                return v.tensor_add(dst[0:rows, 512 * t:512 * (t + 1)],
                                    t1[0:rows, :], t2[0:rows, :])
            add("vector", f5, reads=("SCR_D", "SCR_B"), writes=(dkey,),
                tag=f"{tag}o{t}")

    def fm_mms(wslot, nch_in, srcf, skeyf, ppi, tag, start_new=True):
        pp = PP[ppi]
        def fm(te, sem):
            r = None
            for c in range(nch_in):
                for t in range(2):
                    r = te.matmul(pp[:, 512 * t:512 * (t + 1)], wt(wslot, c),
                                  srcf(c)[:, 512 * t:512 * (t + 1)],
                                  start=(start_new and c == 0), stop=False,
                                  skip_group_check=True)
            return r
        rd = tuple([f"WB{wslot}"] + [skeyf(c) for c in range(nch_in)])
        add("tensor", fm, reads=rd, writes=pk(ppi), tag=tag)
        return pp

    emit_ln(Xc, lambda c: f"X{c}", LN, lambda c: f"LN{c}", NCH, "ln1")

    for c in range(NCH):
        tap(f"ln1_{c}", LN(c), (f"LN{c}",), cast=True)

    C2A = TBL[:, 0:T]
    S2A = TBL[:, T:2 * T]

    def attn_proj_kq(wap, n_o, dst_of, dkey_of, stage_dma, tag):
        for h in range(n_o):
            slot = h % 2
            load_w(wap, h, 2048, slot, tag=f"{tag}w{h}")
            pp = fm_mms(slot, NCH, LN, lambda c: f"LN{c}", h % 2,
                        f"{tag}mm{h}")
            rope_fm(pp, pk(h % 2), dst_of(h), dkey_of(h), 128, C2A, S2A,
                    f"{tag}ro{h}")
            if stage_dma is not None:
                stage_dma(h)

    import os as _os
    _chop0 = int(_os.environ.get("KCHOP", "0"))
    if stages >= 1 and _chop0 >= 3:
        C2A_ = TBL[:, 0:T]
        S2A_ = TBL[:, T:2 * T]
        for hh in range(4):
            h = hh
            slot = h % 2
            load_w(wqa, h, 2048, slot, tag=f"qaw{h}")
            pp = fm_mms(slot, NCH, LN, lambda c: f"LN{c}", h % 2, f"qamm{h}")
            rope_fm(pp, pk(h % 2), QNt(hh), f"Q{hh}", 128, C2A_, S2A_,
                    f"qaro{h}")
            tap(f"q0_{hh}", QNt(hh), (f"Q{hh}",), cast=True)
    if stages >= 1 and _chop0 < 3 and _chop0 != 7:
        def kst_dma(h):
            slot = h % 2
            ld("sync", ka_in.ap()[128 * h:128 * (h + 1), :],
               STG[:, T * slot:T * (slot + 1)], f"DRAM_ka{h}",
               reads=(f"STG{slot}",), tag=f"kadma{h}", dgroup=f"ka{h % 2}")
        attn_proj_kq(wka, H, lambda h: STG[:, T * (h % 2):T * (h % 2 + 1)],
                     lambda h: f"STG{h % 2}", kst_dma, "ka")

        pass
    if stages >= 1 and _chop0 < 3 and _chop0 != 6:
        def cc_ka(g, sem):
            return g.collective_compute(
                "AllGather", ALU.bypass, replica_groups=CCG,
                ins=[ka_in.ap().opt()], outs=[ka_out.ap().opt()])
        add("gpsimd", cc_ka,
            reads=tuple([f"DRAM_ka{h}" for h in range(H)]),
            writes=("DRAM_kaout",), coll=True, tag="cc_ka")
        for o in range(8):
            slot = o % 2
            load_w(wva, o, 4096, slot, tag=f"vaw{o}")
            for tt in range(8):
                ppi = tt % 2
                pp = PP[ppi]
                def fv(te, sem, slot=slot, tt=tt, pp=pp):
                    r = None
                    for c in range(NCH):
                        r = te.matmul(pp[:, 0:256],
                                      LN(c)[:, 128 * tt:128 * (tt + 1)],
                                      wt(slot, c, 256),
                                      start=(c == 0), stop=(c == NCH - 1),
                                      skip_group_check=True)
                    return r
                add("tensor", fv,
                    reads=tuple([f"WB{slot}"] + [f"LN{c}" for c in range(NCH)]),
                    writes=pk(ppi), tag=f"vamm{o}_{tt}")
                vs = STG[:, T * (tt % 2) + 256:T * (tt % 2) + 512]
                def fe(v, sem, pp=pp, vs=vs):
                    return v.tensor_copy(vs[:, 0:256], pp[:, 0:256])
                add("vector", fe, reads=pk(ppi), writes=(f"STG{tt % 2}",),
                    tag=f"vaev{o}_{tt}")
                ld("sync", va_in.ap()[128 * tt:128 * (tt + 1),
                                      256 * o:256 * (o + 1)],
                   vs[:, 0:256], f"DRAM_va{o}_{tt}", reads=(f"STG{tt % 2}",),
                   tag=f"vadma{o}_{tt}", dgroup=f"va{tt % 2}")

        pass
    if stages >= 1 and _chop0 < 3 and _chop0 not in (6, 7):
        def cc_va(g, sem):
            return g.collective_compute(
                "AllGather", ALU.bypass, replica_groups=CCG,
                ins=[va_in.ap().opt()], outs=[va_out.ap().opt()])
        add("gpsimd", cc_va,
            reads=tuple([f"DRAM_va{o}_{t}" for o in range(8)
                         for t in range(8)]),
            writes=("DRAM_vaout",), coll=True, tag="cc_va")

        pass
    if stages >= 1 and _chop0 < 3:
        tap("kain", ka_in.ap(),
            tuple(f"DRAM_ka{h}" for h in range(H)), cast=True)
        tap("kaout", ka_out.ap(), ("DRAM_kaout",), cast=True)
        tap("vain", va_in.ap(),
            tuple(f"DRAM_va{o}_{t}" for o in range(8) for t in range(8)),
            cast=True)

        def head_loads(h, layer):
            slot = h % 2
            kout = ka_out if layer == 0 else kb_out
            vout = va_out if layer == 0 else vb_out
            kok = "DRAM_kaout" if layer == 0 else "DRAM_kbout"
            vok = "DRAM_vaout" if layer == 0 else "DRAM_vbout"
            rowB = 2048 if layer == 0 else 2112
            ld("sync", KH[:, 2048 * slot:2048 * slot + 1024],
               kout.ap()[128 * h:128 * (h + 1), :], f"KH{slot}",
               reads=(kok,), tag=f"khA{h}", dgroup=f"kh{slot}")
            ld("sync", KH[:, 2048 * slot + 1024:2048 * slot + 2048],
               kout.ap()[rowB + 128 * h:rowB + 128 * (h + 1), :], f"KH{slot}",
               reads=(kok,), tag=f"khB{h}", dgroup=f"kh{slot}")
            for half in range(4):
                src = vout.ap()[:, 128 * h:128 * (h + 1)].rearrange(
                    "(k p) d -> p k d", p=128)[:, 4 * half:4 * (half + 1), :]
                dst = VH[:, 2048 * slot + 512 * half:
                         2048 * slot + 512 * (half + 1)].rearrange(
                    "p (k d) -> p k d", d=128)
                ld("sync", dst, src, f"VH{slot}", reads=(vok,),
                   tag=f"vh{h}_{half}", dgroup=f"vh{slot}")

        def head_loads_own(h):
            slot = h % 2
            ld("sync", KOWN[:, T * slot:T * (slot + 1)],
               ka_in.ap()[128 * h:128 * (h + 1), :], f"KOWN{slot}",
               reads=(f"DRAM_ka{h}",), tag=f"ko{h}", dgroup=f"ko{slot}")
            src = va_in.ap()[:, 128 * h:128 * (h + 1)].rearrange(
                "(k p) d -> p k d", p=128)
            ld("sync", VOWN[:, T * slot:T * (slot + 1)].rearrange(
                "p (k d) -> p k d", d=128), src, f"VOWN{slot}",
               reads=tuple([f"DRAM_va{h // 2}_{tt}" for tt in range(8)]),
               tag=f"vo{h}", dgroup=f"vo{slot}")

        def attn_head(h, hh, layer, qrope_src=None):
            slot = h % 2
            scale = SCALE_A if layer == 0 else SCALE_B
            qtile = QNt(hh)
            if layer == 1:
                emitters = []
                for it_i in range(16):
                    ps = it_i % 2
                    psl = PP[0] if ps == 0 else PP[3]
                    pskey = ("P0A", "P0B") if ps == 0 else ("P3A", "P3B")
                    ptv = STG[:, 1024 * ps:1024 * (ps + 1)]
                    ptkey = f"STG{ps}"
                    first = it_i == 0
                    def es(it_i=it_i, psl=psl, pskey=pskey):
                        def fs(te, sem, idx=it_i, psl=psl):
                            r = None
                            for qb in range(2):
                                te.matmul(
                                    psl[:, 512 * qb:512 * (qb + 1)],
                                    KH[:, 2048 * slot + 128 * idx:
                                       2048 * slot + 128 * (idx + 1)],
                                    qtile[:, 512 * qb:512 * (qb + 1)],
                                    start=True, stop=False,
                                    skip_group_check=True)
                            qrt, rbase = qrope_src
                            for qb in range(2):
                                r = te.matmul(
                                    psl[:, 512 * qb:512 * (qb + 1)],
                                    KRB[rbase:rbase + 64,
                                        128 * idx:128 * (idx + 1)],
                                    qrt[rbase:rbase + 64,
                                        512 * qb:512 * (qb + 1)],
                                    start=False, stop=True,
                                    skip_group_check=True)
                            return r
                        add("tensor", fs,
                            reads=(f"KH{slot}", f"Q{hh}", "KRB", "QROPE"),
                            writes=pskey, tag=f"S{h}_m{it_i}")
                    def ee(it_i=it_i, psl=psl, ptv=ptv, pskey=pskey,
                           ptkey=ptkey):
                        def fe_(s, sem, psl=psl, ptv=ptv):
                            return s.activation(ptv[:, :], psl[:, 0:1024],
                                                AF.Exp, bias=MB20[:, 0:1],
                                                scale=scale)
                        add("scalar", fe_, reads=pskey + ("MB20",),
                            writes=(ptkey,), tag=f"E{h}_m{it_i}")
                    def ep(it_i=it_i, ptv=ptv, ptkey=ptkey, first=first):
                        vsrc = VH[:, 2048 * slot + 128 * it_i:
                                  2048 * slot + 128 * (it_i + 1)]
                        def fpv(te, sem, vsrc=vsrc, ptv=ptv, first=first):
                            r = None
                            for qb in range(2):
                                te.matmul(
                                    PP[1][:, 512 * qb:512 * (qb + 1)],
                                    vsrc, ptv[:, 512 * qb:512 * (qb + 1)],
                                    start=first, stop=False,
                                    skip_group_check=True)
                                r = te.matmul(
                                    PP[2][:, 512 * qb:512 * (qb + 1)],
                                    ONB[:, :],
                                    ptv[:, 512 * qb:512 * (qb + 1)],
                                    start=first, stop=False,
                                    skip_group_check=True)
                            return r
                        add("tensor", fpv, reads=(ptkey, f"VH{slot}", "ONB"),
                            writes=(pk(1)[0], pk(1)[1], pk(2)[0], pk(2)[1]),
                            tag=f"PV{h}_m{it_i}")
                    emitters.append((es, ee, ep))
                emitters[0][0]()
                for i in range(1, 16):
                    emitters[i - 1][1]()
                    emitters[i][0]()
                    emitters[i - 1][2]()
                emitters[-1][1]()
                emitters[-1][2]()
                RINV2 = SG(0, 2)
                def frc(v, sem):
                    return v.reciprocal(RINV2, PP[2][:, 0:1024])
                add("vector", frc, reads=("P2A", "P2B"),
                    writes=("SCR_A", "SCR_B"), tag=f"rc{h}")
                att = QRt(hh)
                def fat(v, sem, att=att):
                    return v.tensor_mul(att[:, 0:1024], PP[1][:, 0:1024],
                                        RINV2)
                add("vector", fat, reads=("P1A", "P1B", "SCR_A", "SCR_B"),
                    writes=(f"QR{hh}",), tag=f"at{h}")
                return
            for qb in range(2):
                items = []
                if layer == 0:
                    for tw in range(4):
                        items.append(("d", tw, None))
                    for p_, kt in enumerate(ITEMS_QB[qb]):
                        items.append(("g", kt,
                                      (p_ if qb == 0 else 4 + p_)))
                else:
                    for kt in range(16):
                        items.append(("g", kt, None))
                emitters = []
                for it_i, (kind, idx, bcol) in enumerate(items):
                    ps_i = it_i % 2
                    pskey = pk(0)[ps_i]
                    psl = PP[0][:, 512 * ps_i:512 * (ps_i + 1)]
                    ptv = PTS[:, 512 * ps_i:512 * (ps_i + 1)]
                    ptkey = f"PT{ps_i}"
                    first = it_i == 0
                    if kind == "g":
                        qoff = 0
                        def es(it_i=it_i, idx=idx, psl=psl, pskey=pskey,
                               slot=slot, qb=qb, qtile=qtile):
                            def fs(te, sem, idx=idx, psl=psl, slot=slot,
                                   qb=qb, qtile=qtile):
                                r = te.matmul(
                                    psl[:, :],
                                    KH[:, 2048 * slot + 128 * idx:
                                       2048 * slot + 128 * (idx + 1)],
                                    qtile[:, 512 * qb:512 * (qb + 1)],
                                    start=True, stop=(layer == 0),
                                    skip_group_check=True)
                                if layer == 1:
                                    qrt, rbase = qrope_src
                                    r = te.matmul(
                                        psl[:, :],
                                        KRB[rbase:rbase + 64,
                                            128 * idx:128 * (idx + 1)],
                                        qrt[rbase:rbase + 64,
                                            512 * qb:512 * (qb + 1)],
                                        start=False, stop=True,
                                        skip_group_check=True)
                                return r
                            add("tensor", fs,
                                reads=(f"KH{slot}", f"Q{hh}") if layer == 0
                                else (f"KH{slot}", f"Q{hh}", "KRB", "QROPE"),
                                writes=(pskey,), tag=f"S{h}_{qb}_{it_i}")
                        if layer == 0:
                            def ee(it_i=it_i, psl=psl, ptv=ptv, bcol=bcol,
                                   pskey=pskey, ptkey=ptkey):
                                def fe_(s, sem, psl=psl, ptv=ptv, bcol=bcol):
                                    return s.activation(
                                        ptv[:, :], psl[:, :], AF.Exp,
                                        bias=ABIAS[:, bcol:bcol + 1],
                                        scale=scale)
                                add("scalar", fe_, reads=(pskey, "ABIAS"),
                                    writes=(ptkey,), tag=f"E{h}_{qb}_{it_i}")
                        else:
                            def ee(it_i=it_i, psl=psl, ptv=ptv,
                                   pskey=pskey, ptkey=ptkey):
                                def fe_(s, sem, psl=psl, ptv=ptv):
                                    return s.activation(ptv[:, :], psl[:, :],
                                                        AF.Exp,
                                                        bias=MB20[:, 0:1],
                                                        scale=scale)
                                add("scalar", fe_, reads=(pskey, "MB20"),
                                    writes=(ptkey,), tag=f"E{h}_{qb}_{it_i}")
                        vsrc = VH[:, 2048 * slot + 128 * idx:
                                  2048 * slot + 128 * (idx + 1)]
                        vkey = f"VH{slot}"
                    else:
                        tw = idx
                        qoff = 128 * tw
                        def es(it_i=it_i, tw=tw, psl=psl, pskey=pskey,
                               slot=slot, qb=qb, qtile=qtile, qoff=qoff):
                            def fs(te, sem, tw=tw, psl=psl, slot=slot, qb=qb,
                                   qtile=qtile, qoff=qoff):
                                te.matmul(
                                    psl[:, qoff:512],
                                    KOWN[:, T * slot + 512 * qb + qoff:
                                         T * slot + 512 * qb + qoff + 128],
                                    qtile[:, 512 * qb + qoff:512 * (qb + 1)],
                                    start=True, stop=False,
                                    skip_group_check=True)
                                return te.matmul(
                                    psl[:, qoff:qoff + 128], IDB[:, :],
                                    TRIt[:, :], start=False, stop=True,
                                    skip_group_check=True)
                            add("tensor", fs,
                                reads=(f"KOWN{slot}", f"Q{hh}", "IDB", "TRI"),
                                writes=(pskey,), tag=f"Sd{h}_{qb}_{tw}")
                        def ee(it_i=it_i, psl=psl, ptv=ptv, qoff=qoff,
                               pskey=pskey, ptkey=ptkey, tw=tw):
                            def fe_(s, sem, psl=psl, ptv=ptv, qoff=qoff):
                                return s.activation(ptv[:, qoff:512],
                                                    psl[:, qoff:512], AF.Exp,
                                                    bias=MB20[:, 0:1],
                                                    scale=scale)
                            add("scalar", fe_, reads=(pskey, "MB20"),
                                writes=(ptkey,), tag=f"Ed{h}_{qb}_{tw}")
                        vsrc = VOWN[:, T * slot + 512 * qb + qoff:
                                    T * slot + 512 * qb + qoff + 128]
                        vkey = f"VOWN{slot}"
                    pvkey = pk(1)[qb]
                    denkey = pk(2)[qb]
                    def ep(it_i=it_i, vsrc=vsrc, ptv=ptv, qoff=qoff, qb=qb,
                           first=first, ptkey=ptkey, vkey=vkey,
                           pvkey=pvkey, denkey=denkey):
                        def fpv(te, sem, vsrc=vsrc, ptv=ptv, qoff=qoff,
                                qb=qb, first=first):
                            te.matmul(
                                PP[1][:, 512 * qb + qoff:512 * (qb + 1)],
                                vsrc, ptv[:, qoff:512], start=first,
                                stop=False, skip_group_check=True)
                            return te.matmul(
                                PP[2][:, 512 * qb + qoff:512 * (qb + 1)],
                                ONB[:, :], ptv[:, qoff:512], start=first,
                                stop=False, skip_group_check=True)
                        add("tensor", fpv, reads=(ptkey, vkey, "ONB"),
                            writes=(pvkey, denkey), tag=f"PV{h}_{qb}_{it_i}")
                    emitters.append((es, ee, ep))
                emitters[0][0]()
                for i in range(1, len(emitters)):
                    emitters[i - 1][1]()
                    emitters[i][0]()
                    emitters[i - 1][2]()
                emitters[-1][1]()
                emitters[-1][2]()
                def frc(v, sem, qb=qb):
                    return v.reciprocal(RINV[:, :],
                                        PP[2][:, 512 * qb:512 * (qb + 1)])
                add("vector", frc, reads=(pk(2)[qb],), writes=("RINV",),
                    tag=f"rc{h}_{qb}")
                att = QRt(hh)
                def fat(v, sem, qb=qb, att=att):
                    return v.tensor_mul(att[:, 512 * qb:512 * (qb + 1)],
                                        PP[1][:, 512 * qb:512 * (qb + 1)],
                                        RINV[:, :])
                add("vector", fat, reads=(pk(1)[qb], "RINV"),
                    writes=(f"QR{hh}",), tag=f"at{h}_{qb}")

        def out_proj(wap, g, tag):
            for o in range(16):
                sub = o % 4
                base = 4096 * (sub // 2) + 512 * (sub % 2)
                load_w_og(wap, g, o, tag=f"{tag}w{g}_{o}")
                ppi = o % 2
                pp = PP[ppi]
                def fm(te, sem, base=base, pp=pp):
                    r = None
                    for cc in range(4):
                        for t in range(2):
                            r = te.matmul(pp[:, 512 * t:512 * (t + 1)],
                                          WBUF[:, base + 128 * cc:
                                               base + 128 * (cc + 1)],
                                          QRt(cc)[:, 512 * t:512 * (t + 1)],
                                          start=(cc == 0), stop=(cc == 3),
                                          skip_group_check=True)
                    return r
                add("tensor", fm,
                    reads=tuple([f"WBo{sub}"] +
                                [f"QR{cc}" for cc in range(4)]),
                    writes=pk(ppi), tag=f"{tag}mm{g}_{o}")
                def fr(v, sem, o=o, pp=pp):
                    return v.tensor_add(Xc(o), pp[:, 0:T], Xc(o))
                add("vector", fr, reads=pk(ppi) + (f"X{o}",),
                    writes=(f"X{o}",), tag=f"{tag}res{g}_{o}")

        import os as _os
        _chop = int(_os.environ.get("KCHOP", "0"))
        for g in range(4):
            if _chop >= 2 and g > 0:
                break
            for hh in range(4):
                h = 4 * g + hh
                slot = h % 2
                load_w(wqa, h, 2048, slot, tag=f"qaw{h}")
                pp = fm_mms(slot, NCH, LN, lambda c: f"LN{c}", h % 2,
                            f"qamm{h}")
                rope_fm(pp, pk(h % 2), QNt(hh), f"Q{hh}", 128, C2A, S2A,
                        f"qaro{h}")
            if g == 0:
                for hh in range(4):
                    tap(f"q0_{hh}", QNt(hh), (f"Q{hh}",), cast=True)
            if _chop >= 1:
                continue
            for hh in range(4):
                h = 4 * g + hh
                head_loads(h, 0)
                head_loads_own(h)
                attn_head(h, hh, 0)
            if g == 0:
                for hh in range(4):
                    tap(f"att0_{hh}", QRt(hh), (f"QR{hh}",), cast=True)
            out_proj(woa, g, "oa")

    def ffn(tag):
        for tt in range(8):
            ppi = tt % 2
            def fl(te, sem, tt=tt, ppi=ppi):
                r = None
                for c in range(NCH):
                    r = te.matmul(PP[ppi][:, 0:8],
                                  LN(c)[:, 128 * tt:128 * (tt + 1)],
                                  RRs[:, 8 * c:8 * (c + 1)],
                                  start=(c == 0), stop=(c == NCH - 1),
                                  skip_group_check=True)
                return r
            add("tensor", fl,
                reads=tuple([f"LN{c}" for c in range(NCH)] + ["RR"]),
                writes=pk(ppi), tag=f"{tag}lg{tt}")
            sl = 28 * (tt % 2)
            lg = GTMP[:, sl:sl + 8]
            ex = GTMP[:, sl + 8:sl + 16]
            msk = GTMP[:, sl + 16:sl + 24]
            sel = GTMP[:, sl + 16:sl + 24]
            m1 = GTMP[:, sl + 24:sl + 25]
            m2 = GTMP[:, sl + 25:sl + 26]
            nm1 = GTMP[:, sl + 26:sl + 27]
            s2c = GTMP[:, sl + 27:sl + 28]
            gsl = f"GT{tt % 2}"
            def g1(v, sem, tt=tt, ppi=ppi, lg=lg):
                return v.tensor_add(lg, PP[ppi][:, 0:8], RB8[:, :])
            add("vector", g1, reads=pk(ppi) + ("RB8",), writes=(gsl,))
            def g2(v, sem, lg=lg, m1=m1):
                return v.reduce_max(m1, lg, axis=AX.X)
            add("vector", g2, reads=(gsl,), writes=(gsl,))
            def g3(v, sem, m1=m1, nm1=nm1):
                return v.tensor_single_scalar(nm1, m1, -1.0, ALU.mult)
            add("vector", g3, reads=(gsl,), writes=(gsl,))
            def g4(v, sem, lg=lg, m1=m1, msk=msk):
                return v.tensor_single_scalar(msk, lg, m1[:, 0:1],
                                              ALU.is_equal)
            add("vector", g4, reads=(gsl,), writes=("GMSK",))
            def g5(v, sem, lg=lg, msk=msk):
                return v.scalar_tensor_tensor(msk, msk, NEG, lg, ALU.mult,
                                              ALU.add)
            add("vector", g5, reads=("GMSK", gsl), writes=("GMSK",))
            def g6(v, sem, msk=msk, m2=m2):
                return v.reduce_max(m2, msk, axis=AX.X)
            add("vector", g6, reads=("GMSK",), writes=(gsl,))
            def g7(s, sem, lg=lg, nm1=nm1, ex=ex):
                return s.activation(ex, lg, AF.Exp, bias=nm1[:, 0:1])
            add("scalar", g7, reads=(gsl,), writes=("GEX",))
            def g8(v, sem, lg=lg, m2=m2, sel=sel):
                return v.tensor_single_scalar(sel, lg, m2[:, 0:1], ALU.is_ge)
            add("vector", g8, reads=(gsl,), writes=("GSEL",))
            def g9(v, sem, ex=ex, sel=sel):
                return v.tensor_mul(ex, ex, sel)
            add("vector", g9, reads=("GEX", "GSEL"), writes=("GEX",))
            def g10(v, sem, ex=ex, s2c=s2c):
                return v.reduce_sum(s2c, ex, axis=AX.X)
            add("vector", g10, reads=("GEX",), writes=(gsl,))
            def g11(v, sem, s2c=s2c):
                return v.reciprocal(s2c, s2c)
            add("vector", g11, reads=(gsl,), writes=(gsl,))
            def g12(v, sem, ex=ex, s2c=s2c, tt=tt):
                return v.tensor_scalar_mul(GDAT[:, 8 * tt:8 * (tt + 1)], ex,
                                           s2c[:, 0:1])
            add("vector", g12, reads=("GEX", gsl), writes=(f"GD{tt}", "QN0"))
            def g13(te, sem, tt=tt):
                return te.transpose(PP[3][0:8, 0:128],
                                    GDAT[:, 8 * tt:8 * (tt + 1)], IDF[:, :])
            add("tensor", g13, reads=(f"GD{tt}", "QN0", "IDF"), writes=("P3A",),
                tag=f"{tag}gt{tt}")
            def g14(v, sem, tt=tt):
                return v.tensor_copy(GT[0:8, 128 * tt:128 * (tt + 1)],
                                     PP[3][0:8, 0:128])
            add("vector", g14, reads=("P3A",), writes=("SCR_A", "SCR_B"),
                tag=f"{tag}gte{tt}")
        def fmid(te, sem):
            r = None
            for c in range(NCH):
                for t in range(2):
                    r = te.matmul(PP[0][:, 512 * t:512 * (t + 1)],
                                  RAs[:, 128 * c:128 * (c + 1)],
                                  LN(c)[:, 512 * t:512 * (t + 1)],
                                  start=(c == 0), stop=(c == NCH - 1),
                                  skip_group_check=True)
            return r
        add("tensor", fmid,
            reads=tuple([f"LN{c}" for c in range(NCH)] + ["RA"]),
            writes=("P0A", "P0B"), tag=f"{tag}mid")
        for t in range(2):
            def fgm(te, sem, t=t):
                return te.matmul(PP[1][:, 512 * t:512 * (t + 1)], E2[:, :],
                                 GT[0:8, 512 * t:512 * (t + 1)], start=True,
                                 stop=True)
            add("tensor", fgm, reads=("SCR_A", "SCR_B", "E2"),
                writes=(pk(1)[t],), tag=f"{tag}gmm{t}")
        GS = SG(0, 2)
        def fgs(v, sem):
            return v.tensor_copy(GS, PP[1][:, 0:T])
        add("vector", fgs, reads=("P1A", "P1B"), writes=("SCR_A", "SCR_B"),
            tag=f"{tag}gs")
        MIDG = SG(3, 1).bitcast(BF16)
        def fmg(v, sem):
            return v.tensor_mul(MIDG, PP[0][:, 0:T], GS)
        add("vector", fmg, reads=("P0A", "P0B", "SCR_A", "SCR_B"),
            writes=("SCR_D",), tag=f"{tag}midg")
        for o in range(16):
            slot = o % 2
            load_w(basew, o, 2048, slot, tag=f"{tag}bw{o}")
            ppi = o % 2
            pp = PP[ppi]
            def fb(te, sem, slot=slot, pp=pp, o=o):
                for c in range(NCH):
                    for t in range(2):
                        te.matmul(pp[:, 512 * t:512 * (t + 1)], wt(slot, c),
                                  LN(c)[:, 512 * t:512 * (t + 1)],
                                  start=(c == 0), stop=False,
                                  skip_group_check=True)
                r = None
                for t in range(2):
                    r = te.matmul(pp[:, 512 * t:512 * (t + 1)],
                                  RBs[:, 128 * o:128 * (o + 1)],
                                  MIDG[:, 512 * t:512 * (t + 1)],
                                  start=False, stop=True,
                                  skip_group_check=True)
                return r
            add("tensor", fb,
                reads=tuple([f"WB{slot}", "RB_", "SCR_D"] +
                            [f"LN{c}" for c in range(NCH)]),
                writes=pk(ppi), tag=f"{tag}base{o}")
            def fr(v, sem, o=o, pp=pp):
                return v.scalar_tensor_tensor(Xc(o), pp[:, 0:T],
                                              BIASC[:, o:o + 1], Xc(o),
                                              ALU.add, ALU.add)
            add("vector", fr, reads=pk(ppi) + (f"X{o}", "BIASC"),
                writes=(f"X{o}",), tag=f"{tag}res{o}")

    if stages >= 2:
        emit_ln(Xc, lambda c: f"X{c}", LN, lambda c: f"LN{c}", NCH, "ln2")
        ffn("f1")

    if stages >= 2.2:
        CTT = SCR[0:32, 0:T]
        STT = SCR[0:32, T:2 * T]
        for tt in range(8):
            th = FRQ[:, 0:32]
            ld("sync", th, freqs[128 * tt:128 * (tt + 1), :], "QN0",
               tag=f"fq{tt}", dgroup="frq")
            a_ = FRQ[:, 32:64]
            b_ = FRQ[:, 64:96]
            m_ = FRQ[:, 96:128]
            ct = FRQ[:, 128:160]
            st = FRQ[:, 160:192]
            def w1(v, sem, th=th, m_=m_):
                return v.tensor_single_scalar(m_, th, PI, ALU.is_gt)
            add("vector", w1, reads=("QN0",), writes=("QN0",))
            def w2(v, sem, th=th, m_=m_, a_=a_):
                return v.scalar_tensor_tensor(a_, m_, -2 * PI, th, ALU.mult,
                                              ALU.add)
            add("vector", w2, reads=("QN0",), writes=("QN0",))
            def w3(s, sem, a_=a_, st=st):
                return s.activation(st, a_, AF.Sin)
            add("scalar", w3, reads=("QN0",), writes=("QN0",))
            def w4(v, sem, th=th, b_=b_):
                return v.tensor_scalar(b_, th, -1.0, PI / 2, ALU.mult, ALU.add)
            add("vector", w4, reads=("QN0",), writes=("QN0",))
            def w5(v, sem, b_=b_, m_=m_):
                return v.tensor_single_scalar(m_, b_, -PI, ALU.is_lt)
            add("vector", w5, reads=("QN0",), writes=("QN0",))
            def w6(v, sem, b_=b_, m_=m_):
                return v.scalar_tensor_tensor(b_, m_, 2 * PI, b_, ALU.mult,
                                              ALU.add)
            add("vector", w6, reads=("QN0",), writes=("QN0",))
            def w7(s, sem, b_=b_, ct=ct):
                return s.activation(ct, b_, AF.Sin)
            add("scalar", w7, reads=("QN0",), writes=("QN0",))
            def w8(te, sem, ct=ct):
                return te.transpose(PP[3][0:32, 0:128], ct, IDF[:, :])
            add("tensor", w8, reads=("QN0", "IDF"), writes=("P3A",))
            def w9(v, sem, tt=tt):
                return v.tensor_copy(CTT[:, 128 * tt:128 * (tt + 1)],
                                     PP[3][0:32, 0:128])
            add("vector", w9, reads=("P3A",), writes=("SCR_A", "SCR_B"),
                tag=f"ctt{tt}")
            def wA(te, sem, st=st):
                return te.transpose(PP[3][0:32, 0:128], st, IDF[:, :])
            add("tensor", wA, reads=("QN0", "IDF"), writes=("P3A",))
            def wB(v, sem, tt=tt):
                return v.tensor_copy(STT[:, 128 * tt:128 * (tt + 1)],
                                     PP[3][0:32, 0:128])
            add("vector", wB, reads=("P3A",), writes=("SCR_C", "SCR_D"),
                tag=f"stt{tt}")
        for t in range(2):
            def x1(te, sem, t=t):
                return te.matmul(PP[2][:, 512 * t:512 * (t + 1)], EC[:, :],
                                 CTT[:, 512 * t:512 * (t + 1)], start=True,
                                 stop=True)
            add("tensor", x1, reads=("SCR_A", "SCR_B", "EC"),
                writes=(pk(2)[t],))
            def x2(v, sem, t=t):
                return v.tensor_copy(TBL[:, 512 * t:512 * (t + 1)],
                                     PP[2][:, 512 * t:512 * (t + 1)])
            add("vector", x2, reads=(pk(2)[t],), writes=("TBL",),
                tag=f"c2b{t}")
            def x3(te, sem, t=t):
                return te.matmul(PP[3][:, 512 * t:512 * (t + 1)], ES[:, :],
                                 STT[:, 512 * t:512 * (t + 1)], start=True,
                                 stop=True)
            add("tensor", x3, reads=("SCR_C", "SCR_D", "ES"),
                writes=(pk(3)[t],))
            def x4(v, sem, t=t):
                return v.tensor_copy(TBL[:, T + 512 * t:T + 512 * (t + 1)],
                                     PP[3][:, 512 * t:512 * (t + 1)])
            add("vector", x4, reads=(pk(3)[t],), writes=("TBL",),
                tag=f"s2b{t}")

        emit_ln(Xc, lambda c: f"X{c}", LN, lambda c: f"LN{c}", NCH, "ln3")
        C2B = TBL[:, 0:T]
        S2B = TBL[:, T:2 * T]

    if stages >= 2.4:
        for o in range(4):
            slot = o % 2
            load_w(wkvac, o, 2048, slot, tag=f"kvaw{o}")
            pp = fm_mms(slot, NCH, LN, lambda c: f"LN{c}", o % 2, f"kvamm{o}")
            def fe_(s, sem, o=o, pp=pp):
                return s.activation(KVCB(o), pp[:, 0:T], AF.Copy)
            add("scalar", fe_, reads=pk(o % 2), writes=(f"QN{o}",),
                tag=f"kvaev{o}")
        load_w(wkvar, 0, 1024, 0, tag="kvrw")
        def fkr(te, sem):
            r = None
            for c in range(NCH):
                for t in range(2):
                    r = te.matmul(PP[0][0:64, 512 * t:512 * (t + 1)],
                                  wt(0, c, 64),
                                  LN(c)[:, 512 * t:512 * (t + 1)],
                                  start=(c == 0), stop=(c == NCH - 1),
                                  skip_group_check=True)
            return r
        add("tensor", fkr,
            reads=tuple(["WB0"] + [f"LN{c}" for c in range(NCH)]),
            writes=("P0A", "P0B"), tag="kvrmm")
        kst = STG[:, 0:T]
        rope_fm(PP[0], ("P0A", "P0B"), kst, "STG0", 64, C2B, S2B, "kvro")
        ld("sync", kb_in.ap()[2048:2112, :], kst[0:64, :], "DRAM_kbr",
           reads=("STG0",), tag="kbrdma", dgroup="kb0")
        emit_ln(lambda i: KVCB(i), lambda i: f"QN{i}",
                lambda i: KVCN(i), lambda i: f"QN{4 + i}", 4, "lnkv",
                gb=True)
        pass
    if stages >= 2.6:
        for o in range(16):
            slot = o % 2
            load_w(wkvbk, o, 512, slot, tag=f"kbkw{o}")
            pp = PP[o % 2]
            def fm(te, sem, slot=slot, pp=pp):
                r = None
                for c in range(4):
                    for t in range(2):
                        r = te.matmul(pp[:, 512 * t:512 * (t + 1)],
                                      wt(slot, c),
                                      KVCN(c)[:, 512 * t:512 * (t + 1)],
                                      start=(c == 0), stop=(c == 3),
                                      skip_group_check=True)
                return r
            add("tensor", fm,
                reads=tuple([f"WB{slot}"] + [f"QN{4 + c}" for c in range(4)]),
                writes=pk(o % 2), tag=f"kbkmm{o}")
            st_ = STG[:, T * (o % 2):T * (o % 2 + 1)]
            def fe_(s, sem, pp=pp, st_=st_):
                return s.activation(st_[:, :], pp[:, 0:T], AF.Copy)
            add("scalar", fe_, reads=pk(o % 2), writes=(f"STG{o % 2}",),
                tag=f"kbkev{o}")
            ld("sync", kb_in.ap()[128 * o:128 * (o + 1), :], st_[:, :],
               f"DRAM_kbk{o}", reads=(f"STG{o % 2}",), tag=f"kbkdma{o}",
               dgroup=f"kb{o % 2}")
        for o in range(8):
            slot = o % 2
            load_w(wkvbv, o, 1024, slot, tag=f"kbvw{o}")
            for tt in range(8):
                ppi = tt % 2
                pp = PP[ppi]
                def fv(te, sem, slot=slot, tt=tt, pp=pp):
                    r = None
                    for c in range(4):
                        r = te.matmul(pp[:, 0:256],
                                      KVCN(c)[:, 128 * tt:128 * (tt + 1)],
                                      wt(slot, c, 256), start=(c == 0),
                                      stop=(c == 3), skip_group_check=True)
                    return r
                add("tensor", fv,
                    reads=tuple([f"WB{slot}"] +
                                [f"QN{4 + c}" for c in range(4)]),
                    writes=pk(ppi), tag=f"kbvmm{o}_{tt}")
                vs = STG[:, T * (tt % 2) + 256:T * (tt % 2) + 512]
                def fe_(v, sem, pp=pp, vs=vs):
                    return v.tensor_copy(vs[:, 0:256], pp[:, 0:256])
                add("vector", fe_, reads=pk(ppi), writes=(f"STG{tt % 2}",),
                    tag=f"kbvev{o}_{tt}")
                ld("sync", vb_in.ap()[128 * tt:128 * (tt + 1),
                                      256 * o:256 * (o + 1)],
                   vs[:, 0:256], f"DRAM_vb{o}_{tt}", reads=(f"STG{tt % 2}",),
                   tag=f"kbvdma{o}_{tt}", dgroup=f"vb{tt % 2}")
        if stages >= 2.55:
            def cc_kb(g, sem):
                return g.collective_compute(
                    "AllGather", ALU.bypass, replica_groups=CCG,
                    ins=[kb_in.ap().opt()], outs=[kb_out.ap().opt()])
            add("gpsimd", cc_kb,
                reads=tuple([f"DRAM_kbk{o}" for o in range(16)] +
                            ["DRAM_kbr"]),
                writes=("DRAM_kbout",), coll=True, tag="cc_kb")
        if stages >= 2.56:
            def cc_vb(g, sem):
                return g.collective_compute(
                    "AllGather", ALU.bypass, replica_groups=CCG,
                    ins=[vb_in.ap().opt()], outs=[vb_out.ap().opt()])
            add("gpsimd", cc_vb,
                reads=tuple([f"DRAM_vb{o}_{t}" for o in range(8)
                             for t in range(8)]),
                writes=("DRAM_vbout",), coll=True, tag="cc_vb")
        for dst_base in ((0, 64) if stages >= 2.58 else ()):
            ld("sync", KRB[dst_base:dst_base + 64, 0:1024],
               kb_out.ap()[2048:2112, :], ("KRB", "KOWN0", "KOWN1"),
               reads=("DRAM_kbout",), tag=f"krb{dst_base}a", dgroup="krb")
            ld("sync", KRB[dst_base:dst_base + 64, 1024:2048],
               kb_out.ap()[4160:4224, :], ("KRB", "KOWN0", "KOWN1"),
               reads=("DRAM_kbout",), tag=f"krb{dst_base}b", dgroup="krb")

        pass
    if stages >= 2.8:
        for g in range(4):
            if stages < 3 and g > 0:
                break
            for hh in range(4):
                h = 4 * g + hh
                slot = h % 2
                load_w(wqb, h, 2048, slot, tag=f"qbnw{h}")
                pp = fm_mms(slot, NCH, LN, lambda c: f"LN{c}", h % 2,
                            f"qbnmm{h}")
                def fe_(s, sem, hh=hh, pp=pp):
                    return s.activation(QNt(hh), pp[:, 0:T], AF.Copy)
                add("scalar", fe_, reads=pk(h % 2), writes=(f"Q{hh}",),
                    tag=f"qbnev{h}")
            for jj in range(2):
                j = 2 * g + jj
                slot = j % 2
                load_w(wqb, 16 + j, 2048, slot, tag=f"qbrw{j}")
                pp = fm_mms(slot, NCH, LN, lambda c: f"LN{c}", j % 2,
                            f"qbrmm{j}")
                rope_fm(pp, pk(j % 2), QNt(4 + jj), "QROPE", 128, C2B, S2B,
                        f"qbro{j}")
            if stages < 2.9:
                continue
            for hh in range(4):
                h = 4 * g + hh
                head_loads(h, 1)
                attn_head(h, hh, 1,
                          qrope_src=(QNt(4 + hh // 2), 64 * (hh % 2)))
            if stages < 3:
                continue
            out_proj(wob, g, "ob")

    if stages >= 4:
        emit_ln(Xc, lambda c: f"X{c}", LN, lambda c: f"LN{c}", NCH, "ln4")
        ffn("f2")

    for c in range(NCH):
        ld("sync", y_out[128 * c:128 * (c + 1), :], Xc(c), f"Y{c}",
           reads=(f"X{c}",), tag=f"st{c}", dgroup="y")

    sc.finalize(nc, stack)
    for en in ENGINES:
        getattr(blk, en)(lambda eng, en=en: sc.emit_engine(eng, en))
    stack.close()
    return nc


def kernel(L0_attn_a_x, L0_attn_a_Wq, L0_attn_a_Wk, L0_attn_a_Wv, L0_attn_a_Wo,
           L0_attn_b_freqs_cis, L0_attn_b_Wq, L0_attn_b_Wkv_a,
           L0_attn_b_kv_gamma, L0_attn_b_kv_beta, L0_attn_b_Wkv_b,
           L0_attn_b_Wout, L0_ffn_base_w, L0_ffn_base_b, L0_ffn_router_w,
           L0_ffn_router_b, L0_ffn_lora_A, L0_ffn_lora_B, _stages=4,
           _taps=()):
    f32 = np.float32
    x = np.asarray(L0_attn_a_x, dtype=f32)

    def tile_w_og(W):
        A = W.reshape(16, 128, 16, 128)
        Bm = A.reshape(4, 4, 128, 16, 128).transpose(0, 3, 2, 1, 4)
        return np.ascontiguousarray(Bm.reshape(4, 16, 128, 512))

    wqa_h = tile_w(np.asarray(L0_attn_a_Wq, f32).T, 128)
    wka_h = tile_w(np.asarray(L0_attn_a_Wk, f32).T, 128)
    wva_h = tile_w(np.asarray(L0_attn_a_Wv, f32).T, 256)
    woa_h = tile_w_og(np.asarray(L0_attn_a_Wo, f32).T)
    basew_h = tile_w(np.asarray(L0_ffn_base_w, f32).T, 128)
    biasT_h = np.ascontiguousarray(np.asarray(L0_ffn_base_b, f32).reshape(16, 128).T)
    rb8_h = np.ascontiguousarray(
        np.broadcast_to(np.asarray(L0_ffn_router_b, f32), (128, 8)))
    routr_h = np.ascontiguousarray(
        np.asarray(L0_ffn_router_w, f32).T.reshape(16, 128, 8)
        .transpose(1, 0, 2).reshape(128, 128))
    aall_h = np.ascontiguousarray(
        np.asarray(L0_ffn_lora_A, f32).transpose(1, 0, 2).reshape(D, 128)
        .reshape(16, 128, 128).transpose(1, 0, 2).reshape(128, 2048))
    ball_h = np.ascontiguousarray(
        np.asarray(L0_ffn_lora_B, f32).reshape(128, D))
    wqb_r = np.asarray(L0_attn_b_Wq, f32)
    cols = []
    for h in range(16):
        cols.extend(range(192 * h, 192 * h + 128))
    for h in range(16):
        cols.extend(range(192 * h + 128, 192 * h + 192))
    wqb_h = tile_w(np.ascontiguousarray(wqb_r[:, cols]), 128)
    wkva_r = np.asarray(L0_attn_b_Wkv_a, f32)
    wkvac_h = tile_w(np.ascontiguousarray(wkva_r[:, 0:512]), 128)
    wkvar_h = tile_w(np.ascontiguousarray(wkva_r[:, 512:576]), 64)
    kvgb_h = np.zeros((128, 8), f32)
    kvgb_h[:, 0:4] = np.asarray(L0_attn_b_kv_gamma, f32).reshape(4, 128).T
    kvgb_h[:, 4:8] = np.asarray(L0_attn_b_kv_beta, f32).reshape(4, 128).T
    wkvb_r = np.asarray(L0_attn_b_Wkv_b, f32)
    kcols, vcols = [], []
    for h in range(16):
        kcols.extend(range(256 * h, 256 * h + 128))
        vcols.extend(range(256 * h + 128, 256 * h + 256))
    wkvbk_h = tile_w(np.ascontiguousarray(wkvb_r[:, kcols]), 128)
    wkvbv_h = tile_w(np.ascontiguousarray(wkvb_r[:, vcols]), 256)
    wob_h = tile_w_og(np.asarray(L0_attn_b_Wout, f32))
    fcis = np.asarray(L0_attn_b_freqs_cis, f32)

    nc = build_nc(_stages, taps=_taps)
    in_maps = []
    for core in range(8):
        b = core // 2
        par = core % 2
        pos = _positions(par)
        xc = np.ascontiguousarray(x[b][pos, :].T)
        c2a_h, s2a_h = _rope_tables_a(par)
        in_maps.append({
            "x_fm": xc, "wqa": wqa_h, "wka": wka_h, "wva": wva_h,
            "woa": woa_h, "basew": basew_h, "biasT": biasT_h, "rb8": rb8_h,
            "routr": routr_h, "aall": aall_h, "ball": ball_h, "wqb": wqb_h,
            "wkvac": wkvac_h, "wkvar": wkvar_h, "kvgb": kvgb_h,
            "wkvbk": wkvbk_h, "wkvbv": wkvbv_h, "wob": wob_h,
            "freqs": np.ascontiguousarray(fcis[pos, :]),
            "c2a": c2a_h, "s2a": s2a_h, "abias": _abias(par),
        })
    res = run_bass_kernel_spmd(nc, in_maps, core_ids=list(range(8)))
    y = np.empty((B, S, D), dtype=np.float32)
    for core in range(8):
        b = core // 2
        par = core % 2
        pos = _positions(par)
        y[b][pos, :] = res.results[core]["y_fm"].T
    if _taps:
        tapd = [{k[4:]: v for k, v in res.results[c].items()
                 if k.startswith("tap_")} for c in range(8)]
        return y, tapd
    return y


# version 17
# speedup vs baseline: 1.1925x; 1.0408x over previous
import numpy as np
import ml_dtypes
import concourse.bass as bass
import concourse.mybir as mybir
from concourse.bass_utils import run_bass_kernel_spmd

F32 = mybir.dt.float32
BF16 = mybir.dt.bfloat16
AF = mybir.ActivationFunctionType
ALU = mybir.AluOpType
AX = mybir.AxisListType
BF = ml_dtypes.bfloat16

B, S, D, H = 4, 2048, 2048, 16
T = 1024
NCH = 16
PERM = [0, 3, 1, 2]
OBK = ([0, 3], [1, 2])
SCALE_A = float(1.0 / np.sqrt(128.0))
SCALE_B = float(1.0 / np.sqrt(192.0))
MSUB = 20.0
NEG = -30000.0
EPS = 1e-5
THETA = 10000.0
PI = float(np.pi)
ITEMS_QB = ([0, 1, 2, 3], [0, 1, 2, 3, 8, 9, 10, 11, 12, 13, 14, 15])
ENGINES = ("tensor", "scalar", "vector", "gpsimd", "sync")


STRICT_SAME_ENGINE = True

SLOT_GROUPS = ("wb0", "wb1", "wbo0", "wbo1", "wbo2", "wbo3",
               "wbv0", "wbv1", "wbv2", "wbv3",
               "kh0", "kh1", "vh0", "vh1", "ko0", "ko1",
               "vo0", "vo1", "frq", "ka0", "ka1", "va0", "va1",
               "kb0", "kb1", "vb0", "vb1")


class Sched:
    def __init__(self):
        self.steps = []
        self.last_w = {}
        self.readers = {}

    def add(self, engine, emit, reads=(), writes=(), n_dma=0, coll=False,
            tag="", dgroup=None):
        deps = set()
        for b in reads:
            w = self.last_w.get(b)
            if w is not None:
                deps.add(w)
        for b in writes:
            w = self.last_w.get(b)
            if w is not None:
                deps.add(w)
            for r in self.readers.get(b, ()):
                deps.add(r)
        i = len(self.steps)
        if n_dma:
            assert dgroup is not None, f"dma step {tag} needs dgroup"
        self.steps.append(dict(engine=engine, emit=emit, deps=deps,
                               n_dma=n_dma, coll=coll, idx=i, tag=tag,
                               dgroup=dgroup, pub=False, sem=None, cnt=None,
                               inc=0))
        for b in reads:
            self.readers.setdefault(b, []).append(i)
        for b in writes:
            self.last_w[b] = i
            self.readers[b] = []
        return i

    def finalize(self, nc, stack):
        self.semc = {en: stack.enter_context(nc.semaphore(f"sc_{en}"))
                     for en in ("tensor", "scalar", "vector", "gpsimd")}
        self.semcc = stack.enter_context(nc.semaphore("scc"))
        groups = []
        for s in self.steps:
            if s["n_dma"] and s["dgroup"] not in groups:
                groups.append(s["dgroup"])
        self.semd = {g: stack.enter_context(nc.semaphore(f"sg_{g}"))
                     for g in groups}
        for s in self.steps:
            for d in s["deps"]:
                p = self.steps[d]
                if (p["n_dma"] or p["coll"] or p["engine"] != s["engine"]
                        or STRICT_SAME_ENGINE):
                    p["pub"] = True
        ctr = {}
        self.gtotal = {}
        self.glast = {}
        for s in self.steps:
            en = s["engine"]
            if s["coll"]:
                ctr["cc"] = ctr.get("cc", 0) + 1
                s["sem"], s["cnt"], s["inc"] = self.semcc, ctr["cc"], 1
            elif s["n_dma"]:
                g = s["dgroup"]
                ctr[g] = ctr.get(g, 0) + 16 * s["n_dma"]
                s["sem"], s["cnt"], s["inc"] = self.semd[g], ctr[g], 16
                self.gtotal[g] = ctr[g]
                self.glast[g] = s["idx"]
            elif s["pub"]:
                k = ("c", en)
                ctr[k] = ctr.get(k, 0) + 1
                s["sem"], s["cnt"], s["inc"] = self.semc[en], ctr[k], 1
        for s in self.steps:
            for d in s["deps"]:
                p = self.steps[d]
                if p["n_dma"] and p["dgroup"] not in SLOT_GROUPS:
                    assert s["idx"] > self.glast[p["dgroup"]] or \
                        s["n_dma"] and s["dgroup"] == p["dgroup"], (
                        f"batch dgroup {p['dgroup']} consumed at step "
                        f"{s['idx']}({s['tag']}) before group closed at "
                        f"{self.glast[p['dgroup']]}")
        self._fd = {}
        for s in self.steps:
            if s["n_dma"]:
                self._fd.setdefault(s["engine"], {})[s["dgroup"]] = \
                    self.gtotal[s["dgroup"]]

    def emit_engine(self, eng, en):
        hw = {}
        for s in self.steps:
            if s["engine"] != en:
                continue
            need = {}
            for d in sorted(s["deps"]):
                p = self.steps[d]
                if p["sem"] is None:
                    continue
                if (not STRICT_SAME_ENGINE and not p["n_dma"]
                        and not p["coll"] and p["engine"] == en):
                    continue
                if (p["n_dma"] and s["n_dma"]
                        and s["dgroup"] == p["dgroup"]):
                    continue
                cnt = p["cnt"]
                if p["n_dma"] and p["dgroup"] not in SLOT_GROUPS:
                    cnt = self.gtotal[p["dgroup"]]
                key = id(p["sem"])
                if need.get(key, (None, -1))[1] < cnt:
                    need[key] = (p["sem"], cnt)
            for sem, cnt in need.values():
                if hw.get(id(sem), 0) < cnt:
                    eng.wait_ge(sem, cnt)
                    hw[id(sem)] = cnt
            out = s["emit"](eng, s["sem"])
            if s["sem"] is not None and not s["n_dma"] and not s["coll"]:
                last = out[-1] if isinstance(out, (list, tuple)) else out
                last.then_inc(s["sem"], s["inc"])
            elif s["coll"]:
                out.then_inc(s["sem"], 1)
        for g, tot in self._fd.get(en, {}).items():
            eng.wait_ge(self.semd[g], tot)


def tile_w(W, ob):
    Din, O = W.shape
    return np.ascontiguousarray(
        W.reshape(Din // 128, 128, O // ob, ob).transpose(2, 1, 0, 3))


def _positions(parity):
    b0, b1 = OBK[parity]
    return np.concatenate([512 * b0 + np.arange(512), 512 * b1 + np.arange(512)])


def _rope_tables_a(parity):
    pos = _positions(parity).astype(np.float64)
    idx = np.arange(0, 128, 2, dtype=np.float64)
    inv = THETA ** (-idx / 128.0)
    ang = inv[:, None] * pos[None, :]
    c = np.repeat(np.cos(ang), 2, axis=0)
    s = np.repeat(np.sin(ang), 2, axis=0)
    s[0::2] *= -1.0
    return c.astype(np.float32), s.astype(np.float32)


def _abias(parity):
    ab = np.zeros((128, 16), dtype=np.float32)
    col = 0
    for qb in range(2):
        gqb = OBK[parity][qb]
        for kt in ITEMS_QB[qb]:
            kb = PERM[kt // 4]
            ab[:, col] = -MSUB if kb < gqb else NEG
            col += 1
    return ab


def build_nc(stages=4, taps=()):
    from contextlib import ExitStack
    nc = bass.Bass("TRN2", target_bir_lowering=False)
    sc = Sched()
    stack = ExitStack()
    e = stack.enter_context

    def din(name, shape):
        return nc.dram_tensor(name, list(shape), F32, kind="ExternalInput").ap()

    x_in = din("x_fm", (D, T))
    y_out = nc.dram_tensor("y_fm", [D, T], F32, kind="ExternalOutput").ap()
    wqa = din("wqa", (16, 128, 16, 128))
    wka = din("wka", (16, 128, 16, 128))
    wva = din("wva", (8, 128, 16, 256))
    woa = din("woa", (4, 16, 128, 512))
    basew = din("basew", (16, 128, 16, 128))
    biasT_i = din("biasT", (128, 16))
    rb8_i = din("rb8", (128, 8))
    routr = din("routr", (128, 128))
    aall = din("aall", (128, 2048))
    ball = din("ball", (128, 2048))
    wqb = din("wqb", (24, 128, 16, 128))
    wkvac = din("wkvac", (4, 128, 16, 128))
    wkvar = din("wkvar", (1, 128, 16, 64))
    kvgb_i = din("kvgb", (128, 8))
    wkvbk = din("wkvbk", (16, 128, 4, 128))
    wkvbv = din("wkvbv", (8, 128, 4, 256))
    wob = din("wob", (4, 16, 128, 512))
    freqs = din("freqs", (T, 32))
    c2a_i = din("c2a", (128, T))
    s2a_i = din("s2a", (128, T))
    abias_i = din("abias", (128, 16))

    ka_in = nc.dram_tensor("ka_in", [D, T], BF16, kind="Internal")
    ka_out = nc.dram_tensor("ka_out", [2 * D, T], BF16, kind="Internal")
    va_in = nc.dram_tensor("va_in", [T, D], BF16, kind="Internal")
    va_out = nc.dram_tensor("va_out", [2 * T, D], BF16, kind="Internal")
    kb_in = nc.dram_tensor("kb_in", [2112, T], BF16, kind="Internal")
    kb_out = nc.dram_tensor("kb_out", [4224, T], BF16, kind="Internal")
    vb_in = nc.dram_tensor("vb_in", [T, D], BF16, kind="Internal")
    vb_out = nc.dram_tensor("vb_out", [2 * T, D], BF16, kind="Internal")
    CCG = [[0, 1], [2, 3], [4, 5], [6, 7]]

    ident_f = nc.inline_tensor(np.eye(128, dtype=np.float32), name="idf").ap()
    psw_np = np.zeros((128, 128), dtype=BF)
    for i in range(128):
        psw_np[i, i ^ 1] = 1
    pswap_c = nc.inline_tensor(psw_np, name="psw").ap()
    tri_np = np.where(np.triu(np.ones((128, 128))) > 0, 0.0, NEG).astype(BF)
    tri_c = nc.inline_tensor(tri_np, name="tri").ap()
    idb_c = nc.inline_tensor(np.eye(128, dtype=BF), name="idb").ap()
    onc_c = nc.inline_tensor(np.ones((128, 1), dtype=BF), name="onc").ap()
    onb_c = nc.inline_tensor(np.ones((128, 128), dtype=BF), name="onb").ap()
    onrf_c = nc.inline_tensor(np.ones((1, 128), dtype=np.float32),
                              name="onrf").ap()
    e2_np = np.zeros((8, 128), dtype=np.float32)
    for ex in range(8):
        e2_np[ex, 16 * ex:16 * ex + 16] = 2.0
    e2_c = nc.inline_tensor(e2_np, name="e2").ap()
    ec_np = np.zeros((32, 128), dtype=np.float32)
    es_np = np.zeros((32, 128), dtype=np.float32)
    for r in range(128):
        fi = (r % 64) // 2
        ec_np[fi, r] = 1.0
        es_np[fi, r] = -1.0 if r % 2 == 0 else 1.0
    ec_c = nc.inline_tensor(ec_np, name="ec").ap()
    mb20_c = nc.inline_tensor(np.full((128, 1), -MSUB, dtype=np.float32),
                              name="mb20").ap()
    epsc_c = nc.inline_tensor(np.full((128, 1), EPS, dtype=np.float32),
                              name="epsc").ap()
    es_c = nc.inline_tensor(es_np, name="es").ap()

    XR = e(nc.sbuf_tensor("XR", [128, NCH * T], F32))
    LNR = e(nc.sbuf_tensor("LNR", [128, NCH * T], BF16))
    QN = e(nc.sbuf_tensor("QN", [128, 8 * T], BF16))
    QR = e(nc.sbuf_tensor("QR", [128, 4 * T], BF16))
    KH = e(nc.sbuf_tensor("KH", [128, 2 * 2048], BF16))
    VH = e(nc.sbuf_tensor("VH", [128, 2 * 2048], BF16))
    KOWN = e(nc.sbuf_tensor("KOWN", [128, 2 * T], BF16))
    VOWN = e(nc.sbuf_tensor("VOWN", [128, 2 * T], BF16))
    KRB = KOWN
    WBUF = e(nc.sbuf_tensor("WBUF", [128, 2 * 4096], BF16))
    TBL = e(nc.sbuf_tensor("TBL", [128, 2 * T], F32))
    SCR = e(nc.sbuf_tensor("SCR", [128, 2 * T], F32))
    ROWB = e(nc.sbuf_tensor("ROWB", [1, 2 * T], F32))
    PTS = e(nc.sbuf_tensor("PTS", [128, 2 * 512], BF16))
    STG = e(nc.sbuf_tensor("STG", [128, 2 * T], BF16))
    RINV = e(nc.sbuf_tensor("RINV", [128, 512], F32))
    RAs = e(nc.sbuf_tensor("RAs", [128, 2048], BF16))
    RBs = e(nc.sbuf_tensor("RBs", [128, 2048], BF16))
    RRs = e(nc.sbuf_tensor("RRs", [128, 128], BF16))
    RB8 = e(nc.sbuf_tensor("RB8", [128, 8], F32))
    KVGB = e(nc.sbuf_tensor("KVGB", [128, 8], F32))
    ABIAS = e(nc.sbuf_tensor("ABIAS", [128, 16], F32))
    IDF = e(nc.sbuf_tensor("IDF", [128, 128], F32))
    IDB = e(nc.sbuf_tensor("IDB", [128, 128], BF16))
    PSW = e(nc.sbuf_tensor("PSW", [128, 128], BF16))
    TRIt = e(nc.sbuf_tensor("TRIt", [128, 128], BF16))
    ONC = e(nc.sbuf_tensor("ONC", [128, 1], BF16))
    ONB = e(nc.sbuf_tensor("ONB", [128, 128], BF16))
    ONRF = e(nc.sbuf_tensor("ONRF", [1, 128], F32))
    MB20 = e(nc.sbuf_tensor("MB20", [128, 1], F32))
    EPSC = e(nc.sbuf_tensor("EPSC", [128, 1], F32))
    BIASC = e(nc.sbuf_tensor("BIASC", [128, 16], F32))
    E2 = e(nc.sbuf_tensor("E2", [8, 128], F32))
    EC = e(nc.sbuf_tensor("EC", [32, 128], F32))
    ES = e(nc.sbuf_tensor("ES", [32, 128], F32))
    PP = [e(nc.psum_tensor(f"PP{i}", [128, 1024], F32)) for i in range(4)]

    blk = e(nc.Block())
    add = sc.add

    def Xc(c):
        return XR[:, T * c:T * (c + 1)]

    def LN(c):
        return LNR[:, T * c:T * (c + 1)]

    def QNt(i):
        return QN[:, T * i:T * (i + 1)]

    def QRt(i):
        return QR[:, T * i:T * (i + 1)]

    def KVCB(i):
        return QNt(i)

    def KVCN(i):
        return QNt(4 + i)

    MEAN = ROWB[0:1, 0:T]
    AUX = ROWB[0:1, T:2 * T]
    DENR = ROWB[0:1, 0:512]
    FRQ = QN[:, 0:384].bitcast(F32)
    GT = SCR[0:8, 0:T]
    GDAT = QN[:, 384:512].bitcast(F32)
    GTMP = QN[:, 512:624].bitcast(F32)
    def SG(i, n=1):
        return SCR[:, 512 * i:512 * (i + n)]

    PKEY = [("P0A", "P0B"), ("P1A", "P1B"), ("P2A", "P2B"), ("P3A", "P3B")]

    def pk(i):
        return PKEY[i]

    def ld(en, dst, src, wbuf, reads=(), tag="ld", dgroup="init"):
        def f(g, sem, dst=dst, src=src):
            return g.dma_start(out=dst, in_=src).then_inc(sem, 16)
        return add(en, f, reads=reads, writes=(wbuf,) if isinstance(wbuf, str)
                   else tuple(wbuf), n_dma=1, tag=tag, dgroup=dgroup)

    for dst, src, b in ((IDF[:, :], ident_f, "IDF"), (IDB[:, :], idb_c, "IDB"),
                        (PSW[:, :], pswap_c, "PSW"), (TRIt[:, :], tri_c, "TRI"),
                        (ONC[:, :], onc_c, "ONC"), (ONRF[:, :], onrf_c, "ONRF"),
                        (ONB[:, :], onb_c, "ONB"),
                        (MB20[:, :], mb20_c, "MB20"),
                        (EPSC[:, :], epsc_c, "EPSC"),
                        (E2[:, :], e2_c, "E2"),
                        (EC[:, :], ec_c, "EC"), (ES[:, :], es_c, "ES"),
                        (ABIAS[:, :], abias_i, "ABIAS"),
                        (RB8[:, :], rb8_i, "RB8"), (KVGB[:, :], kvgb_i, "KVGB"),
                        (TBL[:, 0:T], c2a_i, "TBL"),
                        (TBL[:, T:2 * T], s2a_i, "TBLS")):
        ld("sync", dst, src, b)
    for c in range(NCH):
        ld("sync", Xc(c), x_in[128 * c:128 * (c + 1), :], f"X{c}", dgroup="x")
    ld("sync", BIASC[:, :], biasT_i, "BIASC", dgroup="init")
    ld("gpsimd", RAs[:, :], aall, "RA", dgroup="ginit")
    ld("gpsimd", RBs[:, :], ball, "RB_", dgroup="ginit")
    ld("gpsimd", RRs[:, :], routr, "RR", dgroup="ginit")

    tap_outs = {}

    def tap(name, ap, keys, cast=False):
        if name not in taps:
            return
        shp = list(ap.shape)
        o = nc.dram_tensor(f"tap_{name}", shp, F32, kind="ExternalOutput").ap()
        tap_outs[name] = True
        ld("gpsimd" if cast else "sync", o, ap, f"TAPO_{name}",
           reads=tuple(keys), tag=f"tap{name}", dgroup="tap")

    def load_w(wap, o, cols, slot, csel=None, tag="w"):
        if csel is None:
            src = wap[o].rearrange("p c j -> p (c j)")
        else:
            src = wap[o, :, csel[0]:csel[1], :].rearrange("p c j -> p (c j)")
        dst = WBUF[:, 4096 * slot:4096 * slot + cols]
        return ld("gpsimd", dst, src,
                  (f"WB{slot}", f"WBo{2 * slot}", f"WBo{2 * slot + 1}",
                   f"WBv{2 * slot}", f"WBv{2 * slot + 1}"),
                  tag=tag, dgroup=f"wb{slot}")

    def wt(slot, c, ob=128):
        return WBUF[:, 4096 * slot + ob * c:4096 * slot + ob * (c + 1)]

    def load_w_og(wap, g, o, tag="w"):
        sub = o % 4
        base = 4096 * (sub // 2) + 512 * (sub % 2)
        dst = WBUF[:, base:base + 512]
        return ld("gpsimd", dst, wap[g, o],
                  (f"WBo{sub}", f"WB{sub // 2}"), tag=tag,
                  dgroup=f"wbo{sub}")

    def emit_ln(src, skey, dst, dkey, nch, tag, gb=None):
        dsum = float(128 * nch)
        for c in range(nch):
            def f(s, sem, c=c):
                return s.activation(dst(c), src(c), AF.Copy)
            add("scalar", f, reads=(skey(c),), writes=(dkey(c),),
                tag=f"{tag}cast{c}")
        for c in range(nch):
            sqb = SG(c % 2, 1).bitcast(BF16)
            def fsq(s, sem, c=c, sqb=sqb):
                return s.activation(sqb, dst(c), AF.Square)
            add("scalar", fsq, reads=(dkey(c),),
                writes=(f"SCR_{'A' if c % 2 == 0 else 'B'}",),
                tag=f"{tag}sq{c}")
            for t in range(2):
                def fmu(te, sem, c=c, t=t):
                    return te.matmul(PP[2][:, 512 * t:512 * (t + 1)],
                                     ONB[:, :],
                                     dst(c)[:, 512 * t:512 * (t + 1)],
                                     start=(c == 0), stop=(c == nch - 1),
                                     skip_group_check=True)
                add("tensor", fmu, reads=(dkey(c), "ONB"),
                    writes=(pk(2)[t],), tag=f"{tag}mu{c}_{t}")
                def fs2(te, sem, c=c, t=t, sqb=sqb):
                    return te.matmul(PP[3][:, 512 * t:512 * (t + 1)],
                                     ONB[:, :], sqb[:, 512 * t:512 * (t + 1)],
                                     start=(c == 0), stop=(c == nch - 1),
                                     skip_group_check=True)
                add("tensor", fs2,
                    reads=(f"SCR_{'A' if c % 2 == 0 else 'B'}", "ONB"),
                    writes=(pk(3)[t],), tag=f"{tag}sm{c}_{t}")
        MB = SG(0, 2)
        TMPB = SG(2, 2)
        def v1(v, sem):
            return v.tensor_single_scalar(MB, PP[3][:, 0:T], 1.0 / dsum,
                                          ALU.mult)
        add("vector", v1, reads=("P3A", "P3B"), writes=("SCR_A", "SCR_B"))
        def v2(v, sem):
            return v.tensor_single_scalar(PP[3][:, 0:T], PP[2][:, 0:T],
                                          1.0 / dsum, ALU.mult)
        add("vector", v2, reads=("P2A", "P2B", "P3A", "P3B"),
            writes=("P3A", "P3B"))
        def s2(s, sem):
            return s.activation(TMPB, PP[3][:, 0:T], AF.Square)
        add("scalar", s2, reads=("P3A", "P3B"), writes=("SCR_C", "SCR_D"))
        def v3(v, sem):
            return v.tensor_sub(MB, MB, TMPB)
        add("vector", v3, reads=("SCR_A", "SCR_B", "SCR_C", "SCR_D"),
            writes=("SCR_A", "SCR_B"))
        def s3(s, sem):
            return s.activation(TMPB, MB, AF.Ln, bias=EPSC[:, 0:1])
        add("scalar", s3, reads=("SCR_A", "SCR_B", "EPSC"),
            writes=("SCR_C", "SCR_D"))
        def s4(s, sem):
            return s.activation(PP[2][:, 0:T], TMPB, AF.Exp, scale=-0.5)
        add("scalar", s4, reads=("SCR_C", "SCR_D", "P2A", "P2B"),
            writes=("P2A", "P2B"))
        for c in range(nch):
            def fa(v, sem, c=c):
                return v.tensor_sub(TMPB, src(c), PP[3][:, 0:T])
            add("vector", fa, reads=(skey(c), "P3A", "P3B"),
                writes=("SCR_C", "SCR_D"), tag=f"{tag}a1_{c}")
            if gb is None:
                def fo(v, sem, c=c):
                    return v.tensor_mul(dst(c), TMPB, PP[2][:, 0:T])
                add("vector", fo,
                    reads=("SCR_C", "SCR_D", "P2A", "P2B"),
                    writes=(dkey(c),), tag=f"{tag}a2_{c}")
            else:
                pp = PP[c % 2]
                pkk = pk(c % 2)
                def fo(v, sem, c=c, pp=pp):
                    return v.tensor_mul(pp[:, 0:T], TMPB, PP[2][:, 0:T])
                add("vector", fo,
                    reads=("SCR_C", "SCR_D", "P2A", "P2B"),
                    writes=(pkk[0], pkk[1]), tag=f"{tag}a2_{c}")
                def fg(v, sem, c=c, pp=pp):
                    return v.tensor_scalar(dst(c), pp[:, 0:T],
                                           KVGB[:, c:c + 1],
                                           KVGB[:, 4 + c:5 + c],
                                           ALU.mult, ALU.add)
                add("vector", fg, reads=(pkk[0], pkk[1], "KVGB"),
                    writes=(dkey(c),), tag=f"{tag}a3_{c}")

    def rope_fm(pp, ppkeys, dst, dkey, rows, c2, s2, tag):
        qfv = [SG(2, 1).bitcast(BF16)[0:128, 512 * t:512 * (t + 1)]
               for t in range(2)]
        for t in range(2):
            def f1(v, sem, t=t):
                return v.tensor_copy(qfv[t][0:rows, :],
                                     pp[0:rows, 512 * t:512 * (t + 1)])
            add("vector", f1, reads=ppkeys, writes=(f"SCR_C",),
                tag=f"{tag}qf{t}")
            def f2(te, sem, t=t):
                return te.matmul(PP[2][0:rows, 512 * t:512 * (t + 1)],
                                 PSW[0:rows, 0:rows], qfv[t][0:rows, :],
                                 start=True, stop=True)
            add("tensor", f2, reads=("SCR_C", "PSW"), writes=(pk(2)[t],),
                tag=f"{tag}sw{t}")
        for t in range(2):
            t1 = SG(3, 1)
            def f3(v, sem, t=t, t1=t1):
                return v.tensor_mul(t1[0:rows, :],
                                    pp[0:rows, 512 * t:512 * (t + 1)],
                                    c2[0:rows, 512 * t:512 * (t + 1)])
            add("vector", f3, reads=ppkeys + ("TBL",), writes=("SCR_D",),
                tag=f"{tag}t1_{t}")
            t2 = SG(1, 1)
            def f4(v, sem, t=t, t2=t2):
                return v.tensor_mul(t2[0:rows, :],
                                    PP[2][0:rows, 512 * t:512 * (t + 1)],
                                    s2[0:rows, 512 * t:512 * (t + 1)])
            add("vector", f4, reads=(pk(2)[t], "TBL"), writes=("SCR_B",),
                tag=f"{tag}t2_{t}")
            def f5(v, sem, t=t, t1=t1, t2=t2):
                return v.tensor_add(dst[0:rows, 512 * t:512 * (t + 1)],
                                    t1[0:rows, :], t2[0:rows, :])
            add("vector", f5, reads=("SCR_D", "SCR_B"), writes=(dkey,),
                tag=f"{tag}o{t}")

    def fm_mms(wslot, nch_in, srcf, skeyf, ppi, tag, start_new=True):
        pp = PP[ppi]
        def fm(te, sem):
            r = None
            for c in range(nch_in):
                for t in range(2):
                    r = te.matmul(pp[:, 512 * t:512 * (t + 1)], wt(wslot, c),
                                  srcf(c)[:, 512 * t:512 * (t + 1)],
                                  start=(start_new and c == 0), stop=False,
                                  skip_group_check=True)
            return r
        rd = tuple([f"WB{wslot}"] + [skeyf(c) for c in range(nch_in)])
        add("tensor", fm, reads=rd, writes=pk(ppi), tag=tag)
        return pp

    emit_ln(Xc, lambda c: f"X{c}", LN, lambda c: f"LN{c}", NCH, "ln1")

    for c in range(NCH):
        tap(f"ln1_{c}", LN(c), (f"LN{c}",), cast=True)

    C2A = TBL[:, 0:T]
    S2A = TBL[:, T:2 * T]

    def attn_proj_kq(wap, n_o, dst_of, dkey_of, stage_dma, tag):
        for h in range(n_o):
            slot = h % 2
            load_w(wap, h, 2048, slot, tag=f"{tag}w{h}")
            pp = fm_mms(slot, NCH, LN, lambda c: f"LN{c}", h % 2,
                        f"{tag}mm{h}")
            rope_fm(pp, pk(h % 2), dst_of(h), dkey_of(h), 128, C2A, S2A,
                    f"{tag}ro{h}")
            if stage_dma is not None:
                stage_dma(h)

    import os as _os
    _chop0 = int(_os.environ.get("KCHOP", "0"))
    if stages >= 1 and _chop0 >= 3:
        C2A_ = TBL[:, 0:T]
        S2A_ = TBL[:, T:2 * T]
        for hh in range(4):
            h = hh
            slot = h % 2
            load_w(wqa, h, 2048, slot, tag=f"qaw{h}")
            pp = fm_mms(slot, NCH, LN, lambda c: f"LN{c}", h % 2, f"qamm{h}")
            rope_fm(pp, pk(h % 2), QNt(hh), f"Q{hh}", 128, C2A_, S2A_,
                    f"qaro{h}")
            tap(f"q0_{hh}", QNt(hh), (f"Q{hh}",), cast=True)
    if stages >= 1 and _chop0 < 3 and _chop0 != 7:
        def kst_dma(h):
            slot = h % 2
            ld("sync", ka_in.ap()[128 * h:128 * (h + 1), :],
               STG[:, T * slot:T * (slot + 1)], f"DRAM_ka{h}",
               reads=(f"STG{slot}",), tag=f"kadma{h}", dgroup=f"ka{h % 2}")
        attn_proj_kq(wka, H, lambda h: STG[:, T * (h % 2):T * (h % 2 + 1)],
                     lambda h: f"STG{h % 2}", kst_dma, "ka")

        pass
    if stages >= 1 and _chop0 < 3 and _chop0 != 6:
        def cc_ka(g, sem):
            return g.collective_compute(
                "AllGather", ALU.bypass, replica_groups=CCG,
                ins=[ka_in.ap().opt()], outs=[ka_out.ap().opt()])
        add("gpsimd", cc_ka,
            reads=tuple([f"DRAM_ka{h}" for h in range(H)]),
            writes=("DRAM_kaout",), coll=True, tag="cc_ka")
        for o in range(8):
            slot = o % 2
            load_w(wva, o, 4096, slot, tag=f"vaw{o}")
            for tt in range(8):
                ppi = tt % 2
                pp = PP[ppi]
                def fv(te, sem, slot=slot, tt=tt, pp=pp):
                    r = None
                    for c in range(NCH):
                        r = te.matmul(pp[:, 0:256],
                                      LN(c)[:, 128 * tt:128 * (tt + 1)],
                                      wt(slot, c, 256),
                                      start=(c == 0), stop=(c == NCH - 1),
                                      skip_group_check=True)
                    return r
                add("tensor", fv,
                    reads=tuple([f"WB{slot}"] + [f"LN{c}" for c in range(NCH)]),
                    writes=pk(ppi), tag=f"vamm{o}_{tt}")
                vs = STG[:, T * (tt % 2) + 256:T * (tt % 2) + 512]
                def fe(v, sem, pp=pp, vs=vs):
                    return v.tensor_copy(vs[:, 0:256], pp[:, 0:256])
                add("vector", fe, reads=pk(ppi), writes=(f"STG{tt % 2}",),
                    tag=f"vaev{o}_{tt}")
                ld("sync", va_in.ap()[128 * tt:128 * (tt + 1),
                                      256 * o:256 * (o + 1)],
                   vs[:, 0:256], f"DRAM_va{o}_{tt}", reads=(f"STG{tt % 2}",),
                   tag=f"vadma{o}_{tt}", dgroup=f"va{tt % 2}")

        pass
    if stages >= 1 and _chop0 < 3 and _chop0 not in (6, 7):
        def cc_va(g, sem):
            return g.collective_compute(
                "AllGather", ALU.bypass, replica_groups=CCG,
                ins=[va_in.ap().opt()], outs=[va_out.ap().opt()])
        add("gpsimd", cc_va,
            reads=tuple([f"DRAM_va{o}_{t}" for o in range(8)
                         for t in range(8)]),
            writes=("DRAM_vaout",), coll=True, tag="cc_va")

        pass
    if stages >= 1 and _chop0 < 3:
        tap("kain", ka_in.ap(),
            tuple(f"DRAM_ka{h}" for h in range(H)), cast=True)
        tap("kaout", ka_out.ap(), ("DRAM_kaout",), cast=True)
        tap("vain", va_in.ap(),
            tuple(f"DRAM_va{o}_{t}" for o in range(8) for t in range(8)),
            cast=True)

        def head_loads(h, layer):
            slot = h % 2
            kout = ka_out if layer == 0 else kb_out
            vout = va_out if layer == 0 else vb_out
            kok = "DRAM_kaout" if layer == 0 else "DRAM_kbout"
            vok = "DRAM_vaout" if layer == 0 else "DRAM_vbout"
            rowB = 2048 if layer == 0 else 2112
            ld("sync", KH[:, 2048 * slot:2048 * slot + 1024],
               kout.ap()[128 * h:128 * (h + 1), :], f"KH{slot}",
               reads=(kok,), tag=f"khA{h}", dgroup=f"kh{slot}")
            ld("sync", KH[:, 2048 * slot + 1024:2048 * slot + 2048],
               kout.ap()[rowB + 128 * h:rowB + 128 * (h + 1), :], f"KH{slot}",
               reads=(kok,), tag=f"khB{h}", dgroup=f"kh{slot}")
            for half in range(4):
                src = vout.ap()[:, 128 * h:128 * (h + 1)].rearrange(
                    "(k p) d -> p k d", p=128)[:, 4 * half:4 * (half + 1), :]
                dst = VH[:, 2048 * slot + 512 * half:
                         2048 * slot + 512 * (half + 1)].rearrange(
                    "p (k d) -> p k d", d=128)
                ld("sync", dst, src, f"VH{slot}", reads=(vok,),
                   tag=f"vh{h}_{half}", dgroup=f"vh{slot}")

        def head_loads_own(h):
            slot = h % 2
            ld("sync", KOWN[:, T * slot:T * (slot + 1)],
               ka_in.ap()[128 * h:128 * (h + 1), :], f"KOWN{slot}",
               reads=(f"DRAM_ka{h}",), tag=f"ko{h}", dgroup=f"ko{slot}")
            src = va_in.ap()[:, 128 * h:128 * (h + 1)].rearrange(
                "(k p) d -> p k d", p=128)
            ld("sync", VOWN[:, T * slot:T * (slot + 1)].rearrange(
                "p (k d) -> p k d", d=128), src, f"VOWN{slot}",
               reads=tuple([f"DRAM_va{h // 2}_{tt}" for tt in range(8)]),
               tag=f"vo{h}", dgroup=f"vo{slot}")

        def attn_head(h, hh, layer, qrope_src=None):
            slot = h % 2
            scale = SCALE_A if layer == 0 else SCALE_B
            qtile = QNt(hh)
            if layer == 1:
                emitters = []
                for it_i in range(16):
                    ps = it_i % 2
                    psl = PP[0] if ps == 0 else PP[3]
                    pskey = ("P0A", "P0B") if ps == 0 else ("P3A", "P3B")
                    ptv = STG[:, 1024 * ps:1024 * (ps + 1)]
                    ptkey = f"STG{ps}"
                    first = it_i == 0
                    def es(it_i=it_i, psl=psl, pskey=pskey):
                        def fs(te, sem, idx=it_i, psl=psl):
                            r = None
                            for qb in range(2):
                                te.matmul(
                                    psl[:, 512 * qb:512 * (qb + 1)],
                                    KH[:, 2048 * slot + 128 * idx:
                                       2048 * slot + 128 * (idx + 1)],
                                    qtile[:, 512 * qb:512 * (qb + 1)],
                                    start=True, stop=False,
                                    skip_group_check=True)
                            qrt, rbase = qrope_src
                            for qb in range(2):
                                r = te.matmul(
                                    psl[:, 512 * qb:512 * (qb + 1)],
                                    KRB[rbase:rbase + 64,
                                        128 * idx:128 * (idx + 1)],
                                    qrt[rbase:rbase + 64,
                                        512 * qb:512 * (qb + 1)],
                                    start=False, stop=True,
                                    skip_group_check=True)
                            return r
                        add("tensor", fs,
                            reads=(f"KH{slot}", f"Q{hh}", "KRB", "QROPE"),
                            writes=pskey, tag=f"S{h}_m{it_i}")
                    def ee(it_i=it_i, psl=psl, ptv=ptv, pskey=pskey,
                           ptkey=ptkey):
                        def fe_(s, sem, psl=psl, ptv=ptv):
                            return s.activation(ptv[:, :], psl[:, 0:1024],
                                                AF.Exp, bias=MB20[:, 0:1],
                                                scale=scale)
                        add("scalar", fe_, reads=pskey + ("MB20",),
                            writes=(ptkey,), tag=f"E{h}_m{it_i}")
                    def ep(it_i=it_i, ptv=ptv, ptkey=ptkey, first=first):
                        vsrc = VH[:, 2048 * slot + 128 * it_i:
                                  2048 * slot + 128 * (it_i + 1)]
                        def fpv(te, sem, vsrc=vsrc, ptv=ptv, first=first):
                            r = None
                            for qb in range(2):
                                te.matmul(
                                    PP[1][:, 512 * qb:512 * (qb + 1)],
                                    vsrc, ptv[:, 512 * qb:512 * (qb + 1)],
                                    start=first, stop=False,
                                    skip_group_check=True)
                                r = te.matmul(
                                    PP[2][:, 512 * qb:512 * (qb + 1)],
                                    ONB[:, :],
                                    ptv[:, 512 * qb:512 * (qb + 1)],
                                    start=first, stop=False,
                                    skip_group_check=True)
                            return r
                        add("tensor", fpv, reads=(ptkey, f"VH{slot}", "ONB"),
                            writes=(pk(1)[0], pk(1)[1], pk(2)[0], pk(2)[1]),
                            tag=f"PV{h}_m{it_i}")
                    emitters.append((es, ee, ep))
                emitters[0][0]()
                for i in range(1, 16):
                    emitters[i - 1][1]()
                    emitters[i][0]()
                    emitters[i - 1][2]()
                emitters[-1][1]()
                emitters[-1][2]()
                RINV2 = SG(0, 2)
                def frc(v, sem):
                    return v.reciprocal(RINV2, PP[2][:, 0:1024])
                add("vector", frc, reads=("P2A", "P2B"),
                    writes=("SCR_A", "SCR_B"), tag=f"rc{h}")
                att = QRt(hh)
                def fat(v, sem, att=att):
                    return v.tensor_mul(att[:, 0:1024], PP[1][:, 0:1024],
                                        RINV2)
                add("vector", fat, reads=("P1A", "P1B", "SCR_A", "SCR_B"),
                    writes=(f"QR{hh}",), tag=f"at{h}")
                return
            for qb in range(2):
                items = []
                if layer == 0:
                    for tw in range(4):
                        items.append(("d", tw, None))
                    for p_, kt in enumerate(ITEMS_QB[qb]):
                        items.append(("g", kt,
                                      (p_ if qb == 0 else 4 + p_)))
                else:
                    for kt in range(16):
                        items.append(("g", kt, None))
                emitters = []
                for it_i, (kind, idx, bcol) in enumerate(items):
                    ps_i = it_i % 2
                    pskey = pk(0)[ps_i]
                    psl = PP[0][:, 512 * ps_i:512 * (ps_i + 1)]
                    ptv = PTS[:, 512 * ps_i:512 * (ps_i + 1)]
                    ptkey = f"PT{ps_i}"
                    first = it_i == 0
                    if kind == "g":
                        qoff = 0
                        def es(it_i=it_i, idx=idx, psl=psl, pskey=pskey,
                               slot=slot, qb=qb, qtile=qtile):
                            def fs(te, sem, idx=idx, psl=psl, slot=slot,
                                   qb=qb, qtile=qtile):
                                r = te.matmul(
                                    psl[:, :],
                                    KH[:, 2048 * slot + 128 * idx:
                                       2048 * slot + 128 * (idx + 1)],
                                    qtile[:, 512 * qb:512 * (qb + 1)],
                                    start=True, stop=(layer == 0),
                                    skip_group_check=True)
                                if layer == 1:
                                    qrt, rbase = qrope_src
                                    r = te.matmul(
                                        psl[:, :],
                                        KRB[rbase:rbase + 64,
                                            128 * idx:128 * (idx + 1)],
                                        qrt[rbase:rbase + 64,
                                            512 * qb:512 * (qb + 1)],
                                        start=False, stop=True,
                                        skip_group_check=True)
                                return r
                            add("tensor", fs,
                                reads=(f"KH{slot}", f"Q{hh}") if layer == 0
                                else (f"KH{slot}", f"Q{hh}", "KRB", "QROPE"),
                                writes=(pskey,), tag=f"S{h}_{qb}_{it_i}")
                        if layer == 0:
                            def ee(it_i=it_i, psl=psl, ptv=ptv, bcol=bcol,
                                   pskey=pskey, ptkey=ptkey):
                                def fe_(s, sem, psl=psl, ptv=ptv, bcol=bcol):
                                    return s.activation(
                                        ptv[:, :], psl[:, :], AF.Exp,
                                        bias=ABIAS[:, bcol:bcol + 1],
                                        scale=scale)
                                add("scalar", fe_, reads=(pskey, "ABIAS"),
                                    writes=(ptkey,), tag=f"E{h}_{qb}_{it_i}")
                        else:
                            def ee(it_i=it_i, psl=psl, ptv=ptv,
                                   pskey=pskey, ptkey=ptkey):
                                def fe_(s, sem, psl=psl, ptv=ptv):
                                    return s.activation(ptv[:, :], psl[:, :],
                                                        AF.Exp,
                                                        bias=MB20[:, 0:1],
                                                        scale=scale)
                                add("scalar", fe_, reads=(pskey, "MB20"),
                                    writes=(ptkey,), tag=f"E{h}_{qb}_{it_i}")
                        vsrc = VH[:, 2048 * slot + 128 * idx:
                                  2048 * slot + 128 * (idx + 1)]
                        vkey = f"VH{slot}"
                    else:
                        tw = idx
                        qoff = 128 * tw
                        def es(it_i=it_i, tw=tw, psl=psl, pskey=pskey,
                               slot=slot, qb=qb, qtile=qtile, qoff=qoff):
                            def fs(te, sem, tw=tw, psl=psl, slot=slot, qb=qb,
                                   qtile=qtile, qoff=qoff):
                                te.matmul(
                                    psl[:, qoff:512],
                                    KOWN[:, T * slot + 512 * qb + qoff:
                                         T * slot + 512 * qb + qoff + 128],
                                    qtile[:, 512 * qb + qoff:512 * (qb + 1)],
                                    start=True, stop=False,
                                    skip_group_check=True)
                                return te.matmul(
                                    psl[:, qoff:qoff + 128], IDB[:, :],
                                    TRIt[:, :], start=False, stop=True,
                                    skip_group_check=True)
                            add("tensor", fs,
                                reads=(f"KOWN{slot}", f"Q{hh}", "IDB", "TRI"),
                                writes=(pskey,), tag=f"Sd{h}_{qb}_{tw}")
                        def ee(it_i=it_i, psl=psl, ptv=ptv, qoff=qoff,
                               pskey=pskey, ptkey=ptkey, tw=tw):
                            def fe_(s, sem, psl=psl, ptv=ptv, qoff=qoff):
                                return s.activation(ptv[:, qoff:512],
                                                    psl[:, qoff:512], AF.Exp,
                                                    bias=MB20[:, 0:1],
                                                    scale=scale)
                            add("scalar", fe_, reads=(pskey, "MB20"),
                                writes=(ptkey,), tag=f"Ed{h}_{qb}_{tw}")
                        vsrc = VOWN[:, T * slot + 512 * qb + qoff:
                                    T * slot + 512 * qb + qoff + 128]
                        vkey = f"VOWN{slot}"
                    pvkey = pk(1)[qb]
                    denkey = pk(2)[qb]
                    def ep(it_i=it_i, vsrc=vsrc, ptv=ptv, qoff=qoff, qb=qb,
                           first=first, ptkey=ptkey, vkey=vkey,
                           pvkey=pvkey, denkey=denkey):
                        def fpv(te, sem, vsrc=vsrc, ptv=ptv, qoff=qoff,
                                qb=qb, first=first):
                            te.matmul(
                                PP[1][:, 512 * qb + qoff:512 * (qb + 1)],
                                vsrc, ptv[:, qoff:512], start=first,
                                stop=False, skip_group_check=True)
                            return te.matmul(
                                PP[2][:, 512 * qb + qoff:512 * (qb + 1)],
                                ONB[:, :], ptv[:, qoff:512], start=first,
                                stop=False, skip_group_check=True)
                        add("tensor", fpv, reads=(ptkey, vkey, "ONB"),
                            writes=(pvkey, denkey), tag=f"PV{h}_{qb}_{it_i}")
                    emitters.append((es, ee, ep))
                emitters[0][0]()
                for i in range(1, len(emitters)):
                    emitters[i - 1][1]()
                    emitters[i][0]()
                    emitters[i - 1][2]()
                emitters[-1][1]()
                emitters[-1][2]()
                def frc(v, sem, qb=qb):
                    return v.reciprocal(RINV[:, :],
                                        PP[2][:, 512 * qb:512 * (qb + 1)])
                add("vector", frc, reads=(pk(2)[qb],), writes=("RINV",),
                    tag=f"rc{h}_{qb}")
                att = QRt(hh)
                def fat(v, sem, qb=qb, att=att):
                    return v.tensor_mul(att[:, 512 * qb:512 * (qb + 1)],
                                        PP[1][:, 512 * qb:512 * (qb + 1)],
                                        RINV[:, :])
                add("vector", fat, reads=(pk(1)[qb], "RINV"),
                    writes=(f"QR{hh}",), tag=f"at{h}_{qb}")

        def out_proj(wap, g, tag):
            for o in range(16):
                sub = o % 4
                base = 4096 * (sub // 2) + 512 * (sub % 2)
                load_w_og(wap, g, o, tag=f"{tag}w{g}_{o}")
                ppi = o % 2
                pp = PP[ppi]
                def fm(te, sem, base=base, pp=pp):
                    r = None
                    for cc in range(4):
                        for t in range(2):
                            r = te.matmul(pp[:, 512 * t:512 * (t + 1)],
                                          WBUF[:, base + 128 * cc:
                                               base + 128 * (cc + 1)],
                                          QRt(cc)[:, 512 * t:512 * (t + 1)],
                                          start=(cc == 0), stop=(cc == 3),
                                          skip_group_check=True)
                    return r
                add("tensor", fm,
                    reads=tuple([f"WBo{sub}"] +
                                [f"QR{cc}" for cc in range(4)]),
                    writes=pk(ppi), tag=f"{tag}mm{g}_{o}")
                def fr(v, sem, o=o, pp=pp):
                    return v.tensor_add(Xc(o), pp[:, 0:T], Xc(o))
                add("vector", fr, reads=pk(ppi) + (f"X{o}",),
                    writes=(f"X{o}",), tag=f"{tag}res{g}_{o}")

        import os as _os
        _chop = int(_os.environ.get("KCHOP", "0"))
        for g in range(4):
            if _chop >= 2 and g > 0:
                break
            for hh in range(4):
                h = 4 * g + hh
                slot = h % 2
                load_w(wqa, h, 2048, slot, tag=f"qaw{h}")
                pp = fm_mms(slot, NCH, LN, lambda c: f"LN{c}", h % 2,
                            f"qamm{h}")
                rope_fm(pp, pk(h % 2), QNt(hh), f"Q{hh}", 128, C2A, S2A,
                        f"qaro{h}")
            if g == 0:
                for hh in range(4):
                    tap(f"q0_{hh}", QNt(hh), (f"Q{hh}",), cast=True)
            if _chop >= 1:
                continue
            for hh in range(4):
                h = 4 * g + hh
                head_loads(h, 0)
                head_loads_own(h)
                attn_head(h, hh, 0)
            if g == 0:
                for hh in range(4):
                    tap(f"att0_{hh}", QRt(hh), (f"QR{hh}",), cast=True)
            out_proj(woa, g, "oa")

    def ffn(tag):
        for tt in range(8):
            ppi = tt % 2
            def fl(te, sem, tt=tt, ppi=ppi):
                r = None
                for c in range(NCH):
                    r = te.matmul(PP[ppi][:, 0:8],
                                  LN(c)[:, 128 * tt:128 * (tt + 1)],
                                  RRs[:, 8 * c:8 * (c + 1)],
                                  start=(c == 0), stop=(c == NCH - 1),
                                  skip_group_check=True)
                return r
            add("tensor", fl,
                reads=tuple([f"LN{c}" for c in range(NCH)] + ["RR"]),
                writes=pk(ppi), tag=f"{tag}lg{tt}")
            sl = 28 * (tt % 2)
            lg = GTMP[:, sl:sl + 8]
            ex = GTMP[:, sl + 8:sl + 16]
            msk = GTMP[:, sl + 16:sl + 24]
            sel = GTMP[:, sl + 16:sl + 24]
            m1 = GTMP[:, sl + 24:sl + 25]
            m2 = GTMP[:, sl + 25:sl + 26]
            nm1 = GTMP[:, sl + 26:sl + 27]
            s2c = GTMP[:, sl + 27:sl + 28]
            gsl = f"GT{tt % 2}"
            def g1(v, sem, tt=tt, ppi=ppi, lg=lg):
                return v.tensor_add(lg, PP[ppi][:, 0:8], RB8[:, :])
            add("vector", g1, reads=pk(ppi) + ("RB8",), writes=(gsl,))
            def g2(v, sem, lg=lg, m1=m1):
                return v.reduce_max(m1, lg, axis=AX.X)
            add("vector", g2, reads=(gsl,), writes=(gsl,))
            def g3(v, sem, m1=m1, nm1=nm1):
                return v.tensor_single_scalar(nm1, m1, -1.0, ALU.mult)
            add("vector", g3, reads=(gsl,), writes=(gsl,))
            def g4(v, sem, lg=lg, m1=m1, msk=msk):
                return v.tensor_single_scalar(msk, lg, m1[:, 0:1],
                                              ALU.is_equal)
            add("vector", g4, reads=(gsl,), writes=("GMSK",))
            def g5(v, sem, lg=lg, msk=msk):
                return v.scalar_tensor_tensor(msk, msk, NEG, lg, ALU.mult,
                                              ALU.add)
            add("vector", g5, reads=("GMSK", gsl), writes=("GMSK",))
            def g6(v, sem, msk=msk, m2=m2):
                return v.reduce_max(m2, msk, axis=AX.X)
            add("vector", g6, reads=("GMSK",), writes=(gsl,))
            def g7(s, sem, lg=lg, nm1=nm1, ex=ex):
                return s.activation(ex, lg, AF.Exp, bias=nm1[:, 0:1])
            add("scalar", g7, reads=(gsl,), writes=("GEX",))
            def g8(v, sem, lg=lg, m2=m2, sel=sel):
                return v.tensor_single_scalar(sel, lg, m2[:, 0:1], ALU.is_ge)
            add("vector", g8, reads=(gsl,), writes=("GSEL",))
            def g9(v, sem, ex=ex, sel=sel):
                return v.tensor_mul(ex, ex, sel)
            add("vector", g9, reads=("GEX", "GSEL"), writes=("GEX",))
            def g10(v, sem, ex=ex, s2c=s2c):
                return v.reduce_sum(s2c, ex, axis=AX.X)
            add("vector", g10, reads=("GEX",), writes=(gsl,))
            def g11(v, sem, s2c=s2c):
                return v.reciprocal(s2c, s2c)
            add("vector", g11, reads=(gsl,), writes=(gsl,))
            def g12(v, sem, ex=ex, s2c=s2c, tt=tt):
                return v.tensor_scalar_mul(GDAT[:, 8 * tt:8 * (tt + 1)], ex,
                                           s2c[:, 0:1])
            add("vector", g12, reads=("GEX", gsl), writes=(f"GD{tt}", "QN0"))
            def g13(te, sem, tt=tt):
                return te.transpose(PP[3][0:8, 0:128],
                                    GDAT[:, 8 * tt:8 * (tt + 1)], IDF[:, :])
            add("tensor", g13, reads=(f"GD{tt}", "QN0", "IDF"), writes=("P3A",),
                tag=f"{tag}gt{tt}")
            def g14(v, sem, tt=tt):
                return v.tensor_copy(GT[0:8, 128 * tt:128 * (tt + 1)],
                                     PP[3][0:8, 0:128])
            add("vector", g14, reads=("P3A",), writes=("SCR_A", "SCR_B"),
                tag=f"{tag}gte{tt}")
        def fmid(te, sem):
            r = None
            for c in range(NCH):
                for t in range(2):
                    r = te.matmul(PP[0][:, 512 * t:512 * (t + 1)],
                                  RAs[:, 128 * c:128 * (c + 1)],
                                  LN(c)[:, 512 * t:512 * (t + 1)],
                                  start=(c == 0), stop=(c == NCH - 1),
                                  skip_group_check=True)
            return r
        add("tensor", fmid,
            reads=tuple([f"LN{c}" for c in range(NCH)] + ["RA"]),
            writes=("P0A", "P0B"), tag=f"{tag}mid")
        for t in range(2):
            def fgm(te, sem, t=t):
                return te.matmul(PP[1][:, 512 * t:512 * (t + 1)], E2[:, :],
                                 GT[0:8, 512 * t:512 * (t + 1)], start=True,
                                 stop=True)
            add("tensor", fgm, reads=("SCR_A", "SCR_B", "E2"),
                writes=(pk(1)[t],), tag=f"{tag}gmm{t}")
        GS = SG(0, 2)
        def fgs(v, sem):
            return v.tensor_copy(GS, PP[1][:, 0:T])
        add("vector", fgs, reads=("P1A", "P1B"), writes=("SCR_A", "SCR_B"),
            tag=f"{tag}gs")
        MIDG = SG(3, 1).bitcast(BF16)
        def fmg(v, sem):
            return v.tensor_mul(MIDG, PP[0][:, 0:T], GS)
        add("vector", fmg, reads=("P0A", "P0B", "SCR_A", "SCR_B"),
            writes=("SCR_D",), tag=f"{tag}midg")
        for o in range(16):
            slot = o % 2
            load_w(basew, o, 2048, slot, tag=f"{tag}bw{o}")
            ppi = o % 2
            pp = PP[ppi]
            def fb(te, sem, slot=slot, pp=pp, o=o):
                for c in range(NCH):
                    for t in range(2):
                        te.matmul(pp[:, 512 * t:512 * (t + 1)], wt(slot, c),
                                  LN(c)[:, 512 * t:512 * (t + 1)],
                                  start=(c == 0), stop=False,
                                  skip_group_check=True)
                r = None
                for t in range(2):
                    r = te.matmul(pp[:, 512 * t:512 * (t + 1)],
                                  RBs[:, 128 * o:128 * (o + 1)],
                                  MIDG[:, 512 * t:512 * (t + 1)],
                                  start=False, stop=True,
                                  skip_group_check=True)
                return r
            add("tensor", fb,
                reads=tuple([f"WB{slot}", "RB_", "SCR_D"] +
                            [f"LN{c}" for c in range(NCH)]),
                writes=pk(ppi), tag=f"{tag}base{o}")
            def fr(v, sem, o=o, pp=pp):
                return v.scalar_tensor_tensor(Xc(o), pp[:, 0:T],
                                              BIASC[:, o:o + 1], Xc(o),
                                              ALU.add, ALU.add)
            add("vector", fr, reads=pk(ppi) + (f"X{o}", "BIASC"),
                writes=(f"X{o}",), tag=f"{tag}res{o}")

    if stages >= 2:
        emit_ln(Xc, lambda c: f"X{c}", LN, lambda c: f"LN{c}", NCH, "ln2")
        ffn("f1")

    if stages >= 2.2:
        CTT = SCR[0:32, 0:T]
        STT = SCR[0:32, T:2 * T]
        for tt in range(8):
            th = FRQ[:, 0:32]
            ld("sync", th, freqs[128 * tt:128 * (tt + 1), :], "QN0",
               tag=f"fq{tt}", dgroup="frq")
            a_ = FRQ[:, 32:64]
            b_ = FRQ[:, 64:96]
            m_ = FRQ[:, 96:128]
            ct = FRQ[:, 128:160]
            st = FRQ[:, 160:192]
            def w1(v, sem, th=th, m_=m_):
                return v.tensor_single_scalar(m_, th, PI, ALU.is_gt)
            add("vector", w1, reads=("QN0",), writes=("QN0",))
            def w2(v, sem, th=th, m_=m_, a_=a_):
                return v.scalar_tensor_tensor(a_, m_, -2 * PI, th, ALU.mult,
                                              ALU.add)
            add("vector", w2, reads=("QN0",), writes=("QN0",))
            def w3(s, sem, a_=a_, st=st):
                return s.activation(st, a_, AF.Sin)
            add("scalar", w3, reads=("QN0",), writes=("QN0",))
            def w4(v, sem, th=th, b_=b_):
                return v.tensor_scalar(b_, th, -1.0, PI / 2, ALU.mult, ALU.add)
            add("vector", w4, reads=("QN0",), writes=("QN0",))
            def w5(v, sem, b_=b_, m_=m_):
                return v.tensor_single_scalar(m_, b_, -PI, ALU.is_lt)
            add("vector", w5, reads=("QN0",), writes=("QN0",))
            def w6(v, sem, b_=b_, m_=m_):
                return v.scalar_tensor_tensor(b_, m_, 2 * PI, b_, ALU.mult,
                                              ALU.add)
            add("vector", w6, reads=("QN0",), writes=("QN0",))
            def w7(s, sem, b_=b_, ct=ct):
                return s.activation(ct, b_, AF.Sin)
            add("scalar", w7, reads=("QN0",), writes=("QN0",))
            def w8(te, sem, ct=ct):
                return te.transpose(PP[3][0:32, 0:128], ct, IDF[:, :])
            add("tensor", w8, reads=("QN0", "IDF"), writes=("P3A",))
            def w9(v, sem, tt=tt):
                return v.tensor_copy(CTT[:, 128 * tt:128 * (tt + 1)],
                                     PP[3][0:32, 0:128])
            add("vector", w9, reads=("P3A",), writes=("SCR_A", "SCR_B"),
                tag=f"ctt{tt}")
            def wA(te, sem, st=st):
                return te.transpose(PP[3][0:32, 0:128], st, IDF[:, :])
            add("tensor", wA, reads=("QN0", "IDF"), writes=("P3A",))
            def wB(v, sem, tt=tt):
                return v.tensor_copy(STT[:, 128 * tt:128 * (tt + 1)],
                                     PP[3][0:32, 0:128])
            add("vector", wB, reads=("P3A",), writes=("SCR_C", "SCR_D"),
                tag=f"stt{tt}")
        for t in range(2):
            def x1(te, sem, t=t):
                return te.matmul(PP[2][:, 512 * t:512 * (t + 1)], EC[:, :],
                                 CTT[:, 512 * t:512 * (t + 1)], start=True,
                                 stop=True)
            add("tensor", x1, reads=("SCR_A", "SCR_B", "EC"),
                writes=(pk(2)[t],))
            def x2(v, sem, t=t):
                return v.tensor_copy(TBL[:, 512 * t:512 * (t + 1)],
                                     PP[2][:, 512 * t:512 * (t + 1)])
            add("vector", x2, reads=(pk(2)[t],), writes=("TBL",),
                tag=f"c2b{t}")
            def x3(te, sem, t=t):
                return te.matmul(PP[3][:, 512 * t:512 * (t + 1)], ES[:, :],
                                 STT[:, 512 * t:512 * (t + 1)], start=True,
                                 stop=True)
            add("tensor", x3, reads=("SCR_C", "SCR_D", "ES"),
                writes=(pk(3)[t],))
            def x4(v, sem, t=t):
                return v.tensor_copy(TBL[:, T + 512 * t:T + 512 * (t + 1)],
                                     PP[3][:, 512 * t:512 * (t + 1)])
            add("vector", x4, reads=(pk(3)[t],), writes=("TBL",),
                tag=f"s2b{t}")

        emit_ln(Xc, lambda c: f"X{c}", LN, lambda c: f"LN{c}", NCH, "ln3")
        C2B = TBL[:, 0:T]
        S2B = TBL[:, T:2 * T]

    if stages >= 2.4:
        for o in range(4):
            slot = o % 2
            load_w(wkvac, o, 2048, slot, tag=f"kvaw{o}")
            pp = fm_mms(slot, NCH, LN, lambda c: f"LN{c}", o % 2, f"kvamm{o}")
            def fe_(s, sem, o=o, pp=pp):
                return s.activation(KVCB(o), pp[:, 0:T], AF.Copy)
            add("scalar", fe_, reads=pk(o % 2), writes=(f"QN{o}",),
                tag=f"kvaev{o}")
        load_w(wkvar, 0, 1024, 0, tag="kvrw")
        def fkr(te, sem):
            r = None
            for c in range(NCH):
                for t in range(2):
                    r = te.matmul(PP[0][0:64, 512 * t:512 * (t + 1)],
                                  wt(0, c, 64),
                                  LN(c)[:, 512 * t:512 * (t + 1)],
                                  start=(c == 0), stop=(c == NCH - 1),
                                  skip_group_check=True)
            return r
        add("tensor", fkr,
            reads=tuple(["WB0"] + [f"LN{c}" for c in range(NCH)]),
            writes=("P0A", "P0B"), tag="kvrmm")
        kst = STG[:, 0:T]
        rope_fm(PP[0], ("P0A", "P0B"), kst, "STG0", 64, C2B, S2B, "kvro")
        ld("sync", kb_in.ap()[2048:2112, :], kst[0:64, :], "DRAM_kbr",
           reads=("STG0",), tag="kbrdma", dgroup="kb0")
        emit_ln(lambda i: KVCB(i), lambda i: f"QN{i}",
                lambda i: KVCN(i), lambda i: f"QN{4 + i}", 4, "lnkv",
                gb=True)
        pass
    if stages >= 2.6:
        for o in range(16):
            kk = o % 4
            kbase = 4096 * (kk // 2) + 512 * (kk % 2)
            ld("gpsimd", WBUF[:, kbase:kbase + 512],
               wkvbk[o].rearrange("p c j -> p (c j)"),
               (f"WBo{kk}", f"WB{kk // 2}"), tag=f"kbkw{o}",
               dgroup=f"wbo{kk}")
            pp = PP[o % 2]
            def fm(te, sem, kbase=kbase, pp=pp):
                r = None
                for c in range(4):
                    for t in range(2):
                        r = te.matmul(pp[:, 512 * t:512 * (t + 1)],
                                      WBUF[:, kbase + 128 * c:
                                           kbase + 128 * (c + 1)],
                                      KVCN(c)[:, 512 * t:512 * (t + 1)],
                                      start=(c == 0), stop=(c == 3),
                                      skip_group_check=True)
                return r
            add("tensor", fm,
                reads=tuple([f"WBo{kk}"] + [f"QN{4 + c}" for c in range(4)]),
                writes=pk(o % 2), tag=f"kbkmm{o}")
            st_ = STG[:, T * (o % 2):T * (o % 2 + 1)]
            def fe_(s, sem, pp=pp, st_=st_):
                return s.activation(st_[:, :], pp[:, 0:T], AF.Copy)
            add("scalar", fe_, reads=pk(o % 2), writes=(f"STG{o % 2}",),
                tag=f"kbkev{o}")
            ld("sync", kb_in.ap()[128 * o:128 * (o + 1), :], st_[:, :],
               f"DRAM_kbk{o}", reads=(f"STG{o % 2}",), tag=f"kbkdma{o}",
               dgroup=f"kb{o % 2}")
        def load_w_v(o, tag):
            k = o % 4
            base = 4096 * (k // 2) + 1024 * (k % 2)
            wkeys = (f"WBv{k}", f"WB{k // 2}")
            if k % 2 == 0:
                wkeys += (f"WBo{k}", f"WBo{k + 1}")
            return ld("gpsimd", WBUF[:, base:base + 1024],
                      wkvbv[o].rearrange("p c j -> p (c j)"), wkeys,
                      tag=tag, dgroup=f"wbv{k}")
        for o in range(8):
            kk = o % 4
            vbase = 4096 * (kk // 2) + 1024 * (kk % 2)
            load_w_v(o, tag=f"kbvw{o}")
            for tt in range(8):
                ppi = tt % 2
                pp = PP[ppi]
                def fv(te, sem, vbase=vbase, tt=tt, pp=pp):
                    r = None
                    for c in range(4):
                        r = te.matmul(pp[:, 0:256],
                                      KVCN(c)[:, 128 * tt:128 * (tt + 1)],
                                      WBUF[:, vbase + 256 * c:
                                           vbase + 256 * (c + 1)],
                                      start=(c == 0),
                                      stop=(c == 3), skip_group_check=True)
                    return r
                add("tensor", fv,
                    reads=tuple([f"WBv{kk}"] +
                                [f"QN{4 + c}" for c in range(4)]),
                    writes=pk(ppi), tag=f"kbvmm{o}_{tt}")
                vs = STG[:, T * (tt % 2) + 256:T * (tt % 2) + 512]
                def fe_(v, sem, pp=pp, vs=vs):
                    return v.tensor_copy(vs[:, 0:256], pp[:, 0:256])
                add("vector", fe_, reads=pk(ppi), writes=(f"STG{tt % 2}",),
                    tag=f"kbvev{o}_{tt}")
                ld("sync", vb_in.ap()[128 * tt:128 * (tt + 1),
                                      256 * o:256 * (o + 1)],
                   vs[:, 0:256], f"DRAM_vb{o}_{tt}", reads=(f"STG{tt % 2}",),
                   tag=f"kbvdma{o}_{tt}", dgroup=f"vb{tt % 2}")
        if stages >= 2.55:
            def cc_kb(g, sem):
                return g.collective_compute(
                    "AllGather", ALU.bypass, replica_groups=CCG,
                    ins=[kb_in.ap().opt()], outs=[kb_out.ap().opt()])
            add("gpsimd", cc_kb,
                reads=tuple([f"DRAM_kbk{o}" for o in range(16)] +
                            ["DRAM_kbr"]),
                writes=("DRAM_kbout",), coll=True, tag="cc_kb")
        if stages >= 2.56:
            def cc_vb(g, sem):
                return g.collective_compute(
                    "AllGather", ALU.bypass, replica_groups=CCG,
                    ins=[vb_in.ap().opt()], outs=[vb_out.ap().opt()])
            add("gpsimd", cc_vb,
                reads=tuple([f"DRAM_vb{o}_{t}" for o in range(8)
                             for t in range(8)]),
                writes=("DRAM_vbout",), coll=True, tag="cc_vb")
        for dst_base in ((0, 64) if stages >= 2.58 else ()):
            ld("sync", KRB[dst_base:dst_base + 64, 0:1024],
               kb_out.ap()[2048:2112, :], ("KRB", "KOWN0", "KOWN1"),
               reads=("DRAM_kbout",), tag=f"krb{dst_base}a", dgroup="krb")
            ld("sync", KRB[dst_base:dst_base + 64, 1024:2048],
               kb_out.ap()[4160:4224, :], ("KRB", "KOWN0", "KOWN1"),
               reads=("DRAM_kbout",), tag=f"krb{dst_base}b", dgroup="krb")

        pass
    if stages >= 2.8:
        for g in range(4):
            if stages < 3 and g > 0:
                break
            for hh in range(4):
                h = 4 * g + hh
                slot = h % 2
                load_w(wqb, h, 2048, slot, tag=f"qbnw{h}")
                pp = fm_mms(slot, NCH, LN, lambda c: f"LN{c}", h % 2,
                            f"qbnmm{h}")
                def fe_(s, sem, hh=hh, pp=pp):
                    return s.activation(QNt(hh), pp[:, 0:T], AF.Copy)
                add("scalar", fe_, reads=pk(h % 2), writes=(f"Q{hh}",),
                    tag=f"qbnev{h}")
            for jj in range(2):
                j = 2 * g + jj
                slot = j % 2
                load_w(wqb, 16 + j, 2048, slot, tag=f"qbrw{j}")
                pp = fm_mms(slot, NCH, LN, lambda c: f"LN{c}", j % 2,
                            f"qbrmm{j}")
                rope_fm(pp, pk(j % 2), QNt(4 + jj), "QROPE", 128, C2B, S2B,
                        f"qbro{j}")
            if stages < 2.9:
                continue
            for hh in range(4):
                h = 4 * g + hh
                head_loads(h, 1)
                attn_head(h, hh, 1,
                          qrope_src=(QNt(4 + hh // 2), 64 * (hh % 2)))
            if stages < 3:
                continue
            out_proj(wob, g, "ob")

    if stages >= 4:
        emit_ln(Xc, lambda c: f"X{c}", LN, lambda c: f"LN{c}", NCH, "ln4")
        ffn("f2")

    for c in range(NCH):
        ld("sync", y_out[128 * c:128 * (c + 1), :], Xc(c), f"Y{c}",
           reads=(f"X{c}",), tag=f"st{c}", dgroup="y")

    sc.finalize(nc, stack)
    for en in ENGINES:
        getattr(blk, en)(lambda eng, en=en: sc.emit_engine(eng, en))
    stack.close()
    return nc


def kernel(L0_attn_a_x, L0_attn_a_Wq, L0_attn_a_Wk, L0_attn_a_Wv, L0_attn_a_Wo,
           L0_attn_b_freqs_cis, L0_attn_b_Wq, L0_attn_b_Wkv_a,
           L0_attn_b_kv_gamma, L0_attn_b_kv_beta, L0_attn_b_Wkv_b,
           L0_attn_b_Wout, L0_ffn_base_w, L0_ffn_base_b, L0_ffn_router_w,
           L0_ffn_router_b, L0_ffn_lora_A, L0_ffn_lora_B, _stages=4,
           _taps=()):
    f32 = np.float32
    x = np.asarray(L0_attn_a_x, dtype=f32)

    def tile_w_og(W):
        A = W.reshape(16, 128, 16, 128)
        Bm = A.reshape(4, 4, 128, 16, 128).transpose(0, 3, 2, 1, 4)
        return np.ascontiguousarray(Bm.reshape(4, 16, 128, 512))

    wqa_h = tile_w(np.asarray(L0_attn_a_Wq, f32).T, 128)
    wka_h = tile_w(np.asarray(L0_attn_a_Wk, f32).T, 128)
    wva_h = tile_w(np.asarray(L0_attn_a_Wv, f32).T, 256)
    woa_h = tile_w_og(np.asarray(L0_attn_a_Wo, f32).T)
    basew_h = tile_w(np.asarray(L0_ffn_base_w, f32).T, 128)
    biasT_h = np.ascontiguousarray(np.asarray(L0_ffn_base_b, f32).reshape(16, 128).T)
    rb8_h = np.ascontiguousarray(
        np.broadcast_to(np.asarray(L0_ffn_router_b, f32), (128, 8)))
    routr_h = np.ascontiguousarray(
        np.asarray(L0_ffn_router_w, f32).T.reshape(16, 128, 8)
        .transpose(1, 0, 2).reshape(128, 128))
    aall_h = np.ascontiguousarray(
        np.asarray(L0_ffn_lora_A, f32).transpose(1, 0, 2).reshape(D, 128)
        .reshape(16, 128, 128).transpose(1, 0, 2).reshape(128, 2048))
    ball_h = np.ascontiguousarray(
        np.asarray(L0_ffn_lora_B, f32).reshape(128, D))
    wqb_r = np.asarray(L0_attn_b_Wq, f32)
    cols = []
    for h in range(16):
        cols.extend(range(192 * h, 192 * h + 128))
    for h in range(16):
        cols.extend(range(192 * h + 128, 192 * h + 192))
    wqb_h = tile_w(np.ascontiguousarray(wqb_r[:, cols]), 128)
    wkva_r = np.asarray(L0_attn_b_Wkv_a, f32)
    wkvac_h = tile_w(np.ascontiguousarray(wkva_r[:, 0:512]), 128)
    wkvar_h = tile_w(np.ascontiguousarray(wkva_r[:, 512:576]), 64)
    kvgb_h = np.zeros((128, 8), f32)
    kvgb_h[:, 0:4] = np.asarray(L0_attn_b_kv_gamma, f32).reshape(4, 128).T
    kvgb_h[:, 4:8] = np.asarray(L0_attn_b_kv_beta, f32).reshape(4, 128).T
    wkvb_r = np.asarray(L0_attn_b_Wkv_b, f32)
    kcols, vcols = [], []
    for h in range(16):
        kcols.extend(range(256 * h, 256 * h + 128))
        vcols.extend(range(256 * h + 128, 256 * h + 256))
    wkvbk_h = tile_w(np.ascontiguousarray(wkvb_r[:, kcols]), 128)
    wkvbv_h = tile_w(np.ascontiguousarray(wkvb_r[:, vcols]), 256)
    wob_h = tile_w_og(np.asarray(L0_attn_b_Wout, f32))
    fcis = np.asarray(L0_attn_b_freqs_cis, f32)

    nc = build_nc(_stages, taps=_taps)
    in_maps = []
    for core in range(8):
        b = core // 2
        par = core % 2
        pos = _positions(par)
        xc = np.ascontiguousarray(x[b][pos, :].T)
        c2a_h, s2a_h = _rope_tables_a(par)
        in_maps.append({
            "x_fm": xc, "wqa": wqa_h, "wka": wka_h, "wva": wva_h,
            "woa": woa_h, "basew": basew_h, "biasT": biasT_h, "rb8": rb8_h,
            "routr": routr_h, "aall": aall_h, "ball": ball_h, "wqb": wqb_h,
            "wkvac": wkvac_h, "wkvar": wkvar_h, "kvgb": kvgb_h,
            "wkvbk": wkvbk_h, "wkvbv": wkvbv_h, "wob": wob_h,
            "freqs": np.ascontiguousarray(fcis[pos, :]),
            "c2a": c2a_h, "s2a": s2a_h, "abias": _abias(par),
        })
    res = run_bass_kernel_spmd(nc, in_maps, core_ids=list(range(8)))
    y = np.empty((B, S, D), dtype=np.float32)
    for core in range(8):
        b = core // 2
        par = core % 2
        pos = _positions(par)
        y[b][pos, :] = res.results[core]["y_fm"].T
    if _taps:
        tapd = [{k[4:]: v for k, v in res.results[c].items()
                 if k.startswith("tap_")} for c in range(8)]
        return y, tapd
    return y
